# version 09
# baseline (speedup 1.0000x reference)
import numpy as np
import ml_dtypes
from contextlib import ExitStack

import concourse.bass as bass
import concourse.mybir as mybir
import concourse.dve_ops as dve_ops
from concourse.bass_utils import run_bass_kernel_spmd

F32 = mybir.dt.float32
BF16 = mybir.dt.bfloat16
AF = mybir.ActivationFunctionType
ALU = mybir.AluOpType

NCORES = 8
SQ = 2
T = 2048
D = 1024
KC = D // 128
TG = 512
NTG = T // TG
DFF = 4096
CSOFT = 24.0
ISQD = 1.0 / np.sqrt(128.0)
ISQS = 1.0 / np.sqrt(64.0)
BF16NP = ml_dtypes.bfloat16


def _dag_plan():
    plan, lind = [], 0
    for c in range(6):
        nsrc = min(c + 1, 5)
        snode = c - nsrc
        plan.append((c, lind + nsrc - 1, lind, lind + nsrc // 2,
                     c - 1, snode, snode + nsrc // 2))
        lind += nsrc
    return plan


PLAN = _dag_plan()


def _node_projs(c):
    (_, eq, ek, ev, iq, ik, iv) = PLAN[c]
    projs, order = {}, []
    for e, i, r in ((eq, iq, 0), (ek, ik, 1), (ev, iv, 2)):
        if (e, i) not in projs:
            projs[(e, i)] = []
            order.append((e, i))
        projs[(e, i)].append(r)
    return [(e, i, projs[(e, i)]) for (e, i) in order]


EWBE_TRIPLES = []
for _c in range(6):
    for _e, _i, _roles in _node_projs(_c):
        for _r in _roles:
            EWBE_TRIPLES.append((_c, _r, _e))

PHASE_LIMIT = None
DEBUG_DUMP = []
DEBUG_LN = None
PROFILE_SCOPES = False
SIM_SAFE = False


def _ev(*evs):
    out = []
    for e in evs:
        if e is None:
            continue
        if isinstance(e, tuple) and e and isinstance(e[0], str):
            out.append(e)
        else:
            for x in e:
                if x is not None:
                    out.append(x)
    return tuple(out)


class K:
    ENGINES = ("tensor", "vector", "scalar", "sync", "gpsimd")

    def __init__(self, nc):
        self.nc = nc
        self.streams = {e: [] for e in self.ENGINES}
        self.cnt = {}
        self.lastwait = {}
        self.ndma_q = 0
        self.nphase = 0
        self.stopped = False
        self.phase_names = {}

    def _rec(self, eng, fn, waits=(), inc=None):
        if self.stopped:
            return None
        ws = []
        for w in waits:
            if w is None:
                continue
            s, c = w
            if SIM_SAFE and (s.startswith("q") or s.startswith("g")):
                c = self.cnt.get(s, 0)
            if c <= 0:
                continue
            key = (eng, s)
            if c > self.lastwait.get(key, 0):
                ws.append((s, c))
                self.lastwait[key] = c
        ev = None
        if inc is not None:
            s, amt = inc
            self.cnt[s] = self.cnt.get(s, 0) + amt
            ev = (s, self.cnt[s])
        self.streams[eng].append((ws, fn, inc, self.nphase))
        return ev

    def qdma(self, out, in_, waits=()):
        sem = "q%d" % ((self.ndma_q // 256) % 8)
        self.ndma_q += 1
        return self._rec("sync", lambda e: e.dma_start(out=out, in_=in_),
                         waits, (sem, 16))

    def gdma(self, out, in_, waits=()):
        return self._rec("gpsimd", lambda e: e.dma_start(out=out, in_=in_),
                         waits, ("g0", 16))

    def mm(self, out, lhsT, rhs, start, stop, waits=(), inc=False):
        return self._rec(
            "tensor",
            lambda e: e.matmul(out, lhsT, rhs, start=start, stop=stop),
            waits, ("t", 1) if inc else None)

    def act(self, out, in_, func, bias=0.0, scale=1.0, accum_out=None,
            waits=(), inc=True):
        return self._rec(
            "scalar",
            lambda e: e.activation(out, in_, func, bias=bias, scale=scale,
                                   accum_out=accum_out),
            waits, ("s", 1) if inc else None)

    def v(self, opname, *args, waits=(), inc=True, **kw):
        return self._rec(
            "vector", lambda e: getattr(e, opname)(*args, **kw),
            waits, ("v", 1) if inc else None)

    def vrecip(self, out, in_, scratch, waits=()):
        self.act(scratch, in_, AF.Ln, waits=waits)
        return self.act(out, scratch, AF.Exp, scale=-1.0)

    def now(self, sem):
        c = self.cnt.get(sem, 0)
        return (sem, c) if c > 0 else None

    def barrier(self, name="", final=False):
        if self.stopped:
            return
        self.nphase += 1
        bs = {"tensor": "bt", "vector": "bv", "scalar": "bs",
              "sync": "bq", "gpsimd": "bg"}
        evs = []
        for eng in self.ENGINES:
            s = bs[eng]
            evs.append(self._rec(eng, lambda e: e.nop(), (), (s, 1)))
        allw = list(evs)
        for s in list(self.cnt):
            if not s.startswith("b") and (final or s != "g0"):
                allw.append(self.now(s))
        sync_w = [w for w in evs if w is not None]
        sync_w += [w for w in (self.now(x) for x in ("t", "v", "s"))
                   if w is not None]
        for eng in self.ENGINES:
            ws = sync_w if (eng == "sync" and not final) else allw
            for w in ws:
                if w is not None:
                    self._rec(eng, None, (w,), None)
        self.phase_names[self.nphase - 1] = name
        if PHASE_LIMIT is not None and self.nphase >= PHASE_LIMIT:
            self.stopped = True

    def unstop(self):
        self.stopped = False

    def finalize(self, es):
        nc = self.nc
        names = (["t", "v", "s", "g0", "g1"]
                 + ["q%d" % i for i in range(8)]
                 + ["bt", "bv", "bs", "bq", "bg"])
        self.semh = {n: es.enter_context(nc.semaphore(n)) for n in names}
        blk = es.enter_context(nc.Block())

        def runner(stream):
            def run(eng):
                for (ws, fn, inc, _ph) in stream:
                    for (s, c) in ws:
                        eng.wait_ge(self.semh[s], c)
                    if fn is None:
                        continue
                    ins = fn(eng)
                    if inc is not None:
                        ins.then_inc(self.semh[inc[0]], inc[1])
            return run

        if not PROFILE_SCOPES:
            blk.tensor(runner(self.streams["tensor"]))
            blk.vector(runner(self.streams["vector"]))
            blk.scalar(runner(self.streams["scalar"]))
            blk.sync(runner(self.streams["sync"]))
            blk.gpsimd(runner(self.streams["gpsimd"]))
        else:
            nph = self.nphase + 1
            byph = {e: {} for e in self.ENGINES}
            for e in self.ENGINES:
                for rec in self.streams[e]:
                    byph[e].setdefault(rec[3], []).append(rec)
            for ph in range(nph + 1):
                nm = "ph%03d_%s" % (ph, self.phase_names.get(ph, "x"))
                has = any(byph[e].get(ph) for e in self.ENGINES)
                if not has:
                    continue
                with nc.named_scope(nm):
                    for e in self.ENGINES:
                        recs = byph[e].get(ph)
                        if recs:
                            getattr(blk, e)(runner(recs))


class Arena:
    def __init__(self, ap_f32):
        self.ap = ap_f32
        self.cap = ap_f32.shape[1]
        self.off = 0

    def reset(self):
        self.off = 0

    def take(self, p, n, dtype):
        nf = n if dtype == F32 else (n + 1) // 2
        off = (self.off + 7) & ~7
        self.off = off + nf
        assert self.off <= self.cap, "arena overflow: %d > %d" % (
            self.off, self.cap)
        sl = self.ap[0:p, off:off + nf]
        return sl if dtype == F32 else sl.bitcast(BF16)


def build_program():
    nc = bass.Bass("TRN2", target_bir_lowering=False)
    k = K(nc)

    def ein(name, shape, dt=F32):
        return nc.dram_tensor(name, list(shape), dt, kind="ExternalInput").ap()

    x_ext = ein("x", (SQ, 128, KC, T))
    out_ext = nc.dram_tensor("out", [SQ, 128, KC, T], F32,
                             kind="ExternalOutput").ap()

    W_We = ein("We", (20 * D, D))
    W_Wq = ein("Wqn", (6 * D, D))
    W_Wk = ein("Wkn", (6 * D, D))
    W_Wv = ein("Wvn", (6 * D, D))
    W_Wo = ein("Won", (6 * D, D))
    W_gvd = ein("Wgvd", (D, 3 * D))
    W_qkv = ein("Wqkv", (D, 3 * D))
    W_oT = ein("WoT", (D, D))
    W_gate = ein("WgateT", (D, DFF))
    W_up = ein("WupT", (D, DFF))
    W_out = ein("WoutT", (DFF, D))

    becols = ein("becols", (128, 20 * KC))
    bgvdc = ein("bgvd", (128, 3 * KC))
    ew_t = ein("ewt", (128, 18))
    actw_t = ein("actwt", (128, 6))
    lngc = ein("lng", (128, KC))
    lnbc = ein("lnb", (128, KC))
    rmscc = ein("rmsc", (128, 3 * KC))

    ones_col_f = nc.inline_tensor(np.ones((128, 1), np.float32), "onescf")
    ones_col_b = nc.inline_tensor(np.ones((128, 1), BF16NP), "onescb")
    ones_row_f = nc.inline_tensor(np.ones((1, 128), np.float32), "onesrf")
    ones_row_b = nc.inline_tensor(np.ones((1, 128), BF16NP), "onesrb")
    kk_, jj_ = np.meshgrid(np.arange(128), np.arange(128), indexing="ij")
    mask_cur = nc.inline_tensor((kk_ <= jj_).astype(BF16NP), "mcur")
    mask_prev = nc.inline_tensor((kk_ >= jj_ + 1).astype(BF16NP), "mprev")
    CONSTVALS = [1e-5, 1e-6, -CSOFT, -0.5, float(ISQD), float(ISQS),
                 0.5, 0.998, 0.001, 1.0, 0.0]
    cvals_d = nc.inline_tensor(
        np.tile(np.asarray(CONSTVALS, np.float32)[None, :], (128, 1)),
        "cvals")

    def scr(name, shape, dt):
        return nc.dram_tensor(name, list(shape), dt, kind="Internal").ap()

    fmshape = (SQ, 128, KC, T)
    R = scr("R", fmshape, F32)
    RS = scr("RS", fmshape, F32)
    XN = scr("XN", fmshape, BF16)
    ND = scr("ND", (5,) + fmshape, BF16)
    EQ = scr("EQ", fmshape, BF16)
    EK = scr("EK", fmshape, BF16)
    EV = scr("EV", fmshape, BF16)
    QD = scr("QD", fmshape, F32)
    N5 = scr("N5", fmshape, F32)
    QP = scr("QP", fmshape, BF16)
    KP = scr("KP", fmshape, BF16)
    VP = scr("VP", (SQ, 128, T // 128, D), BF16)
    CTX = scr("CTX", fmshape, BF16)
    XS = scr("XS", fmshape, F32)
    AA = scr("AA", fmshape, F32)
    GU = scr("GU", (SQ, 128, DFF // 128, T), BF16)

    SCRATCH = {"R": R, "RS": RS, "XN": XN, "ND": ND, "EQ": EQ, "EK": EK,
               "EV": EV, "QD": QD, "N5": N5, "QP": QP, "KP": KP, "VP": VP,
               "CTX": CTX, "XS": XS, "AA": AA, "GU": GU}
    global DEBUG_LN
    if DEBUG_LN is not None:
        DEBUG_LN = nc.dram_tensor("dbg_lnrows", [5, 128, TG], F32,
                                  kind="ExternalOutput").ap()
    dbg_outs = {}
    for nm in DEBUG_DUMP:
        src = SCRATCH[nm]
        dbg_outs[nm] = nc.dram_tensor(
            "dbg_%s" % nm, list(src.shape), src.dtype,
            kind="ExternalOutput").ap()

    cst = nc.alloc_sbuf_tensor("cst", [128, 400], F32).ap()
    onescf_sb = nc.alloc_sbuf_tensor("onescf_s", [128, 1], F32).ap()
    onescb_sb = nc.alloc_sbuf_tensor("onescb_s", [128, 1], BF16).ap()
    onesrf_sb = nc.alloc_sbuf_tensor("onesrf_s", [1, 128], F32).ap()
    onesrb_sb = nc.alloc_sbuf_tensor("onesrb_s", [1, 128], BF16).ap()
    mcur_sb = nc.alloc_sbuf_tensor("mcur_s", [128, 128], BF16).ap()
    mprev_sb = nc.alloc_sbuf_tensor("mprev_s", [128, 128], BF16).ap()
    rows = nc.alloc_sbuf_tensor("rows", [1, 3 * T], F32).ap()
    rowsb = nc.alloc_sbuf_tensor("rowsb", [1, T], BF16).ap()
    cvals_sb = nc.alloc_sbuf_tensor("cvals_s", [128, len(CONSTVALS)],
                                    F32).ap()
    for ci, cv in enumerate(CONSTVALS):
        nc.const_aps.register(F32, cv, cvals_sb[:, ci:ci + 1])

    WPOOL = nc.alloc_sbuf_tensor("wpool", [128, 49152], BF16).ap()
    wslot = [WPOOL[:, 0:24576], WPOOL[:, 24576:49152]]
    wstate = {"i": 0, "free": [None, None]}
    rem = nc.sbuf_bytes_remaining
    arena_words = (rem - 2048) // 4
    AR = nc.alloc_sbuf_tensor("arena", [128, arena_words], F32).ap()
    ar = Arena(AR)

    pbig = [nc.alloc_psum_tensor("pA", [128, 1024], F32).ap(),
            nc.alloc_psum_tensor("pB", [128, 1024], F32).ap()]
    pb = [nc.alloc_psum_tensor("p%d" % i, [128, 512], F32).ap()
          for i in range(4)]

    c_be, c_bgvd, c_ew, c_actw = 0, 160, 184, 202
    c_lng, c_lnb, c_rms, c_ewbe = 208, 216, 224, 248

    evs = [
        k.qdma(cst[:, c_be:c_be + 160], becols),
        k.qdma(cst[:, c_bgvd:c_bgvd + 24], bgvdc),
        k.qdma(cst[:, c_ew:c_ew + 18], ew_t),
        k.qdma(cst[:, c_actw:c_actw + 6], actw_t),
        k.qdma(cst[:, c_lng:c_lng + 8], lngc),
        k.qdma(cst[:, c_lnb:c_lnb + 8], lnbc),
        k.qdma(cst[:, c_rms:c_rms + 24], rmscc),
        k.qdma(onescf_sb, ones_col_f.ap()),
        k.qdma(onescb_sb, ones_col_b.ap()),
        k.qdma(onesrf_sb, ones_row_f.ap()),
        k.qdma(onesrb_sb, ones_row_b.ap()),
        k.qdma(mcur_sb, mask_cur.ap()),
        k.qdma(mprev_sb, mask_prev.ap()),
        k.qdma(cvals_sb, cvals_d.ap()),
    ]
    for idx, (c, r, e) in enumerate(EWBE_TRIPLES):
        k.v("tensor_scalar",
            cst[:, c_ewbe + idx * 8:c_ewbe + (idx + 1) * 8],
            cst[:, c_be + e * 8:c_be + (e + 1) * 8],
            cst[:, c_ew + c * 3 + r:c_ew + c * 3 + r + 1], None,
            ALU.mult, waits=_ev(*evs))
    k.barrier("prep")

    def ew_col(c, r):
        return cst[:, c_ew + c * 3 + r:c_ew + c * 3 + r + 1]

    def ewbe_col(c, r, e, kc):
        i = EWBE_TRIPLES.index((c, r, e))
        return cst[:, c_ewbe + i * 8 + kc:c_ewbe + i * 8 + kc + 1]

    def becol(e, kc):
        return cst[:, c_be + e * 8 + kc:c_be + e * 8 + kc + 1]

    def bgvd_col(g, kc):
        return cst[:, c_bgvd + g * 8 + kc:c_bgvd + g * 8 + kc + 1]

    def actw_col(c):
        return cst[:, c_actw + c:c_actw + c + 1]

    def rms_col(g, kc):
        return cst[:, c_rms + g * 8 + kc:c_rms + g * 8 + kc + 1]

    def lng_col(kc):
        return cst[:, c_lng + kc:c_lng + kc + 1]

    def lnb_col(kc):
        return cst[:, c_lnb + kc:c_lnb + kc + 1]


    def emit_ln(src, kind, eps, dst_bf=None, dst_f32=None, rms_g=None,
                combine=None, name="ln"):
        ar.reset()
        xb = [ar.take(128, KC * TG, F32) for _ in range(2)]
        xsq = ar.take(128, KC * TG, BF16)
        tmp = ar.take(128, TG, F32)
        tmp2 = ar.take(128, TG, F32)
        ob = [ar.take(128, TG, F32) for _ in range(4)]
        ob2 = [ar.take(128, TG, BF16) for _ in range(4)]
        rt = [ar.take(128, KC * TG, F32)]
        mrow = rows[:, 0:TG]
        vrow = rows[:, TG:2 * TG]
        m2row = rows[:, 2 * TG:3 * TG]
        lnrow = rows[:, 3 * TG:4 * TG]
        rstdrow = rows[:, 4 * TG:5 * TG]
        sum_ps, sq_ps, mbc, rbc = pb[0], pb[1], pb[2], pb[3]
        ev_x = [None, None]
        ev_xrd = [None, None]
        ev_rt = [None, None]
        ev_rt_rd = [None, None]
        ev_sum_rd = ev_sq_rd = None
        ev_xsq_rd = None
        ev_mbc_rd = ev_rbc_rd = None
        ev_rows_t = None
        ev_ob = [None] * 4
        ev_ob2 = [None] * 4
        tiles_ln = [(s, tg) for s in range(SQ) for tg in range(NTG)]
        ev_xl = [None] * 8

        def issue_x(idx):
            s, tg = tiles_ln[idx]
            i = idx % 2
            ev_xl[idx] = k.qdma(
                xb[i].rearrange("p (kc t) -> p kc t", kc=KC),
                src[s, :, :, tg * TG:(tg + 1) * TG],
                waits=_ev(ev_x[i], ev_xrd[i]))

        issue_x(0)
        for idx in range(len(tiles_ln)):
            s, tg = tiles_ln[idx]
            if True:
                i = idx % 2
                ri = 0
                t0 = tg * TG
                if idx + 1 < len(tiles_ln):
                    issue_x(idx + 1)
                ev_x[i] = ev_xl[idx]
                if combine is not None:
                    ev_rt[ri] = k.qdma(
                        rt[ri].rearrange("p (kc t) -> p kc t", kc=KC),
                        combine["rsrc"][s, :, :, t0:t0 + TG],
                        waits=_ev(ev_rt[ri], ev_rt_rd[ri]))
                ev_sqa = []
                for c in range(KC):
                    ev_sqa.append(k.act(
                        xsq[:, c * TG:(c + 1) * TG],
                        xb[i][:, c * TG:(c + 1) * TG], AF.Square,
                        waits=_ev(ev_x[i], ev_xsq_rd if c == 0 else None)))
                ev_sum = None
                if kind != "rms":
                    for c in range(KC):
                        ev_sum = k.mm(
                            sum_ps[0:1, :], onescf_sb,
                            xb[i][:, c * TG:(c + 1) * TG],
                            start=(c == 0), stop=(c == KC - 1),
                            waits=_ev(ev_x[i], ev_sum_rd),
                            inc=(c == KC - 1))
                ev_sq = None
                for c in range(KC):
                    ev_sq = k.mm(
                        sq_ps[0:1, :], onescb_sb,
                        xsq[:, c * TG:(c + 1) * TG],
                        start=(c == 0), stop=(c == KC - 1),
                        waits=_ev(ev_sqa[c], ev_sq_rd), inc=(c == KC - 1))
                ev_xsq_rd = ev_sq
                if kind != "rms":
                    ev_m = k.v("tensor_scalar", mrow, sum_ps[0:1, :],
                               1.0 / D, None, ALU.mult,
                               waits=_ev(ev_sum, ev_rows_t))
                    ev_sum_rd = ev_m
                    ev_m2 = k.v("tensor_mul", m2row, mrow, mrow)
                    ev_v0 = k.v("tensor_scalar", vrow, sq_ps[0:1, :],
                                1.0 / D, None, ALU.mult, waits=_ev(ev_sq))
                    ev_sq_rd = ev_v0
                    ev_var = k.v("tensor_sub", vrow, vrow, m2row,
                                 waits=_ev(ev_m2))
                else:
                    ev_var = k.v("tensor_scalar", vrow, sq_ps[0:1, :],
                                 1.0 / D, None, ALU.mult,
                                 waits=_ev(ev_sq, ev_rows_t))
                    ev_sq_rd = ev_var
                    ev_m = None
                ev_ln = k.act(lnrow, vrow, AF.Ln, bias=eps, waits=_ev(ev_var))
                ev_rstd = k.act(rstdrow, lnrow, AF.Exp, scale=-0.5)
                ev_mbc = None
                if kind != "rms":
                    ev_mbc = k.mm(mbc, onesrf_sb, mrow, True, True,
                                  waits=_ev(ev_m, ev_mbc_rd), inc=True)
                ev_rbc = k.mm(rbc, onesrf_sb, rstdrow, True, True,
                              waits=_ev(ev_rstd, ev_rbc_rd), inc=True)
                ev_rows_t = (ev_mbc, ev_rbc)
                if DEBUG_LN is not None and s == 0 and tg == 0:
                    dr = DEBUG_LN
                    k.qdma(dr[0, 0:1, 0:TG], mrow, waits=_ev(ev_m))
                    k.qdma(dr[1, 0:1, 0:TG], vrow, waits=_ev(ev_var))
                    k.qdma(dr[2, 0:1, 0:TG], rstdrow, waits=_ev(ev_rstd))
                    ebc = k.v("tensor_copy", ob[3], rbc, waits=_ev(ev_rbc))
                    k.qdma(dr[3, :, 0:TG], ob[3], waits=_ev(ebc))
                    ebc2 = k.v("tensor_copy", ob[2], mbc, waits=_ev(ev_mbc))
                    k.qdma(dr[4, :, 0:TG], ob[2], waits=_ev(ebc2))
                    ev_ob[3] = k.now("q%d" % (((k.ndma_q - 1) // 256) % 8))
                    ev_ob[2] = ev_ob[3]
                last_apply = None
                for c in range(KC):
                    j = c % 4
                    xc = xb[i][:, c * TG:(c + 1) * TG]
                    if kind == "rms":
                        e1 = k.v("tensor_tensor", tmp, xc, rbc, ALU.mult,
                                 waits=_ev(ev_rbc))
                        eo = k.v("tensor_scalar", ob2[j], tmp,
                                 rms_col(rms_g, c), None, ALU.mult,
                                 waits=_ev(ev_ob2[j]))
                        ev_rbc_rd = e1
                        ev_ob2[j] = k.qdma(dst_bf[s, :, c, t0:t0 + TG],
                                           ob2[j], waits=_ev(eo))
                        last_apply = e1
                    elif kind == "ln":
                        e1 = k.v("tensor_tensor", tmp, xc, mbc, ALU.subtract,
                                 waits=_ev(ev_mbc))
                        ev_mbc_rd = e1
                        if dst_f32 is not None:
                            eo = k.v("tensor_tensor", ob[j], tmp, rbc,
                                     ALU.mult, waits=_ev(ev_rbc, ev_ob[j]))
                            ev_rbc_rd = eo
                            ev_f = k.qdma(dst_f32[s, :, c, t0:t0 + TG],
                                          ob[j], waits=_ev(eo))
                            ec = k.act(ob2[j], ob[j], AF.Copy,
                                       waits=_ev(eo, ev_ob2[j]))
                            ev_ob[j] = (ev_f, ec)
                            ev_ob2[j] = k.qdma(dst_bf[s, :, c, t0:t0 + TG],
                                               ob2[j], waits=_ev(ec))
                            last_apply = e1
                        else:
                            eo = k.v("tensor_tensor", ob2[j], tmp, rbc,
                                     ALU.mult, waits=_ev(ev_rbc, ev_ob2[j]))
                            ev_rbc_rd = eo
                            if (DEBUG_LN is not None and s == 0 and tg == 0
                                    and c == 0):
                                k.qdma(DEBUG_LN[1, :, 0:TG], tmp,
                                       waits=_ev(e1, eo))
                            ev_ob2[j] = k.qdma(dst_bf[s, :, c, t0:t0 + TG],
                                               ob2[j], waits=_ev(eo))
                    else:
                        e1 = k.v("tensor_tensor", tmp, xc, mbc, ALU.subtract,
                                 waits=_ev(ev_mbc))
                        ev_mbc_rd = e1
                        e2 = k.v("tensor_tensor", tmp2, tmp, rbc, ALU.mult,
                                 waits=_ev(ev_rbc))
                        ev_rbc_rd = e2
                        e3 = k.v("tensor_scalar", tmp, tmp2, lng_col(c),
                                 lnb_col(c), ALU.mult, ALU.add)
                        eo = k.v("scalar_tensor_tensor", ob[j], tmp, 0.5,
                                 rt[ri][:, c * TG:(c + 1) * TG],
                                 ALU.mult, ALU.add,
                                 waits=_ev(ev_rt[ri], ev_ob[j]))
                        ev_rt_rd[ri] = eo
                        ev_ob[j] = k.qdma(
                            combine["rdst"][s, :, c, t0:t0 + TG], ob[j],
                            waits=_ev(eo))
                        last_apply = e1
                ev_xrd[i] = last_apply
        k.barrier(name)

    def load_weight(wsb, Wd, nkc, O, col0=0, war=None):
        evl = None
        for kc in range(nkc):
            evl = k.gdma(wsb[:, kc * O:(kc + 1) * O],
                         Wd[kc * 128:(kc + 1) * 128, col0:col0 + O],
                         waits=_ev(war) if kc == 0 else ())
        return evl

    def emit_gemm(src, Wd, Kdim, O, evac, wcol0=0, name="gemm",
                  pre_alloc=None):
        ar.reset()
        nkc, noc = Kdim // 128, O // 128
        nxb = 2 if nkc <= 16 else 1
        if nkc * O <= 24576:
            si = wstate["i"] % 2
            wstate["i"] += 1
            wsb = wslot[si][:, 0:nkc * O]
            wwar = wstate["free"][si]
            wowner = [si]
        else:
            wsb = WPOOL[:, 0:nkc * O]
            wwar = _ev(wstate["free"][0], wstate["free"][1])
            wowner = [0, 1]
        xb = [ar.take(128, nkc * TG, BF16) for _ in range(nxb)]
        ctx = {"evf": [ar.take(128, TG, F32) for _ in range(4)],
               "evb": [ar.take(128, TG, BF16) for _ in range(4)],
               "evf_ev": [None] * 4, "evb_ev": [None] * 4,
               "rt": [ar.take(128, KC * TG, F32) for _ in range(nxb)],
               "rt_ev": [None] * nxb, "rt_rd": [None] * nxb, "ar": ar,
               "nxb": nxb}
        if pre_alloc:
            pre_alloc(ctx)
        ev_w = load_weight(wsb, Wd, nkc, O, wcol0, wwar)
        banks = [pbig[0][:, 0:512], pbig[0][:, 512:1024],
                 pbig[1][:, 0:512], pbig[1][:, 512:1024]]
        bank_free = [None] * 4
        ev_xfree = [None] * nxb
        ev_xl = [None] * 8
        nb = 0
        tiles = [(s, tg) for s in range(SQ) for tg in range(NTG)]

        def issue_loads(idx):
            s, tg = tiles[idx]
            i = idx % nxb
            t0 = tg * TG
            ev_xl[idx] = k.qdma(
                xb[i].rearrange("p (kc t) -> p kc t", kc=nkc),
                src[s, :, 0:nkc, t0:t0 + TG], waits=_ev(ev_xfree[i]))
            if "rt_load" in ctx:
                ctx["rt_ev"][i] = k.qdma(
                    ctx["rt"][i].rearrange("p (kc t) -> p kc t", kc=KC),
                    ctx["rt_load"][s, :, :, t0:t0 + TG],
                    waits=_ev(ctx["rt_ev"][i], ctx["rt_rd"][i]))

        issue_loads(0)
        for idx in range(len(tiles)):
            s, tg = tiles[idx]
            i = idx % nxb
            t0 = tg * TG
            if nxb > 1 and idx + 1 < len(tiles):
                issue_loads(idx + 1)
            ev_x = ev_xl[idx]
            if True:
                ev_last = None
                for oc in range(noc):
                    b = nb % 4
                    nb += 1
                    ev_mm = None
                    for kc in range(nkc):
                        ev_mm = k.mm(
                            banks[b],
                            wsb[:, kc * O + oc * 128:kc * O + (oc + 1) * 128],
                            xb[i][:, kc * TG:(kc + 1) * TG],
                            start=(kc == 0), stop=(kc == nkc - 1),
                            waits=_ev(ev_w, ev_x, bank_free[b]),
                            inc=(kc == nkc - 1))
                    bank_free[b] = evac(ctx, s, oc, tg, banks[b], ev_mm, i)
                    ev_last = ev_mm
                ev_xfree[i] = ev_last
                if nxb == 1 and idx + 1 < len(tiles):
                    issue_loads(idx + 1)
        for si in wowner:
            wstate["free"][si] = ev_last
        k.barrier(name)

    def emit_gemm_tm(src, Wd, Kdim, O, dst, dcol0=0, wcol0=0, name="gemmtm"):
        ar.reset()
        nkc, noh = Kdim // 128, O // 512
        si = wstate["i"] % 2
        wstate["i"] += 1
        wsb = wslot[si][:, 0:nkc * O]
        xb = [ar.take(128, nkc * TG, BF16) for _ in range(2)]
        evb = [ar.take(128, TG, BF16) for _ in range(4)]
        evb_ev = [None] * 4
        ev_w = load_weight(wsb, Wd, nkc, O, wcol0, wstate["free"][si])
        banks = [pbig[0][:, 0:512], pbig[0][:, 512:1024],
                 pbig[1][:, 0:512], pbig[1][:, 512:1024]]
        bank_free = [None] * 4
        ev_xfree = [None, None]
        ev_xl = [None] * 8
        nb = 0
        tiles = [(s, tg) for s in range(SQ) for tg in range(NTG)]

        def issue_loads(idx):
            s, tg = tiles[idx]
            i = idx % 2
            ev_xl[idx] = k.qdma(
                xb[i].rearrange("p (kc t) -> p kc t", kc=nkc),
                src[s, :, 0:nkc, tg * TG:(tg + 1) * TG],
                waits=_ev(ev_xfree[i]))

        issue_loads(0)
        for idx in range(len(tiles)):
            s, tg = tiles[idx]
            i = idx % 2
            t0 = tg * TG
            if idx + 1 < len(tiles):
                issue_loads(idx + 1)
            ev_x = ev_xl[idx]
            if True:
                ev_last = None
                for j in range(TG // 128):
                    for oh in range(noh):
                        b = nb % 4
                        nb += 1
                        ev_mm = None
                        for kc in range(nkc):
                            ev_mm = k.mm(
                                banks[b],
                                xb[i][:, kc * TG + j * 128:
                                      kc * TG + (j + 1) * 128],
                                wsb[:, kc * O + oh * 512:
                                    kc * O + (oh + 1) * 512],
                                start=(kc == 0), stop=(kc == nkc - 1),
                                waits=_ev(ev_w, ev_x, bank_free[b]),
                                inc=(kc == nkc - 1))
                        sl = b
                        ec = k.act(evb[sl], banks[b], AF.Copy,
                                   waits=_ev(ev_mm, evb_ev[sl]))
                        bank_free[b] = ec
                        evb_ev[sl] = k.qdma(
                            dst[s, :, tg * (TG // 128) + j,
                                dcol0 + oh * 512:dcol0 + (oh + 1) * 512],
                            evb[sl], waits=_ev(ec))
                        ev_last = ev_mm
                ev_xfree[i] = ev_last
        wstate["free"][si] = ev_last
        k.barrier(name)


    def mk_evac_plain(dst, dst2=None, split_oc=None):
        def evac(ctx, s, oc, tg, ps, ev_mm, i):
            sl = oc % 4
            d, o2 = (dst, oc) if (split_oc is None or oc < split_oc) \
                else (dst2, oc - split_oc)
            ec = k.act(ctx["evb"][sl], ps, AF.Copy,
                       waits=_ev(ev_mm, ctx["evb_ev"][sl]))
            ctx["evb_ev"][sl] = k.qdma(
                d[s, :, o2, tg * TG:(tg + 1) * TG], ctx["evb"][sl],
                waits=_ev(ec))
            return ec
        return evac

    def mk_evac_edge(c, e, roles):
        multi = len(roles) > 1
        def evac(ctx, s, oc, tg, ps, ev_mm, i):
            t0 = tg * TG
            if multi:
                sf = oc % 2
                e_base = k.v("tensor_scalar", ctx["evf"][sf], ps,
                             becol(e, oc), None, ALU.add,
                             waits=_ev(ev_mm, ctx["evf_ev"][sf]))
                base = ctx["evf"][sf]
                readers = [e_base]
                for r in roles:
                    dstd = (EQ, EK, EV)[r]
                    sl = (oc * 2 + (0 if r == 0 else 1)) % 4
                    if r == 0:
                        sf2 = 2 + oc % 2
                        eq_ = k.v("tensor_scalar", ctx["evf"][sf2], base,
                                  ew_col(c, 0), None, ALU.mult,
                                  waits=_ev(ctx["evf_ev"][sf2]))
                        eqd = k.qdma(QD[s, :, oc, t0:t0 + TG],
                                     ctx["evf"][sf2], waits=_ev(eq_))
                        ecp = k.act(ctx["evb"][sl], ctx["evf"][sf2], AF.Copy,
                                    waits=_ev(eq_, ctx["evb_ev"][sl]))
                        ctx["evf_ev"][sf2] = (eqd, ecp)
                        ctx["evb_ev"][sl] = k.qdma(
                            dstd[s, :, oc, t0:t0 + TG], ctx["evb"][sl],
                            waits=_ev(ecp))
                        readers.append(eq_)
                    else:
                        ecp = k.act(ctx["evb"][sl], base, AF.Identity,
                                    scale=ew_col(c, r),
                                    waits=_ev(e_base, ctx["evb_ev"][sl]))
                        ctx["evb_ev"][sl] = k.qdma(
                            dstd[s, :, oc, t0:t0 + TG], ctx["evb"][sl],
                            waits=_ev(ecp))
                        readers.append(ecp)
                ctx["evf_ev"][sf] = tuple(readers)
                return e_base
            r = roles[0]
            if r == 0:
                sf = oc % 4
                sl = oc % 4
                eq_ = k.v("tensor_scalar", ctx["evf"][sf], ps,
                          becol(e, oc), ew_col(c, 0), ALU.add, ALU.mult,
                          waits=_ev(ev_mm, ctx["evf_ev"][sf]))
                eqd = k.qdma(QD[s, :, oc, t0:t0 + TG], ctx["evf"][sf],
                             waits=_ev(eq_))
                ecp = k.act(ctx["evb"][sl], ctx["evf"][sf], AF.Copy,
                            waits=_ev(eq_, ctx["evb_ev"][sl]))
                ctx["evf_ev"][sf] = (eqd, ecp)
                ctx["evb_ev"][sl] = k.qdma(EQ[s, :, oc, t0:t0 + TG],
                                           ctx["evb"][sl], waits=_ev(ecp))
                return eq_
            dstd = (EQ, EK, EV)[r]
            sl = oc % 4
            ecp = k.act(ctx["evb"][sl], ps, AF.Identity, scale=ew_col(c, r),
                        bias=ewbe_col(c, r, e, oc),
                        waits=_ev(ev_mm, ctx["evb_ev"][sl]))
            ctx["evb_ev"][sl] = k.qdma(dstd[s, :, oc, t0:t0 + TG],
                                       ctx["evb"][sl], waits=_ev(ecp))
            return ecp
        return evac

    def mk_evac_combine(c):
        def evac(ctx, s, oc, tg, ps, ev_mm, i):
            t0 = tg * TG
            rt = ctx["rt"][i][:, oc * TG:(oc + 1) * TG]
            sl = oc % 4
            buf = ctx["evb"] if c < 5 else ctx["evf"]
            bev = "evb_ev" if c < 5 else "evf_ev"
            eo = k.v("scalar_tensor_tensor", buf[sl], ps, actw_col(c), rt,
                     ALU.mult, ALU.add,
                     waits=_ev(ev_mm, ctx["rt_ev"][i], ctx[bev][sl]))
            ctx["rt_rd"][i] = eo
            dstd = ND[c] if c < 5 else N5
            ctx[bev][sl] = k.qdma(dstd[s, :, oc, t0:t0 + TG], buf[sl],
                                  waits=_ev(eo))
            return eo
        return evac

    def mk_evac_resid(rdst):
        def evac(ctx, s, oc, tg, ps, ev_mm, i):
            t0 = tg * TG
            sl = oc % 4
            rt = ctx["rt"][i][:, oc * TG:(oc + 1) * TG]
            eo = k.v("tensor_tensor", ctx["evf"][sl], ps, rt, ALU.add,
                     waits=_ev(ev_mm, ctx["rt_ev"][i], ctx["evf_ev"][sl]))
            ctx["rt_rd"][i] = eo
            ctx["evf_ev"][sl] = k.qdma(rdst[s, :, oc, t0:t0 + TG],
                                       ctx["evf"][sl], waits=_ev(eo))
            return eo
        return evac

    def mk_evac_mingru():
        def pre(ctx):
            ctx["sg"] = ctx["ar"].take(128, KC * TG, F32)
            ctx["sg_w"] = [None] * KC
            ctx["sg_rd"] = [None] * KC
        def evac(ctx, s, oc, tg, ps, ev_mm, i):
            t0 = tg * TG
            grp, cc = oc // KC, oc % KC
            if grp == 0:
                eo = k.act(ctx["sg"][:, cc * TG:(cc + 1) * TG], ps,
                           AF.Sigmoid, bias=bgvd_col(0, cc),
                           waits=_ev(ev_mm, ctx["sg_rd"][cc]))
                ctx["sg_w"][cc] = eo
                return eo
            if grp == 1:
                sf = cc % 2
                et = k.act(ctx["evf"][sf], ps, AF.Tanh, bias=bgvd_col(1, cc),
                           waits=_ev(ev_mm, ctx["evf_ev"][sf]))
                ex = k.v("tensor_tensor", ctx["evf"][2 + sf],
                         ctx["sg"][:, cc * TG:(cc + 1) * TG], ctx["evf"][sf],
                         ALU.mult,
                         waits=_ev(et, ctx["sg_w"][cc],
                                   ctx["evf_ev"][2 + sf]))
                ctx["sg_rd"][cc] = ex
                ctx["evf_ev"][sf] = ex
                ctx["evf_ev"][2 + sf] = k.qdma(
                    XS[s, :, cc, t0:t0 + TG], ctx["evf"][2 + sf],
                    waits=_ev(ex))
                return et
            sf = cc % 2
            et = k.act(ctx["evf"][sf], ps, AF.Sigmoid, bias=bgvd_col(2, cc),
                       waits=_ev(ev_mm, ctx["evf_ev"][sf]))
            ea = k.v("tensor_scalar", ctx["evf"][2 + sf], ctx["evf"][sf],
                     0.998, 0.001, ALU.mult, ALU.add,
                     waits=_ev(et, ctx["evf_ev"][2 + sf]))
            ctx["evf_ev"][sf] = ea
            ctx["evf_ev"][2 + sf] = k.qdma(
                AA[s, :, cc, t0:t0 + TG], ctx["evf"][2 + sf],
                waits=_ev(ea))
            return et
        return pre, evac


    def emit_mha(name="mha"):
        ar.reset()
        qsl = [ar.take(128, T, BF16) for _ in range(2)]
        ksl = [ar.take(128, T, BF16) for _ in range(2)]
        vsl = [ar.take(128, T, BF16) for _ in range(2)]
        ptb = [ar.take(128, 16 * TG, BF16) for _ in range(2)]
        invbc = ar.take(128, TG, BF16)
        ctxn = [ar.take(128, TG, BF16) for _ in range(2)]
        lrow = rows[:, 0:TG]
        invrow = rows[:, TG:2 * TG]
        rscr = rows[:, 2 * TG:3 * TG]
        invrow_b = [rowsb[:, 0:TG], rowsb[:, TG:2 * TG]]
        sc_ps = [pbig[0], pbig[1]]
        ctx_ps = [pb[0], pb[1]]
        lb_ps = [pb[2], pb[3]]
        heads = [(s, h) for s in range(SQ) for h in range(KC)]
        nh = len(heads)
        nu = nh * 4
        ld_q = [None] * nh
        ld_k = [None] * nh
        ld_v = [None] * nh
        exp_ev = {}
        av_ev = [None] * nu
        l_ev = [None] * nu
        lc_ev = [None] * nu
        cast_ev = [None] * nu
        bcm_ev = [None] * nu
        ib_ev = [None] * nu
        ptb_war = [None, None]
        ctx_war = [None, None]
        ctxn_war = [None, None]
        st = {"lrow_war": None}

        def do_load(hi, war):
            sH, hH = heads[hi]
            jj = hi % 2
            ld_q[hi] = k.qdma(qsl[jj], QP[sH, :, hH, :],
                              waits=_ev(ld_q[hi - 2] if hi >= 2 else None,
                                        war))
            ld_k[hi] = k.qdma(ksl[jj], KP[sH, :, hH, :])
            ld_v[hi] = k.qdma(
                vsl[jj].rearrange("p (kt d) -> p kt d", kt=16),
                VP[sH, :, :, hH * 128:(hH + 1) * 128])

        def fin1(v):
            lc_ev[v] = k.v("tensor_copy", lrow, lb_ps[v % 2][0:1, :],
                           waits=_ev(l_ev[v], st["lrow_war"]))
            ev_r = k.vrecip(invrow, lrow, rscr,
                            waits=_ev(lc_ev[v],
                                      cast_ev[v - 1] if v >= 1 else None))
            st["lrow_war"] = ev_r
            cast_ev[v] = k.v("tensor_copy", invrow_b[v % 2], invrow,
                             waits=_ev(ev_r,
                                       bcm_ev[v - 2] if v >= 2 else None))

        def fin2(v):
            hv = v // 4
            sH, hH = heads[hv]
            q0 = (v % 4) * TG
            sl = v % 2
            bcm_ev[v] = k.mm(lb_ps[v % 2], onesrb_sb, invrow_b[v % 2],
                             True, True,
                             waits=_ev(cast_ev[v], lc_ev[v]), inc=True)
            ib_ev[v] = k.v("tensor_copy", invbc, lb_ps[v % 2],
                           waits=_ev(bcm_ev[v]))
            en = k.v("tensor_tensor", ctxn[sl], ctx_ps[v % 2], invbc,
                     ALU.mult, waits=_ev(av_ev[v], ctxn_war[sl]))
            ctx_war[v % 2] = en
            ctxn_war[sl] = k.qdma(CTX[sH, :, hH, q0:q0 + TG], ctxn[sl],
                                  waits=_ev(en))

        def emit_avl_pair(v, kt2):
            hv = v // 4
            jv = hv % 2
            ptv = ptb[v % 2]
            for uu in range(2):
                kt = kt2 * 2 + uu
                av_ev[v] = k.mm(
                    ctx_ps[v % 2], vsl[jv][:, kt * 128:(kt + 1) * 128],
                    ptv[:, kt * TG:(kt + 1) * TG],
                    start=(kt == 0), stop=(kt == 15),
                    waits=_ev(ld_v[hv], exp_ev[(v, kt2)],
                              ctx_war[v % 2] if kt == 0 else None),
                    inc=(kt == 15))
            for uu in range(2):
                kt = kt2 * 2 + uu
                l_ev[v] = k.mm(
                    lb_ps[v % 2][0:1, :], onescb_sb,
                    ptv[:, kt * TG:(kt + 1) * TG],
                    start=(kt == 0), stop=(kt == 15),
                    waits=_ev(ib_ev[v - 2] if (kt == 0 and v >= 2) else None,
                              lc_ev[v - 2] if (kt == 0 and v >= 2) else None),
                    inc=(kt == 15))
            if kt2 == 7:
                ptb_war[v % 2] = av_ev[v]

        do_load(0, None)
        for u in range(nu):
            hi, qg = divmod(u, 4)
            j = hi % 2
            pt = ptb[u % 2]
            q0 = qg * TG
            if qg == 1 and hi + 1 < nh:
                do_load(hi + 1, av_ev[hi * 4 - 1] if hi >= 1 else None)
            for kt2 in range(8):
                sp = sc_ps[kt2 % 2]
                bw = (exp_ev.get((u, kt2 - 2)) if kt2 >= 2
                      else exp_ev.get((u - 1, 6 + kt2)))
                ev_s = None
                for uu in range(2):
                    kt = kt2 * 2 + uu
                    ev_s = k.mm(
                        sp[:, uu * 512:(uu + 1) * 512],
                        ksl[j][:, kt * 128:(kt + 1) * 128],
                        qsl[j][:, q0:q0 + TG], start=True, stop=True,
                        waits=_ev(ld_q[hi], ld_k[hi], bw), inc=(uu == 1))
                ee = k.act(pt[:, kt2 * 1024:(kt2 + 1) * 1024], sp, AF.Exp,
                           bias=-CSOFT, scale=ISQD,
                           waits=_ev(ev_s,
                                     ptb_war[u % 2] if kt2 == 0 else None))
                exp_ev[(u, kt2)] = ee
                if u > 0:
                    emit_avl_pair(u - 1, kt2)
            if u >= 2:
                fin2(u - 2)
            if u > 0:
                fin1(u - 1)
        for kt2 in range(8):
            emit_avl_pair(nu - 1, kt2)
        fin2(nu - 2)
        fin1(nu - 1)
        fin2(nu - 1)
        k.barrier(name)


    def emit_sliding(name="slide"):
        ar.reset()
        qsl = [ar.take(64, T, BF16) for _ in range(2)]
        ksl = [ar.take(64, T, BF16) for _ in range(2)]
        vsl = [ar.take(128, 16 * 64, BF16) for _ in range(2)]
        ptm = [ar.take(128, 128, BF16) for _ in range(4)]
        ctxu = ar.take(64, T, BF16)
        invbc = ar.take(64, T, BF16)
        ctxn = ar.take(64, T, BF16)
        lrow = rows[:, 0:T]
        invrow = rows[:, T:2 * T]
        rscr = rows[:, 2 * T:3 * T]
        invrow_b = rowsb[:, 0:T]
        sc_ps = [pb[0], pb[1]]
        ctx_ps = [pb[2][0:64, 0:128], pb[3][0:64, 0:128]]
        l_ps = [pbig[0][:, 0:512], pbig[0][:, 512:1024]]
        bc_ps = pbig[1][0:64, 0:512]
        ev_q = [None, None]
        ev_k_ = [None, None]
        ev_v_ = [None, None]
        ev_pt = [None] * 4
        ev_sc = [None, None]
        ev_ctx = [None, None]
        ev_l_rd = [None, None]
        ev_lrow_rd = None
        ev_ctxu_rd = None
        ev_bc_rd = None
        ev_invbc_rd = None
        ev_ctxn_rd = None
        heads = [(s, h) for s in range(SQ) for h in range(16)]
        ev_hd_rd = [None, None]
        for hi, (s, h) in enumerate(heads):
            j = hi % 2
            pbase = 64 * (h % 2)
            ev_q[j] = k.qdma(qsl[j], QP[s, pbase:pbase + 64, h // 2, :],
                             waits=_ev(ev_q[j], ev_hd_rd[j]))
            ev_k_[j] = k.qdma(ksl[j], KP[s, pbase:pbase + 64, h // 2, :],
                              waits=_ev(ev_k_[j]))
            ev_v_[j] = k.qdma(
                vsl[j].rearrange("p (kt d) -> p kt d", kt=16),
                VP[s, :, :, h * 64:(h + 1) * 64], waits=_ev(ev_v_[j]))
            for c in range(16):
                bs = c % 2
                q0 = c * 128
                pieces = [(c - 1, True), (c, False)] if c > 0 \
                    else [(c, True)]
                nmm = len(pieces)
                ev_s = None
                for (kt, first) in pieces:
                    u = 0 if first else 1
                    ev_s = k.mm(
                        sc_ps[bs][:, u * 128:(u + 1) * 128],
                        ksl[j][:, kt * 128:(kt + 1) * 128],
                        qsl[j][:, q0:q0 + 128],
                        start=first, stop=(u == nmm - 1),
                        waits=_ev(ev_q[j], ev_k_[j], ev_sc[bs]),
                        inc=(u == nmm - 1))
                pts = []
                for u in range(nmm):
                    pu = ptm[(c % 2) * 2 + u]
                    ee = k.act(pu, sc_ps[bs][:, u * 128:(u + 1) * 128],
                               AF.Exp, bias=-CSOFT, scale=ISQS,
                               waits=_ev(ev_s, ev_pt[(c % 2) * 2 + u]))
                    mk_ = mprev_sb if (nmm == 2 and u == 0) else mcur_sb
                    em = k.v("tensor_tensor", pu, pu, mk_, ALU.mult,
                             waits=_ev(ee,))
                    pts.append((pu, em))
                ev_sc[bs] = pts[-1][1]
                ev_lmm = None
                for u, (pu, em) in enumerate(pts):
                    ev_lmm = k.mm(l_ps[c % 2][0:1, 0:128], onescb_sb, pu,
                                  start=(u == 0), stop=(u == nmm - 1),
                                  waits=_ev(em, ev_l_rd[c % 2]),
                                  inc=(u == nmm - 1))
                ev_l_rd[c % 2] = k.v(
                    "tensor_copy", lrow[:, c * 128:(c + 1) * 128],
                    l_ps[c % 2][0:1, 0:128],
                    waits=_ev(ev_lmm, ev_lrow_rd if c < 2 else None))
                ev_av = None
                for u, (pu, em) in enumerate(pts):
                    kt = c - (nmm - 1) + u
                    ev_av = k.mm(
                        ctx_ps[c % 2], vsl[j][:, kt * 64:(kt + 1) * 64],
                        pu, start=(u == 0), stop=(u == nmm - 1),
                        waits=_ev(ev_v_[j], em, ev_ctx[c % 2]),
                        inc=(u == nmm - 1))
                ev_pt[(c % 2) * 2] = ev_av
                ev_pt[(c % 2) * 2 + 1] = ev_av
                ec = k.act(ctxu[:, q0:q0 + 128], ctx_ps[c % 2], AF.Copy,
                           waits=_ev(ev_av, ev_ctxu_rd if c == 0 else None))
                ev_ctx[c % 2] = ec
                ev_hd_rd[j] = ev_av
            ev_r = k.vrecip(invrow, lrow, rscr,
                            waits=_ev(ev_l_rd[0], ev_l_rd[1]))
            ev_lrow_rd = ev_r
            ev_rb = k.v("tensor_copy", invrow_b, invrow, waits=_ev(ev_r))
            ev_iball = None
            for qg in range(4):
                ev_bcm = k.mm(bc_ps, onesrb_sb[0:1, 0:64],
                              invrow_b[:, qg * 512:(qg + 1) * 512],
                              True, True, waits=_ev(ev_rb, ev_bc_rd),
                              inc=True)
                ev_ib = k.v("tensor_copy",
                            invbc[:, qg * 512:(qg + 1) * 512], bc_ps,
                            waits=_ev(ev_bcm, ev_invbc_rd))
                ev_bc_rd = ev_ib
                ev_iball = ev_ib
            en = k.v("tensor_tensor", ctxn, ctxu, invbc, ALU.mult,
                     waits=_ev(ev_iball, ev_ctxn_rd))
            ev_ctxu_rd = en
            ev_invbc_rd = en
            ev_ctxn_rd = k.qdma(CTX[s, pbase:pbase + 64, h // 2, :], ctxn,
                                waits=_ev(en))
        k.barrier(name)


    def emit_scan(name="scan"):
        ar.reset()
        at = [ar.take(128, T, F32) for _ in range(2)]
        xt = [ar.take(128, T, F32) for _ in range(2)]
        rt = [ar.take(128, T, F32) for _ in range(2)]
        ht = [ar.take(128, T, F32) for _ in range(2)]
        ev_a = [None, None]
        ev_x = [None, None]
        ev_r = [None, None]
        ev_h = [None, None]
        for s in range(SQ):
            for kc in range(KC):
                i = (s * KC + kc) % 2
                ev_a[i] = k.qdma(at[i], AA[s, :, kc, :], waits=_ev(ev_a[i]))
                ev_x[i] = k.qdma(xt[i], XS[s, :, kc, :], waits=_ev(ev_x[i]))
                ev_r[i] = k.qdma(rt[i], RS[s, :, kc, :], waits=_ev(ev_r[i]))
                es = k.v("tensor_tensor_scan", ht[i], at[i], xt[i], 0.0,
                         ALU.mult, ALU.add,
                         waits=_ev(ev_a[i], ev_x[i], ev_h[i]))
                ev_a[i] = es
                ev_x[i] = es
                eo = k.v("tensor_tensor", ht[i], ht[i], rt[i], ALU.add,
                         waits=_ev(ev_r[i]))
                ev_r[i] = eo
                ev_h[i] = k.qdma(RS[s, :, kc, :], ht[i], waits=_ev(eo))
        k.barrier(name)


    def emit_glu(name="glu"):
        ar.reset()
        xs = [ar.take(128, KC * T, BF16) for _ in range(2)]
        wg = [wslot[0][:, 0:KC * 512], wslot[0][:, KC * 512:KC * 1024]]
        wu = [wslot[1][:, 0:KC * 512], wslot[1][:, KC * 512:KC * 1024]]
        glu_war = _ev(wstate["free"][0], wstate["free"][1])
        evb = [ar.take(128, TG, BF16) for _ in range(4)]
        evf = [ar.take(128, TG, F32) for _ in range(2)]
        evb_ev = [None] * 4
        evf_ev = [None] * 2
        ev_xs = []
        for s in range(SQ):
            ev_xs.append(k.qdma(
                xs[s].rearrange("p (kc t) -> p kc t", kc=KC), XN[s]))
        ev_wg = [None, None]
        ev_wu = [None, None]
        ev_wgfree = [None, None]
        ev_wufree = [None, None]
        bank_free = [None] * 4
        banks = [pbig[0][:, 0:512], pbig[0][:, 512:1024],
                 pbig[1][:, 0:512], pbig[1][:, 512:1024]]
        nb = 0
        wgv = W_gate.rearrange("(kc p) o -> p kc o", p=128)
        wuv = W_up.rearrange("(kc p) o -> p kc o", p=128)
        for ff in range(DFF // 512):
            i = ff % 2
            ev_wg[i] = k.gdma(
                wg[i].rearrange("p (kc o) -> p kc o", kc=KC),
                wgv[:, :, ff * 512:(ff + 1) * 512],
                waits=_ev(ev_wgfree[i], glu_war if ff < 2 else None))
            ev_wu[i] = k.gdma(
                wu[i].rearrange("p (kc o) -> p kc o", kc=KC),
                wuv[:, :, ff * 512:(ff + 1) * 512],
                waits=_ev(ev_wufree[i], glu_war if ff < 2 else None))
            for s in range(SQ):
                for tg in range(NTG):
                    t0 = tg * TG
                    for ocq in range(4):
                        bg_ = nb % 4
                        nb += 1
                        bu_ = nb % 4
                        nb += 1
                        ev_g = ev_u = None
                        for kc in range(KC):
                            ev_g = k.mm(
                                banks[bg_],
                                wg[i][:, kc * 512 + ocq * 128:
                                      kc * 512 + (ocq + 1) * 128],
                                xs[s][:, kc * T + t0:kc * T + t0 + TG],
                                start=(kc == 0), stop=(kc == KC - 1),
                                waits=_ev(ev_wg[i], ev_xs[s],
                                          bank_free[bg_]),
                                inc=(kc == KC - 1))
                        for kc in range(KC):
                            ev_u = k.mm(
                                banks[bu_],
                                wu[i][:, kc * 512 + ocq * 128:
                                      kc * 512 + (ocq + 1) * 128],
                                xs[s][:, kc * T + t0:kc * T + t0 + TG],
                                start=(kc == 0), stop=(kc == KC - 1),
                                waits=_ev(ev_wu[i], bank_free[bu_]),
                                inc=(kc == KC - 1))
                        sb_ = (ff * 8 + tg * 4 + ocq) % 2
                        eg = k.act(evf[sb_], banks[bg_], AF.Silu,
                                   waits=_ev(ev_g, evf_ev[sb_]))
                        bank_free[bg_] = eg
                        sl = (ff * 8 + tg * 4 + ocq) % 4
                        em = k.v("tensor_tensor", evb[sl], evf[sb_],
                                 banks[bu_], ALU.mult,
                                 waits=_ev(eg, ev_u, evb_ev[sl]))
                        bank_free[bu_] = em
                        evf_ev[sb_] = em
                        evb_ev[sl] = k.qdma(
                            GU[s, :, ff * 4 + ocq, t0:t0 + TG], evb[sl],
                            waits=_ev(em))
                        ev_wgfree[i] = ev_g
                        ev_wufree[i] = ev_u
        wstate["free"][0] = ev_g
        wstate["free"][1] = ev_u
        k.barrier(name)


    emit_ln(x_ext, "ln", 1e-5, dst_bf=XN, name="ln1")

    def emit_ffn(call, rsrc, rdst):
        nsrc = {-1: XN}
        for c in range(5):
            nsrc[c] = ND[c]
        for (c, eq, ek, evv, iq, ik, iv) in PLAN:
            for e, i, roles in _node_projs(c):
                emit_gemm(nsrc[i], W_We[e * D:(e + 1) * D, :], D, D,
                          mk_evac_edge(c, e, roles),
                          name="f%dn%de%d" % (call, c, e))
            emit_gemm(EQ, W_Wq[c * D:(c + 1) * D, :], D, D,
                      mk_evac_plain(QP), name="f%dn%dwq" % (call, c))
            emit_gemm(EK, W_Wk[c * D:(c + 1) * D, :], D, D,
                      mk_evac_plain(KP), name="f%dn%dwk" % (call, c))
            emit_gemm_tm(EV, W_Wv[c * D:(c + 1) * D, :], D, D, VP,
                         name="f%dn%dwv" % (call, c))
            emit_mha(name="f%dn%dmha" % (call, c))

            def pre_rt(ctx):
                ctx["rt_load"] = QD
            emit_gemm(CTX, W_Wo[c * D:(c + 1) * D, :], D, D,
                      mk_evac_combine(c), name="f%dn%dwo" % (call, c),
                      pre_alloc=pre_rt)
        emit_ln(N5, "lnc", 1e-6, combine={"rsrc": rsrc, "rdst": rdst},
                name="f%dfin" % call)

    emit_ffn(0, x_ext, R)

    emit_ln(R, "ln", 1e-5, dst_bf=XN, dst_f32=RS, name="ln2")
    emit_ln(RS, "rms", 1e-6, dst_bf=XN, rms_g=0, name="rmsmix")
    pre_mg, evac_mg = mk_evac_mingru()
    emit_gemm(XN, W_gvd, D, 3 * D, evac_mg, name="gvd", pre_alloc=pre_mg)
    emit_scan()
    emit_ln(RS, "rms", 1e-6, dst_bf=XN, rms_g=1, name="rmsattn")
    emit_gemm(XN, W_qkv, D, 2 * D, mk_evac_plain(QP, KP, split_oc=KC),
              name="qkvfm")
    emit_gemm_tm(XN, W_qkv, D, D, VP, wcol0=2 * D, name="qkvtm")
    emit_sliding()

    def pre_rs(ctx):
        ctx["rt_load"] = RS
    emit_gemm(CTX, W_oT, D, D, mk_evac_resid(RS), name="slwo",
              pre_alloc=pre_rs)
    emit_ln(RS, "rms", 1e-6, dst_bf=XN, rms_g=2, name="rmsffn")
    emit_glu()

    def pre_rs2(ctx):
        ctx["rt_load"] = RS
        nxb = ctx["nxb"]
        ctx["rt2"] = [ctx["ar"].take(128, KC * TG, F32)
                      for _ in range(nxb)]
        ctx["rt2_ev"] = [None] * nxb
        ctx["rt2_rd"] = [None] * nxb

    def evac_wout(ctx, s, oc, tg, ps, ev_mm, i):
        t0 = tg * TG
        if oc == 0:
            ctx["rt2_ev"][i] = k.qdma(
                ctx["rt2"][i].rearrange("p (kc t) -> p kc t", kc=KC),
                R[s, :, :, t0:t0 + TG],
                waits=_ev(ctx["rt2_ev"][i], ctx["rt2_rd"][i]))
        sl = oc % 4
        rt = ctx["rt"][i][:, oc * TG:(oc + 1) * TG]
        rt2 = ctx["rt2"][i][:, oc * TG:(oc + 1) * TG]
        e1 = k.v("tensor_tensor", ctx["evf"][sl], ps, rt, ALU.add,
                 waits=_ev(ev_mm, ctx["rt_ev"][i], ctx["evf_ev"][sl]))
        ctx["rt_rd"][i] = e1
        eo = k.v("tensor_tensor", ctx["evf"][sl], ctx["evf"][sl], rt2,
                 ALU.add, waits=_ev(ctx["rt2_ev"][i]))
        ctx["rt2_rd"][i] = eo
        ctx["evf_ev"][sl] = k.qdma(R[s, :, oc, t0:t0 + TG], ctx["evf"][sl],
                                   waits=_ev(eo))
        return e1

    emit_gemm(GU, W_out, DFF, D, evac_wout, name="wout", pre_alloc=pre_rs2)

    emit_ln(R, "ln", 1e-5, dst_bf=XN, name="ln3")
    emit_ffn(1, R, out_ext)

    if DEBUG_DUMP:
        k.unstop()
        for nm in DEBUG_DUMP:
            k.qdma(dbg_outs[nm], SCRATCH[nm])
    k.unstop()
    k.barrier("final", final=True)

    with ExitStack() as es:
        k.finalize(es)
    global LAST_K
    LAST_K = k
    return nc


LAST_K = None


def _prep_inputs(inputs):
    f = np.float32
    g = {}
    x = np.asarray(inputs["L0_attn_x"], f)
    g["x_cores"] = np.ascontiguousarray(
        x.reshape(16, T, KC, 128).transpose(0, 3, 2, 1))

    def tp(a):
        return np.ascontiguousarray(np.asarray(a, f).T)

    g["We"] = np.ascontiguousarray(
        np.asarray(inputs["L0_ffn_We"], f).reshape(20 * D, D))
    g["Wqn"] = np.ascontiguousarray(
        np.asarray(inputs["L0_ffn_Wq"], f).reshape(6 * D, D))
    g["Wkn"] = np.ascontiguousarray(
        np.asarray(inputs["L0_ffn_Wk"], f).reshape(6 * D, D))
    g["Wvn"] = np.ascontiguousarray(
        np.asarray(inputs["L0_ffn_Wv"], f).reshape(6 * D, D))
    g["Won"] = np.ascontiguousarray(
        np.asarray(inputs["L0_ffn_Wo"], f).reshape(6 * D, D))
    g["Wgvd"] = np.ascontiguousarray(np.concatenate(
        [np.asarray(inputs["L0_attn_Wg"], f).T,
         np.asarray(inputs["L0_attn_Wv"], f).T,
         np.asarray(inputs["L0_attn_Wd"], f).T], axis=1))
    g["Wqkv"] = tp(inputs["L0_attn_Wqkv"])
    g["WoT"] = tp(inputs["L0_attn_Wo"])
    g["WgateT"] = tp(inputs["L0_attn_Wgate"])
    g["WupT"] = tp(inputs["L0_attn_Wup"])
    g["WoutT"] = tp(inputs["L0_attn_Wout"])

    be = np.asarray(inputs["L0_ffn_be"], f)
    g["becols"] = np.ascontiguousarray(
        be.reshape(20, KC, 128).transpose(2, 0, 1).reshape(128, 20 * KC))
    bgvd = np.stack([np.asarray(inputs["L0_attn_bg"], f),
                     np.asarray(inputs["L0_attn_bv"], f),
                     np.asarray(inputs["L0_attn_bd"], f)])
    g["bgvd"] = np.ascontiguousarray(
        bgvd.reshape(3, KC, 128).transpose(2, 0, 1).reshape(128, 3 * KC))
    g["ewt"] = np.ascontiguousarray(np.tile(
        np.asarray(inputs["L0_ffn_ew"], f).reshape(1, 18), (128, 1)))
    g["actwt"] = np.ascontiguousarray(np.tile(
        np.asarray(inputs["L0_ffn_act_w"], f).reshape(1, 6), (128, 1)))
    g["lng"] = np.ascontiguousarray(
        np.asarray(inputs["L0_ffn_ln_g"], f).reshape(KC, 128).T)
    g["lnb"] = np.ascontiguousarray(
        np.asarray(inputs["L0_ffn_ln_b"], f).reshape(KC, 128).T)
    rms = np.stack([np.asarray(inputs["L0_attn_w_rms_mix"], f),
                    np.asarray(inputs["L0_attn_w_rms_attn"], f),
                    np.asarray(inputs["L0_attn_w_rms_ffn"], f)])
    g["rmsc"] = np.ascontiguousarray(
        rms.reshape(3, KC, 128).transpose(2, 0, 1).reshape(128, 3 * KC))
    return g


SHARED_KEYS = ("We", "Wqn", "Wkn", "Wvn", "Won", "Wgvd", "Wqkv", "WoT",
               "WgateT", "WupT", "WoutT", "becols", "bgvd", "ewt", "actwt",
               "lng", "lnb", "rmsc")


def kernel(**inputs):
    g = _prep_inputs(inputs)
    nc = build_program()
    in_maps = []
    for c in range(NCORES):
        m = {n: g[n] for n in SHARED_KEYS}
        m["x"] = np.ascontiguousarray(g["x_cores"][2 * c:2 * c + 2])
        in_maps.append(m)
    res = run_bass_kernel_spmd(nc, in_maps, core_ids=list(range(NCORES)))
    outs = []
    for c in range(NCORES):
        o = res.results[c]["out"]
        outs.append(o.transpose(0, 3, 2, 1).reshape(2, T, D))
    return np.ascontiguousarray(
        np.concatenate(outs, axis=0).astype(np.float32))


# version 10
# speedup vs baseline: 1.0551x; 1.0551x over previous
import numpy as np
import ml_dtypes
from contextlib import ExitStack

import concourse.bass as bass
import concourse.mybir as mybir
import concourse.dve_ops as dve_ops
from concourse.bass_utils import run_bass_kernel_spmd

F32 = mybir.dt.float32
BF16 = mybir.dt.bfloat16
AF = mybir.ActivationFunctionType
ALU = mybir.AluOpType

NCORES = 8
SQ = 2
T = 2048
D = 1024
KC = D // 128
TG = 512
NTG = T // TG
DFF = 4096
CSOFT = 24.0
ISQD = 1.0 / np.sqrt(128.0)
ISQS = 1.0 / np.sqrt(64.0)
BF16NP = ml_dtypes.bfloat16


def _dag_plan():
    plan, lind = [], 0
    for c in range(6):
        nsrc = min(c + 1, 5)
        snode = c - nsrc
        plan.append((c, lind + nsrc - 1, lind, lind + nsrc // 2,
                     c - 1, snode, snode + nsrc // 2))
        lind += nsrc
    return plan


PLAN = _dag_plan()


def _node_projs(c):
    (_, eq, ek, ev, iq, ik, iv) = PLAN[c]
    projs, order = {}, []
    for e, i, r in ((eq, iq, 0), (ek, ik, 1), (ev, iv, 2)):
        if (e, i) not in projs:
            projs[(e, i)] = []
            order.append((e, i))
        projs[(e, i)].append(r)
    return [(e, i, projs[(e, i)]) for (e, i) in order]


EWBE_TRIPLES = []
for _c in range(6):
    for _e, _i, _roles in _node_projs(_c):
        for _r in _roles:
            EWBE_TRIPLES.append((_c, _r, _e))

PHASE_LIMIT = None
DEBUG_DUMP = []
DEBUG_LN = None
PROFILE_SCOPES = False
SIM_SAFE = False


def _ev(*evs):
    out = []
    for e in evs:
        if e is None:
            continue
        if isinstance(e, tuple) and e and isinstance(e[0], str):
            out.append(e)
        else:
            for x in e:
                if x is not None:
                    out.append(x)
    return tuple(out)


class K:
    ENGINES = ("tensor", "vector", "scalar", "sync", "gpsimd")

    def __init__(self, nc):
        self.nc = nc
        self.streams = {e: [] for e in self.ENGINES}
        self.cnt = {}
        self.lastwait = {}
        self.ndma_q = 0
        self.nphase = 0
        self.stopped = False
        self.phase_names = {}

    def _rec(self, eng, fn, waits=(), inc=None):
        if self.stopped:
            return None
        ws = []
        for w in waits:
            if w is None:
                continue
            s, c = w
            if SIM_SAFE and (s.startswith("q") or s.startswith("g")):
                c = self.cnt.get(s, 0)
            if c <= 0:
                continue
            key = (eng, s)
            if c > self.lastwait.get(key, 0):
                ws.append((s, c))
                self.lastwait[key] = c
        ev = None
        if inc is not None:
            s, amt = inc
            self.cnt[s] = self.cnt.get(s, 0) + amt
            ev = (s, self.cnt[s])
        self.streams[eng].append((ws, fn, inc, self.nphase))
        return ev

    def qdma(self, out, in_, waits=()):
        sem = "q%d" % ((self.ndma_q // 256) % 8)
        self.ndma_q += 1
        return self._rec("sync", lambda e: e.dma_start(out=out, in_=in_),
                         waits, (sem, 16))

    def gdma(self, out, in_, waits=()):
        return self._rec("gpsimd", lambda e: e.dma_start(out=out, in_=in_),
                         waits, ("g0", 16))

    def mm(self, out, lhsT, rhs, start, stop, waits=(), inc=False):
        return self._rec(
            "tensor",
            lambda e: e.matmul(out, lhsT, rhs, start=start, stop=stop),
            waits, ("t", 1) if inc else None)

    def act(self, out, in_, func, bias=0.0, scale=1.0, accum_out=None,
            waits=(), inc=True):
        return self._rec(
            "scalar",
            lambda e: e.activation(out, in_, func, bias=bias, scale=scale,
                                   accum_out=accum_out),
            waits, ("s", 1) if inc else None)

    def v(self, opname, *args, waits=(), inc=True, **kw):
        return self._rec(
            "vector", lambda e: getattr(e, opname)(*args, **kw),
            waits, ("v", 1) if inc else None)

    def vrecip(self, out, in_, scratch, waits=()):
        self.act(scratch, in_, AF.Ln, waits=waits)
        return self.act(out, scratch, AF.Exp, scale=-1.0)

    def now(self, sem):
        c = self.cnt.get(sem, 0)
        return (sem, c) if c > 0 else None

    def dma_now(self):
        return tuple(w for w in (self.now("q%d" % i) for i in range(8))
                     if w is not None)

    def barrier(self, name="", final=False):
        if self.stopped:
            return
        self.nphase += 1
        bs = {"tensor": "bt", "vector": "bv", "scalar": "bs",
              "sync": "bq", "gpsimd": "bg"}
        evs = []
        for eng in self.ENGINES:
            s = bs[eng]
            evs.append(self._rec(eng, lambda e: e.nop(), (), (s, 1)))
        allw = list(evs)
        for s in list(self.cnt):
            if not s.startswith("b") and (final or s != "g0"):
                allw.append(self.now(s))
        sync_w = [w for w in evs if w is not None]
        sync_w += [w for w in (self.now(x) for x in ("t", "v", "s"))
                   if w is not None]
        for eng in self.ENGINES:
            ws = sync_w if (eng == "sync" and not final) else allw
            for w in ws:
                if w is not None:
                    self._rec(eng, None, (w,), None)
        self.phase_names[self.nphase - 1] = name
        if PHASE_LIMIT is not None and self.nphase >= PHASE_LIMIT:
            self.stopped = True

    def unstop(self):
        self.stopped = False

    def finalize(self, es):
        nc = self.nc
        names = (["t", "v", "s", "g0", "g1"]
                 + ["q%d" % i for i in range(8)]
                 + ["bt", "bv", "bs", "bq", "bg"])
        self.semh = {n: es.enter_context(nc.semaphore(n)) for n in names}
        blk = es.enter_context(nc.Block())

        def runner(stream):
            def run(eng):
                for (ws, fn, inc, _ph) in stream:
                    for (s, c) in ws:
                        eng.wait_ge(self.semh[s], c)
                    if fn is None:
                        continue
                    ins = fn(eng)
                    if inc is not None:
                        ins.then_inc(self.semh[inc[0]], inc[1])
            return run

        if not PROFILE_SCOPES:
            blk.tensor(runner(self.streams["tensor"]))
            blk.vector(runner(self.streams["vector"]))
            blk.scalar(runner(self.streams["scalar"]))
            blk.sync(runner(self.streams["sync"]))
            blk.gpsimd(runner(self.streams["gpsimd"]))
        else:
            nph = self.nphase + 1
            byph = {e: {} for e in self.ENGINES}
            for e in self.ENGINES:
                for rec in self.streams[e]:
                    byph[e].setdefault(rec[3], []).append(rec)
            for ph in range(nph + 1):
                nm = "ph%03d_%s" % (ph, self.phase_names.get(ph, "x"))
                has = any(byph[e].get(ph) for e in self.ENGINES)
                if not has:
                    continue
                with nc.named_scope(nm):
                    for e in self.ENGINES:
                        recs = byph[e].get(ph)
                        if recs:
                            getattr(blk, e)(runner(recs))


class Arena:
    def __init__(self, ap_f32):
        self.ap = ap_f32
        self.cap = ap_f32.shape[1]
        self.off = 0

    def reset(self):
        self.off = 0

    def take(self, p, n, dtype):
        nf = n if dtype == F32 else (n + 1) // 2
        off = (self.off + 7) & ~7
        self.off = off + nf
        assert self.off <= self.cap, "arena overflow: %d > %d" % (
            self.off, self.cap)
        sl = self.ap[0:p, off:off + nf]
        return sl if dtype == F32 else sl.bitcast(BF16)


def build_program():
    nc = bass.Bass("TRN2", target_bir_lowering=False)
    k = K(nc)

    def ein(name, shape, dt=F32):
        return nc.dram_tensor(name, list(shape), dt, kind="ExternalInput").ap()

    x_ext = ein("x", (SQ, 128, KC, T))
    out_ext = nc.dram_tensor("out", [SQ, 128, KC, T], F32,
                             kind="ExternalOutput").ap()

    W_We = ein("We", (20 * D, D))
    W_Wq = ein("Wqn", (6 * D, D))
    W_Wk = ein("Wkn", (6 * D, D))
    W_Wv = ein("Wvn", (6 * D, D))
    W_Wo = ein("Won", (6 * D, D))
    W_gvd = ein("Wgvd", (D, 3 * D))
    W_qkv = ein("Wqkv", (D, 3 * D))
    W_oT = ein("WoT", (D, D))
    W_gate = ein("WgateT", (D, DFF))
    W_up = ein("WupT", (D, DFF))
    W_out = ein("WoutT", (DFF, D))

    becols = ein("becols", (128, 20 * KC))
    bgvdc = ein("bgvd", (128, 3 * KC))
    ew_t = ein("ewt", (128, 18))
    actw_t = ein("actwt", (128, 6))
    lngc = ein("lng", (128, KC))
    lnbc = ein("lnb", (128, KC))
    rmscc = ein("rmsc", (128, 3 * KC))

    ones_col_f = nc.inline_tensor(np.ones((128, 1), np.float32), "onescf")
    ones_col_b = nc.inline_tensor(np.ones((128, 1), BF16NP), "onescb")
    ones_row_f = nc.inline_tensor(np.ones((1, 128), np.float32), "onesrf")
    ones_row_b = nc.inline_tensor(np.ones((1, 128), BF16NP), "onesrb")
    kk_, jj_ = np.meshgrid(np.arange(128), np.arange(128), indexing="ij")
    mask_cur = nc.inline_tensor((kk_ <= jj_).astype(BF16NP), "mcur")
    mask_prev = nc.inline_tensor((kk_ >= jj_ + 1).astype(BF16NP), "mprev")
    CONSTVALS = [1e-5, 1e-6, -CSOFT, -0.5, float(ISQD), float(ISQS),
                 0.5, 0.998, 0.001, 1.0, 0.0]
    cvals_d = nc.inline_tensor(
        np.tile(np.asarray(CONSTVALS, np.float32)[None, :], (128, 1)),
        "cvals")

    def scr(name, shape, dt):
        return nc.dram_tensor(name, list(shape), dt, kind="Internal").ap()

    fmshape = (SQ, 128, KC, T)
    R = scr("R", fmshape, F32)
    RS = scr("RS", fmshape, F32)
    XN = scr("XN", fmshape, BF16)
    ND = scr("ND", (5,) + fmshape, BF16)
    EQ = scr("EQ", fmshape, BF16)
    EK = scr("EK", fmshape, BF16)
    EV = scr("EV", fmshape, BF16)
    QD = scr("QD", fmshape, F32)
    N5 = scr("N5", fmshape, F32)
    QP = scr("QP", fmshape, BF16)
    KP = scr("KP", fmshape, BF16)
    VP = scr("VP", (SQ, 128, T // 128, D), BF16)
    CTX = scr("CTX", fmshape, BF16)
    XS = scr("XS", fmshape, F32)
    AA = scr("AA", fmshape, F32)
    GU = scr("GU", (SQ, 128, DFF // 128, T), BF16)

    SCRATCH = {"R": R, "RS": RS, "XN": XN, "ND": ND, "EQ": EQ, "EK": EK,
               "EV": EV, "QD": QD, "N5": N5, "QP": QP, "KP": KP, "VP": VP,
               "CTX": CTX, "XS": XS, "AA": AA, "GU": GU}
    global DEBUG_LN
    if DEBUG_LN is not None:
        DEBUG_LN = nc.dram_tensor("dbg_lnrows", [5, 128, TG], F32,
                                  kind="ExternalOutput").ap()
    dbg_outs = {}
    for nm in DEBUG_DUMP:
        src = SCRATCH[nm]
        dbg_outs[nm] = nc.dram_tensor(
            "dbg_%s" % nm, list(src.shape), src.dtype,
            kind="ExternalOutput").ap()

    cst = nc.alloc_sbuf_tensor("cst", [128, 400], F32).ap()
    onescf_sb = nc.alloc_sbuf_tensor("onescf_s", [128, 1], F32).ap()
    onescb_sb = nc.alloc_sbuf_tensor("onescb_s", [128, 1], BF16).ap()
    onesrf_sb = nc.alloc_sbuf_tensor("onesrf_s", [1, 128], F32).ap()
    onesrb_sb = nc.alloc_sbuf_tensor("onesrb_s", [1, 128], BF16).ap()
    mcur_sb = nc.alloc_sbuf_tensor("mcur_s", [128, 128], BF16).ap()
    mprev_sb = nc.alloc_sbuf_tensor("mprev_s", [128, 128], BF16).ap()
    rows = nc.alloc_sbuf_tensor("rows", [1, 3 * T], F32).ap()
    rowsb = nc.alloc_sbuf_tensor("rowsb", [1, 2 * T], BF16).ap()
    cvals_sb = nc.alloc_sbuf_tensor("cvals_s", [128, len(CONSTVALS)],
                                    F32).ap()
    for ci, cv in enumerate(CONSTVALS):
        nc.const_aps.register(F32, cv, cvals_sb[:, ci:ci + 1])

    WPOOL = nc.alloc_sbuf_tensor("wpool", [128, 49152], BF16).ap()
    wslot = [WPOOL[:, 0:24576], WPOOL[:, 24576:49152]]
    wstate = {"i": 0, "free": [None, None]}
    rem = nc.sbuf_bytes_remaining
    arena_words = (rem - 2048) // 4
    AR = nc.alloc_sbuf_tensor("arena", [128, arena_words], F32).ap()
    ar = Arena(AR)

    pbig = [nc.alloc_psum_tensor("pA", [128, 1024], F32).ap(),
            nc.alloc_psum_tensor("pB", [128, 1024], F32).ap()]
    pb = [nc.alloc_psum_tensor("p%d" % i, [128, 512], F32).ap()
          for i in range(4)]

    c_be, c_bgvd, c_ew, c_actw = 0, 160, 184, 202
    c_lng, c_lnb, c_rms, c_ewbe = 208, 216, 224, 248

    evs = [
        k.qdma(cst[:, c_be:c_be + 160], becols),
        k.qdma(cst[:, c_bgvd:c_bgvd + 24], bgvdc),
        k.qdma(cst[:, c_ew:c_ew + 18], ew_t),
        k.qdma(cst[:, c_actw:c_actw + 6], actw_t),
        k.qdma(cst[:, c_lng:c_lng + 8], lngc),
        k.qdma(cst[:, c_lnb:c_lnb + 8], lnbc),
        k.qdma(cst[:, c_rms:c_rms + 24], rmscc),
        k.qdma(onescf_sb, ones_col_f.ap()),
        k.qdma(onescb_sb, ones_col_b.ap()),
        k.qdma(onesrf_sb, ones_row_f.ap()),
        k.qdma(onesrb_sb, ones_row_b.ap()),
        k.qdma(mcur_sb, mask_cur.ap()),
        k.qdma(mprev_sb, mask_prev.ap()),
        k.qdma(cvals_sb, cvals_d.ap()),
    ]
    for idx, (c, r, e) in enumerate(EWBE_TRIPLES):
        k.v("tensor_scalar",
            cst[:, c_ewbe + idx * 8:c_ewbe + (idx + 1) * 8],
            cst[:, c_be + e * 8:c_be + (e + 1) * 8],
            cst[:, c_ew + c * 3 + r:c_ew + c * 3 + r + 1], None,
            ALU.mult, waits=_ev(*evs))
    k.barrier("prep")

    def ew_col(c, r):
        return cst[:, c_ew + c * 3 + r:c_ew + c * 3 + r + 1]

    def ewbe_col(c, r, e, kc):
        i = EWBE_TRIPLES.index((c, r, e))
        return cst[:, c_ewbe + i * 8 + kc:c_ewbe + i * 8 + kc + 1]

    def becol(e, kc):
        return cst[:, c_be + e * 8 + kc:c_be + e * 8 + kc + 1]

    def bgvd_col(g, kc):
        return cst[:, c_bgvd + g * 8 + kc:c_bgvd + g * 8 + kc + 1]

    def actw_col(c):
        return cst[:, c_actw + c:c_actw + c + 1]

    def rms_col(g, kc):
        return cst[:, c_rms + g * 8 + kc:c_rms + g * 8 + kc + 1]

    def lng_col(kc):
        return cst[:, c_lng + kc:c_lng + kc + 1]

    def lnb_col(kc):
        return cst[:, c_lnb + kc:c_lnb + kc + 1]


    def emit_ln(src, kind, eps, dst_bf=None, dst_f32=None, rms_g=None,
                combine=None, name="ln"):
        ar.reset()
        xb = [ar.take(128, KC * TG, F32) for _ in range(2)]
        xsq = ar.take(128, KC * TG, BF16)
        tmp = ar.take(128, TG, F32)
        tmp2 = ar.take(128, TG, F32)
        ob = [ar.take(128, TG, F32) for _ in range(4)]
        ob2 = [ar.take(128, TG, BF16) for _ in range(4)]
        rt = [ar.take(128, KC * TG, F32)]
        mrow = rows[:, 0:TG]
        vrow = rows[:, TG:2 * TG]
        m2row = rows[:, 2 * TG:3 * TG]
        lnrow = rows[:, 3 * TG:4 * TG]
        rstdrow = rows[:, 4 * TG:5 * TG]
        sum_ps, sq_ps, mbc, rbc = pb[0], pb[1], pb[2], pb[3]
        ev_x = [None, None]
        ev_xrd = [None, None]
        ev_rt = [None, None]
        ev_rt_rd = [None, None]
        ev_sum_rd = ev_sq_rd = None
        ev_xsq_rd = None
        ev_mbc_rd = ev_rbc_rd = None
        ev_rows_t = None
        ev_ob = [None] * 4
        ev_ob2 = [None] * 4
        tiles_ln = [(s, tg) for s in range(SQ) for tg in range(NTG)]
        ev_xl = [None] * 8

        def issue_x(idx):
            s, tg = tiles_ln[idx]
            i = idx % 2
            ev_xl[idx] = k.qdma(
                xb[i].rearrange("p (kc t) -> p kc t", kc=KC),
                src[s, :, :, tg * TG:(tg + 1) * TG],
                waits=_ev(ev_x[i], ev_xrd[i]))

        issue_x(0)
        for idx in range(len(tiles_ln)):
            s, tg = tiles_ln[idx]
            if True:
                i = idx % 2
                ri = 0
                t0 = tg * TG
                if idx + 1 < len(tiles_ln):
                    issue_x(idx + 1)
                ev_x[i] = ev_xl[idx]
                if combine is not None:
                    ev_rt[ri] = k.qdma(
                        rt[ri].rearrange("p (kc t) -> p kc t", kc=KC),
                        combine["rsrc"][s, :, :, t0:t0 + TG],
                        waits=_ev(ev_rt[ri], ev_rt_rd[ri]))
                ev_sqa = []
                for c in range(KC):
                    ev_sqa.append(k.act(
                        xsq[:, c * TG:(c + 1) * TG],
                        xb[i][:, c * TG:(c + 1) * TG], AF.Square,
                        waits=_ev(ev_x[i], ev_xsq_rd if c == 0 else None)))
                ev_sum = None
                if kind != "rms":
                    for c in range(KC):
                        ev_sum = k.mm(
                            sum_ps[0:1, :], onescf_sb,
                            xb[i][:, c * TG:(c + 1) * TG],
                            start=(c == 0), stop=(c == KC - 1),
                            waits=_ev(ev_x[i], ev_sum_rd),
                            inc=(c == KC - 1))
                ev_sq = None
                for c in range(KC):
                    ev_sq = k.mm(
                        sq_ps[0:1, :], onescb_sb,
                        xsq[:, c * TG:(c + 1) * TG],
                        start=(c == 0), stop=(c == KC - 1),
                        waits=_ev(ev_sqa[c], ev_sq_rd), inc=(c == KC - 1))
                ev_xsq_rd = ev_sq
                if kind != "rms":
                    ev_m = k.v("tensor_scalar", mrow, sum_ps[0:1, :],
                               1.0 / D, None, ALU.mult,
                               waits=_ev(ev_sum, ev_rows_t))
                    ev_sum_rd = ev_m
                    ev_m2 = k.v("tensor_mul", m2row, mrow, mrow)
                    ev_v0 = k.v("tensor_scalar", vrow, sq_ps[0:1, :],
                                1.0 / D, None, ALU.mult, waits=_ev(ev_sq))
                    ev_sq_rd = ev_v0
                    ev_var = k.v("tensor_sub", vrow, vrow, m2row,
                                 waits=_ev(ev_m2))
                else:
                    ev_var = k.v("tensor_scalar", vrow, sq_ps[0:1, :],
                                 1.0 / D, None, ALU.mult,
                                 waits=_ev(ev_sq, ev_rows_t))
                    ev_sq_rd = ev_var
                    ev_m = None
                ev_ln = k.act(lnrow, vrow, AF.Ln, bias=eps, waits=_ev(ev_var))
                ev_rstd = k.act(rstdrow, lnrow, AF.Exp, scale=-0.5)
                ev_mbc = None
                if kind != "rms":
                    ev_mbc = k.mm(mbc, onesrf_sb, mrow, True, True,
                                  waits=_ev(ev_m, ev_mbc_rd), inc=True)
                ev_rbc = k.mm(rbc, onesrf_sb, rstdrow, True, True,
                              waits=_ev(ev_rstd, ev_rbc_rd), inc=True)
                ev_rows_t = (ev_mbc, ev_rbc)
                if DEBUG_LN is not None and s == 0 and tg == 0:
                    dr = DEBUG_LN
                    k.qdma(dr[0, 0:1, 0:TG], mrow, waits=_ev(ev_m))
                    k.qdma(dr[1, 0:1, 0:TG], vrow, waits=_ev(ev_var))
                    k.qdma(dr[2, 0:1, 0:TG], rstdrow, waits=_ev(ev_rstd))
                    ebc = k.v("tensor_copy", ob[3], rbc, waits=_ev(ev_rbc))
                    k.qdma(dr[3, :, 0:TG], ob[3], waits=_ev(ebc))
                    ebc2 = k.v("tensor_copy", ob[2], mbc, waits=_ev(ev_mbc))
                    k.qdma(dr[4, :, 0:TG], ob[2], waits=_ev(ebc2))
                    ev_ob[3] = k.now("q%d" % (((k.ndma_q - 1) // 256) % 8))
                    ev_ob[2] = ev_ob[3]
                last_apply = None
                for c in range(KC):
                    j = c % 4
                    xc = xb[i][:, c * TG:(c + 1) * TG]
                    if kind == "rms":
                        e1 = k.v("tensor_tensor", tmp, xc, rbc, ALU.mult,
                                 waits=_ev(ev_rbc))
                        eo = k.v("tensor_scalar", ob2[j], tmp,
                                 rms_col(rms_g, c), None, ALU.mult,
                                 waits=_ev(ev_ob2[j]))
                        ev_rbc_rd = e1
                        ev_ob2[j] = k.qdma(dst_bf[s, :, c, t0:t0 + TG],
                                           ob2[j], waits=_ev(eo))
                        last_apply = e1
                    elif kind == "ln":
                        e1 = k.v("tensor_tensor", tmp, xc, mbc, ALU.subtract,
                                 waits=_ev(ev_mbc))
                        ev_mbc_rd = e1
                        if dst_f32 is not None:
                            eo = k.v("tensor_tensor", ob[j], tmp, rbc,
                                     ALU.mult, waits=_ev(ev_rbc, ev_ob[j]))
                            ev_rbc_rd = eo
                            ev_f = k.qdma(dst_f32[s, :, c, t0:t0 + TG],
                                          ob[j], waits=_ev(eo))
                            ec = k.act(ob2[j], ob[j], AF.Copy,
                                       waits=_ev(eo, ev_ob2[j]))
                            ev_ob[j] = (ev_f, ec)
                            ev_ob2[j] = k.qdma(dst_bf[s, :, c, t0:t0 + TG],
                                               ob2[j], waits=_ev(ec))
                            last_apply = e1
                        else:
                            eo = k.v("tensor_tensor", ob2[j], tmp, rbc,
                                     ALU.mult, waits=_ev(ev_rbc, ev_ob2[j]))
                            ev_rbc_rd = eo
                            if (DEBUG_LN is not None and s == 0 and tg == 0
                                    and c == 0):
                                k.qdma(DEBUG_LN[1, :, 0:TG], tmp,
                                       waits=_ev(e1, eo))
                            ev_ob2[j] = k.qdma(dst_bf[s, :, c, t0:t0 + TG],
                                               ob2[j], waits=_ev(eo))
                    else:
                        e1 = k.v("tensor_tensor", tmp, xc, mbc, ALU.subtract,
                                 waits=_ev(ev_mbc))
                        ev_mbc_rd = e1
                        e2 = k.v("tensor_tensor", tmp2, tmp, rbc, ALU.mult,
                                 waits=_ev(ev_rbc))
                        ev_rbc_rd = e2
                        e3 = k.v("tensor_scalar", tmp, tmp2, lng_col(c),
                                 lnb_col(c), ALU.mult, ALU.add)
                        eo = k.v("scalar_tensor_tensor", ob[j], tmp, 0.5,
                                 rt[ri][:, c * TG:(c + 1) * TG],
                                 ALU.mult, ALU.add,
                                 waits=_ev(ev_rt[ri], ev_ob[j]))
                        ev_rt_rd[ri] = eo
                        ev_ob[j] = k.qdma(
                            combine["rdst"][s, :, c, t0:t0 + TG], ob[j],
                            waits=_ev(eo))
                        last_apply = e1
                ev_xrd[i] = last_apply
        k.barrier(name)

    def load_weight(wsb, Wd, nkc, O, col0=0, war=None):
        evl = None
        for kc in range(nkc):
            evl = k.gdma(wsb[:, kc * O:(kc + 1) * O],
                         Wd[kc * 128:(kc + 1) * 128, col0:col0 + O],
                         waits=_ev(war) if kc == 0 else ())
        return evl

    def emit_gemm(src, Wd, Kdim, O, evac, wcol0=0, name="gemm",
                  pre_alloc=None):
        ar.reset()
        nkc, noc = Kdim // 128, O // 128
        nxb = 2 if nkc <= 16 else 1
        if nkc * O <= 24576:
            si = wstate["i"] % 2
            wstate["i"] += 1
            wsb = wslot[si][:, 0:nkc * O]
            wwar = wstate["free"][si]
            wowner = [si]
        else:
            wsb = WPOOL[:, 0:nkc * O]
            wwar = _ev(wstate["free"][0], wstate["free"][1])
            wowner = [0, 1]
        xb = [ar.take(128, nkc * TG, BF16) for _ in range(nxb)]
        ctx = {"evf": [ar.take(128, TG, F32) for _ in range(4)],
               "evb": [ar.take(128, TG, BF16) for _ in range(4)],
               "evf_ev": [None] * 4, "evb_ev": [None] * 4,
               "rt_ev": [None] * nxb, "rt_rd": [None] * nxb, "ar": ar,
               "nxb": nxb}
        if pre_alloc:
            pre_alloc(ctx)
        if "rt_load" in ctx:
            ctx["rt"] = [ar.take(128, KC * TG, F32) for _ in range(nxb)]
        ev_w = load_weight(wsb, Wd, nkc, O, wcol0, wwar)
        banks = [pbig[0][:, 0:512], pbig[0][:, 512:1024],
                 pbig[1][:, 0:512], pbig[1][:, 512:1024]]
        bank_free = [None] * 4
        ev_xfree = [None] * nxb
        ev_xl = [None] * 8
        nb = 0
        tiles = [(s, tg) for s in range(SQ) for tg in range(NTG)]

        def issue_loads(idx):
            s, tg = tiles[idx]
            i = idx % nxb
            t0 = tg * TG
            ev_xl[idx] = k.qdma(
                xb[i].rearrange("p (kc t) -> p kc t", kc=nkc),
                src[s, :, 0:nkc, t0:t0 + TG], waits=_ev(ev_xfree[i]))
            if "rt_load" in ctx:
                ctx["rt_ev"][i] = k.qdma(
                    ctx["rt"][i].rearrange("p (kc t) -> p kc t", kc=KC),
                    ctx["rt_load"][s, :, :, t0:t0 + TG],
                    waits=_ev(ctx["rt_ev"][i], ctx["rt_rd"][i]))

        issue_loads(0)
        for idx in range(len(tiles)):
            s, tg = tiles[idx]
            i = idx % nxb
            t0 = tg * TG
            if nxb > 1 and idx + 1 < len(tiles):
                issue_loads(idx + 1)
            ev_x = ev_xl[idx]
            if True:
                ev_last = None
                for oc in range(noc):
                    b = nb % 4
                    nb += 1
                    ev_mm = None
                    for kc in range(nkc):
                        ev_mm = k.mm(
                            banks[b],
                            wsb[:, kc * O + oc * 128:kc * O + (oc + 1) * 128],
                            xb[i][:, kc * TG:(kc + 1) * TG],
                            start=(kc == 0), stop=(kc == nkc - 1),
                            waits=_ev(ev_w, ev_x, bank_free[b]),
                            inc=(kc == nkc - 1))
                    bank_free[b] = evac(ctx, s, oc, tg, banks[b], ev_mm, i)
                    ev_last = ev_mm
                ev_xfree[i] = ev_last
                if nxb == 1 and idx + 1 < len(tiles):
                    issue_loads(idx + 1)
        for si in wowner:
            wstate["free"][si] = ev_last
        k.barrier(name)

    def emit_gemm_tm(src, Wd, Kdim, O, dst, dcol0=0, wcol0=0, name="gemmtm"):
        ar.reset()
        nkc, noh = Kdim // 128, O // 512
        si = wstate["i"] % 2
        wstate["i"] += 1
        wsb = wslot[si][:, 0:nkc * O]
        xb = [ar.take(128, nkc * TG, BF16) for _ in range(2)]
        evb = [ar.take(128, TG, BF16) for _ in range(4)]
        evb_ev = [None] * 4
        ev_w = load_weight(wsb, Wd, nkc, O, wcol0, wstate["free"][si])
        banks = [pbig[0][:, 0:512], pbig[0][:, 512:1024],
                 pbig[1][:, 0:512], pbig[1][:, 512:1024]]
        bank_free = [None] * 4
        ev_xfree = [None, None]
        ev_xl = [None] * 8
        nb = 0
        tiles = [(s, tg) for s in range(SQ) for tg in range(NTG)]

        def issue_loads(idx):
            s, tg = tiles[idx]
            i = idx % 2
            ev_xl[idx] = k.qdma(
                xb[i].rearrange("p (kc t) -> p kc t", kc=nkc),
                src[s, :, 0:nkc, tg * TG:(tg + 1) * TG],
                waits=_ev(ev_xfree[i]))

        issue_loads(0)
        for idx in range(len(tiles)):
            s, tg = tiles[idx]
            i = idx % 2
            t0 = tg * TG
            if idx + 1 < len(tiles):
                issue_loads(idx + 1)
            ev_x = ev_xl[idx]
            if True:
                ev_last = None
                for j in range(TG // 128):
                    for oh in range(noh):
                        b = nb % 4
                        nb += 1
                        ev_mm = None
                        for kc in range(nkc):
                            ev_mm = k.mm(
                                banks[b],
                                xb[i][:, kc * TG + j * 128:
                                      kc * TG + (j + 1) * 128],
                                wsb[:, kc * O + oh * 512:
                                    kc * O + (oh + 1) * 512],
                                start=(kc == 0), stop=(kc == nkc - 1),
                                waits=_ev(ev_w, ev_x, bank_free[b]),
                                inc=(kc == nkc - 1))
                        sl = b
                        ec = k.act(evb[sl], banks[b], AF.Copy,
                                   waits=_ev(ev_mm, evb_ev[sl]))
                        bank_free[b] = ec
                        evb_ev[sl] = k.qdma(
                            dst[s, :, tg * (TG // 128) + j,
                                dcol0 + oh * 512:dcol0 + (oh + 1) * 512],
                            evb[sl], waits=_ev(ec))
                        ev_last = ev_mm
                ev_xfree[i] = ev_last
        wstate["free"][si] = ev_last
        k.barrier(name)


    def mk_evac_plain(dst, dst2=None, split_oc=None):
        def evac(ctx, s, oc, tg, ps, ev_mm, i):
            sl = oc % 4
            d, o2 = (dst, oc) if (split_oc is None or oc < split_oc) \
                else (dst2, oc - split_oc)
            ec = k.act(ctx["evb"][sl], ps, AF.Copy,
                       waits=_ev(ev_mm, ctx["evb_ev"][sl]))
            ctx["evb_ev"][sl] = k.qdma(
                d[s, :, o2, tg * TG:(tg + 1) * TG], ctx["evb"][sl],
                waits=_ev(ec))
            return ec
        return evac

    def mk_evac_edge(c, e, roles):
        multi = len(roles) > 1
        def evac(ctx, s, oc, tg, ps, ev_mm, i):
            t0 = tg * TG
            if multi:
                sf = oc % 2
                e_base = k.v("tensor_scalar", ctx["evf"][sf], ps,
                             becol(e, oc), None, ALU.add,
                             waits=_ev(ev_mm, ctx["evf_ev"][sf]))
                base = ctx["evf"][sf]
                readers = [e_base]
                for r in roles:
                    dstd = (EQ, EK, EV)[r]
                    sl = (oc * 2 + (0 if r == 0 else 1)) % 4
                    if r == 0:
                        sf2 = 2 + oc % 2
                        eq_ = k.v("tensor_scalar", ctx["evf"][sf2], base,
                                  ew_col(c, 0), None, ALU.mult,
                                  waits=_ev(ctx["evf_ev"][sf2]))
                        eqd = k.qdma(QD[s, :, oc, t0:t0 + TG],
                                     ctx["evf"][sf2], waits=_ev(eq_))
                        ecp = k.act(ctx["evb"][sl], ctx["evf"][sf2], AF.Copy,
                                    waits=_ev(eq_, ctx["evb_ev"][sl]))
                        ctx["evf_ev"][sf2] = (eqd, ecp)
                        ctx["evb_ev"][sl] = k.qdma(
                            dstd[s, :, oc, t0:t0 + TG], ctx["evb"][sl],
                            waits=_ev(ecp))
                        readers.append(eq_)
                    else:
                        ecp = k.act(ctx["evb"][sl], base, AF.Identity,
                                    scale=ew_col(c, r),
                                    waits=_ev(e_base, ctx["evb_ev"][sl]))
                        ctx["evb_ev"][sl] = k.qdma(
                            dstd[s, :, oc, t0:t0 + TG], ctx["evb"][sl],
                            waits=_ev(ecp))
                        readers.append(ecp)
                ctx["evf_ev"][sf] = tuple(readers)
                return e_base
            r = roles[0]
            if r == 0:
                sf = oc % 4
                sl = oc % 4
                eq_ = k.v("tensor_scalar", ctx["evf"][sf], ps,
                          becol(e, oc), ew_col(c, 0), ALU.add, ALU.mult,
                          waits=_ev(ev_mm, ctx["evf_ev"][sf]))
                eqd = k.qdma(QD[s, :, oc, t0:t0 + TG], ctx["evf"][sf],
                             waits=_ev(eq_))
                ecp = k.act(ctx["evb"][sl], ctx["evf"][sf], AF.Copy,
                            waits=_ev(eq_, ctx["evb_ev"][sl]))
                ctx["evf_ev"][sf] = (eqd, ecp)
                ctx["evb_ev"][sl] = k.qdma(EQ[s, :, oc, t0:t0 + TG],
                                           ctx["evb"][sl], waits=_ev(ecp))
                return eq_
            dstd = (EQ, EK, EV)[r]
            sl = oc % 4
            ecp = k.act(ctx["evb"][sl], ps, AF.Identity, scale=ew_col(c, r),
                        bias=ewbe_col(c, r, e, oc),
                        waits=_ev(ev_mm, ctx["evb_ev"][sl]))
            ctx["evb_ev"][sl] = k.qdma(dstd[s, :, oc, t0:t0 + TG],
                                       ctx["evb"][sl], waits=_ev(ecp))
            return ecp
        return evac

    def mk_evac_combine(c):
        def evac(ctx, s, oc, tg, ps, ev_mm, i):
            t0 = tg * TG
            rt = ctx["rt"][i][:, oc * TG:(oc + 1) * TG]
            sl = oc % 4
            buf = ctx["evb"] if c < 5 else ctx["evf"]
            bev = "evb_ev" if c < 5 else "evf_ev"
            eo = k.v("scalar_tensor_tensor", buf[sl], ps, actw_col(c), rt,
                     ALU.mult, ALU.add,
                     waits=_ev(ev_mm, ctx["rt_ev"][i], ctx[bev][sl]))
            ctx["rt_rd"][i] = eo
            dstd = ND[c] if c < 5 else N5
            ctx[bev][sl] = k.qdma(dstd[s, :, oc, t0:t0 + TG], buf[sl],
                                  waits=_ev(eo))
            return eo
        return evac

    def mk_evac_resid(rdst):
        def evac(ctx, s, oc, tg, ps, ev_mm, i):
            t0 = tg * TG
            sl = oc % 4
            rt = ctx["rt"][i][:, oc * TG:(oc + 1) * TG]
            eo = k.v("tensor_tensor", ctx["evf"][sl], ps, rt, ALU.add,
                     waits=_ev(ev_mm, ctx["rt_ev"][i], ctx["evf_ev"][sl]))
            ctx["rt_rd"][i] = eo
            ctx["evf_ev"][sl] = k.qdma(rdst[s, :, oc, t0:t0 + TG],
                                       ctx["evf"][sl], waits=_ev(eo))
            return eo
        return evac

    def mk_evac_mingru():
        def pre(ctx):
            ctx["sg"] = ctx["ar"].take(128, KC * TG, F32)
            ctx["sg_w"] = [None] * KC
            ctx["sg_rd"] = [None] * KC
        def evac(ctx, s, oc, tg, ps, ev_mm, i):
            t0 = tg * TG
            grp, cc = oc // KC, oc % KC
            if grp == 0:
                eo = k.act(ctx["sg"][:, cc * TG:(cc + 1) * TG], ps,
                           AF.Sigmoid, bias=bgvd_col(0, cc),
                           waits=_ev(ev_mm, ctx["sg_rd"][cc]))
                ctx["sg_w"][cc] = eo
                return eo
            if grp == 1:
                sf = cc % 2
                et = k.act(ctx["evf"][sf], ps, AF.Tanh, bias=bgvd_col(1, cc),
                           waits=_ev(ev_mm, ctx["evf_ev"][sf]))
                ex = k.v("tensor_tensor", ctx["evf"][2 + sf],
                         ctx["sg"][:, cc * TG:(cc + 1) * TG], ctx["evf"][sf],
                         ALU.mult,
                         waits=_ev(et, ctx["sg_w"][cc],
                                   ctx["evf_ev"][2 + sf]))
                ctx["sg_rd"][cc] = ex
                ctx["evf_ev"][sf] = ex
                ctx["evf_ev"][2 + sf] = k.qdma(
                    XS[s, :, cc, t0:t0 + TG], ctx["evf"][2 + sf],
                    waits=_ev(ex))
                return et
            sf = cc % 2
            et = k.act(ctx["evf"][sf], ps, AF.Sigmoid, bias=bgvd_col(2, cc),
                       waits=_ev(ev_mm, ctx["evf_ev"][sf]))
            ea = k.v("tensor_scalar", ctx["evf"][2 + sf], ctx["evf"][sf],
                     0.998, 0.001, ALU.mult, ALU.add,
                     waits=_ev(et, ctx["evf_ev"][2 + sf]))
            ctx["evf_ev"][sf] = ea
            ctx["evf_ev"][2 + sf] = k.qdma(
                AA[s, :, cc, t0:t0 + TG], ctx["evf"][2 + sf],
                waits=_ev(ea))
            return et
        return pre, evac


    def emit_mha(name="mha"):
        ar.reset()
        qsl = [ar.take(128, T, BF16) for _ in range(2)]
        ksl = [ar.take(128, T, BF16) for _ in range(2)]
        vsl = [ar.take(128, T, BF16) for _ in range(2)]
        ptb = [ar.take(128, 16 * TG, BF16) for _ in range(2)]
        invbc = ar.take(128, TG, BF16)
        ctxn = [ar.take(128, TG, BF16) for _ in range(2)]
        lrow = rows[:, 0:TG]
        invrow = rows[:, TG:2 * TG]
        rscr = rows[:, 2 * TG:3 * TG]
        invrow_b = [rowsb[:, 0:TG], rowsb[:, TG:2 * TG]]
        sc_ps = [pbig[0], pbig[1]]
        ctx_ps = [pb[0], pb[1]]
        lb_ps = [pb[2], pb[3]]
        heads = [(s, h) for s in range(SQ) for h in range(KC)]
        nh = len(heads)
        nu = nh * 4
        ld_q = [None] * nh
        ld_k = [None] * nh
        ld_v = [None] * nh
        exp_ev = {}
        av_ev = [None] * nu
        l_ev = [None] * nu
        lc_ev = [None] * nu
        cast_ev = [None] * nu
        bcm_ev = [None] * nu
        ib_ev = [None] * nu
        ptb_war = [None, None]
        ctx_war = [None, None]
        ctxn_war = [None, None]
        st = {"lrow_war": None}

        def do_load(hi, war):
            sH, hH = heads[hi]
            jj = hi % 2
            ld_q[hi] = k.qdma(qsl[jj], QP[sH, :, hH, :],
                              waits=_ev(ld_q[hi - 2] if hi >= 2 else None,
                                        war))
            ld_k[hi] = k.qdma(ksl[jj], KP[sH, :, hH, :])
            ld_v[hi] = k.qdma(
                vsl[jj].rearrange("p (kt d) -> p kt d", kt=16),
                VP[sH, :, :, hH * 128:(hH + 1) * 128])

        def fin1(v):
            lc_ev[v] = k.v("tensor_copy", lrow, lb_ps[v % 2][0:1, :],
                           waits=_ev(l_ev[v], st["lrow_war"]))
            ev_r = k.vrecip(invrow, lrow, rscr,
                            waits=_ev(lc_ev[v],
                                      cast_ev[v - 1] if v >= 1 else None))
            st["lrow_war"] = ev_r
            cast_ev[v] = k.v("tensor_copy", invrow_b[v % 2], invrow,
                             waits=_ev(ev_r,
                                       bcm_ev[v - 2] if v >= 2 else None))

        def fin2(v):
            hv = v // 4
            sH, hH = heads[hv]
            q0 = (v % 4) * TG
            sl = v % 2
            bcm_ev[v] = k.mm(lb_ps[v % 2], onesrb_sb, invrow_b[v % 2],
                             True, True,
                             waits=_ev(cast_ev[v], lc_ev[v]), inc=True)
            ib_ev[v] = k.v("tensor_copy", invbc, lb_ps[v % 2],
                           waits=_ev(bcm_ev[v]))
            en = k.v("tensor_tensor", ctxn[sl], ctx_ps[v % 2], invbc,
                     ALU.mult, waits=_ev(av_ev[v], ctxn_war[sl]))
            ctx_war[v % 2] = en
            ctxn_war[sl] = k.qdma(CTX[sH, :, hH, q0:q0 + TG], ctxn[sl],
                                  waits=_ev(en))

        def emit_avl_pair(v, kt2):
            hv = v // 4
            jv = hv % 2
            ptv = ptb[v % 2]
            for uu in range(2):
                kt = kt2 * 2 + uu
                av_ev[v] = k.mm(
                    ctx_ps[v % 2], vsl[jv][:, kt * 128:(kt + 1) * 128],
                    ptv[:, kt * TG:(kt + 1) * TG],
                    start=(kt == 0), stop=(kt == 15),
                    waits=_ev(ld_v[hv], exp_ev[(v, kt2)],
                              ctx_war[v % 2] if kt == 0 else None),
                    inc=(kt == 15))
            for uu in range(2):
                kt = kt2 * 2 + uu
                l_ev[v] = k.mm(
                    lb_ps[v % 2][0:1, :], onescb_sb,
                    ptv[:, kt * TG:(kt + 1) * TG],
                    start=(kt == 0), stop=(kt == 15),
                    waits=_ev(ib_ev[v - 2] if (kt == 0 and v >= 2) else None,
                              lc_ev[v - 2] if (kt == 0 and v >= 2) else None),
                    inc=(kt == 15))
            if kt2 == 7:
                ptb_war[v % 2] = av_ev[v]

        do_load(0, None)
        for u in range(nu):
            hi, qg = divmod(u, 4)
            j = hi % 2
            pt = ptb[u % 2]
            q0 = qg * TG
            if qg == 1 and hi + 1 < nh:
                do_load(hi + 1, av_ev[hi * 4 - 1] if hi >= 1 else None)
            for kt2 in range(8):
                sp = sc_ps[kt2 % 2]
                bw = (exp_ev.get((u, kt2 - 2)) if kt2 >= 2
                      else exp_ev.get((u - 1, 6 + kt2)))
                ev_s = None
                for uu in range(2):
                    kt = kt2 * 2 + uu
                    ev_s = k.mm(
                        sp[:, uu * 512:(uu + 1) * 512],
                        ksl[j][:, kt * 128:(kt + 1) * 128],
                        qsl[j][:, q0:q0 + TG], start=True, stop=True,
                        waits=_ev(ld_q[hi], ld_k[hi], bw), inc=(uu == 1))
                ee = k.act(pt[:, kt2 * 1024:(kt2 + 1) * 1024], sp, AF.Exp,
                           bias=-CSOFT, scale=ISQD,
                           waits=_ev(ev_s,
                                     ptb_war[u % 2] if kt2 == 0 else None))
                exp_ev[(u, kt2)] = ee
                if u > 0:
                    emit_avl_pair(u - 1, kt2)
            if u >= 2:
                fin2(u - 2)
            if u > 0:
                fin1(u - 1)
        for kt2 in range(8):
            emit_avl_pair(nu - 1, kt2)
        fin2(nu - 2)
        fin1(nu - 1)
        fin2(nu - 1)
        k.barrier(name)


    def emit_sliding(name="slide"):
        ar.reset()
        qsl = [ar.take(64, T, BF16) for _ in range(2)]
        ksl = [ar.take(64, T, BF16) for _ in range(2)]
        vsl = [ar.take(128, 16 * 64, BF16) for _ in range(2)]
        ptp = [ar.take(128, 256, BF16) for _ in range(2)]
        ctxu = [ar.take(64, T, BF16) for _ in range(2)]
        invbc = ar.take(64, T, BF16)
        ctxn = ar.take(64, T, BF16)
        maskcat = ar.take(128, 256, BF16)
        lrow = [rows[:, 0:T], rows[:, T:2 * T]]
        invrow = rows[:, 2 * T:3 * T]
        invrow_b = [rowsb[:, 0:T], rowsb[:, T:2 * T]]
        sc_ps = [pb[0][:, 0:256], pb[1][:, 0:256]]
        ctx_ps = [pb[2][0:64, 0:128], pb[3][0:64, 0:128]]
        lb_ps = [pbig[0][:, 0:512], pbig[0][:, 512:1024]]
        bc_ps = pbig[1][0:64, 0:512]
        ev_mc = k.qdma(maskcat[:, 0:128], mask_prev.ap())
        ev_mc2 = k.qdma(maskcat[:, 128:256], mask_cur.ap())
        heads = [(s, h) for s in range(SQ) for h in range(16)]
        nh2 = len(heads)
        ld_q = [None] * nh2
        ld_k = [None] * nh2
        ld_v = [None] * nh2
        d0 = k.dma_now()

        def do_load(hi, war):
            sH, hH = heads[hi]
            jj = hi % 2
            pbase = 64 * (hH % 2)
            ld_q[hi] = k.qdma(qsl[jj], QP[sH, pbase:pbase + 64, hH // 2, :],
                              waits=_ev(ld_q[hi - 2] if hi >= 2 else None,
                                        war))
            ld_k[hi] = k.qdma(ksl[jj], KP[sH, pbase:pbase + 64, hH // 2, :])
            ld_v[hi] = k.qdma(
                vsl[jj].rearrange("p (kt d) -> p kt d", kt=16),
                VP[sH, :, :, hH * 64:(hH + 1) * 64])

        exp_ev = {}
        msk_ev = {}
        av_ev = {}
        l_ev = {}
        lc_ev = {}
        ec_ev = {}
        last_av = [None, None]
        fin_cast = [None] * nh2
        fin_bcm = [None] * nh2
        st = {"en": None, "inv_rd": None, "ctxn_rd": d0, "lrow_rd": [None,
              None], "ibc_rd": None}

        def fin(hv):
            sH, hH = heads[hv]
            pbase = 64 * (hH % 2)
            e1 = k.act(invrow, lrow[hv % 2], AF.Ln,
                       waits=_ev(lc_ev[(hv, 15)], st["inv_rd"]))
            e2 = k.act(invrow, invrow, AF.Exp, scale=-1.0)
            ec = k.v("tensor_copy", invrow_b[hv % 2], invrow,
                     waits=_ev(e2, fin_bcm[hv - 2] if hv >= 2 else None))
            fin_cast[hv] = ec
            st["inv_rd"] = ec
            ev_ib = None
            for qg in range(4):
                ebc = k.mm(bc_ps, onesrb_sb[0:1, 0:64],
                           invrow_b[hv % 2][:, qg * 512:(qg + 1) * 512],
                           True, True,
                           waits=_ev(ec, st["ibc_rd"]), inc=True)
                fin_bcm[hv] = ebc
                ev_ib = k.v("tensor_copy",
                            invbc[:, qg * 512:(qg + 1) * 512], bc_ps,
                            waits=_ev(ebc,))
                st["ibc_rd"] = ev_ib
            en = k.v("tensor_tensor", ctxn, ctxu[hv % 2], invbc, ALU.mult,
                     waits=_ev(ev_ib, ec_ev[(hv, 15)], st["ctxn_rd"]))
            st["en"] = en
            st["ctxn_rd"] = k.qdma(CTX[sH, pbase:pbase + 64, hH // 2, :],
                                   ctxn, waits=_ev(en))

        do_load(0, None)
        for hi in range(nh2):
            sH, hH = heads[hi]
            j = hi % 2
            if hi + 1 < nh2:
                do_load(hi + 1, last_av[(hi + 1) % 2])
            for c in range(16):
                bs = c % 2
                q0 = c * 128
                ev_s = None
                if c > 0:
                    ev_s = k.mm(sc_ps[bs][:, 0:128],
                                ksl[j][:, (c - 1) * 128:c * 128],
                                qsl[j][:, q0:q0 + 128],
                                start=True, stop=False,
                                waits=_ev(ld_q[hi], ld_k[hi],
                                          exp_ev.get((hi, c - 2))))
                ev_s = k.mm(sc_ps[bs][:, 128:256],
                            ksl[j][:, c * 128:(c + 1) * 128],
                            qsl[j][:, q0:q0 + 128],
                            start=(c == 0), stop=True,
                            waits=_ev(ld_q[hi], ld_k[hi],
                                      exp_ev.get((hi, c - 2))), inc=True)
                lo = 128 if c == 0 else 0
                ee = k.act(ptp[bs][:, lo:256], sc_ps[bs][:, lo:256], AF.Exp,
                           bias=-CSOFT, scale=ISQS,
                           waits=_ev(ev_s, msk_ev.get((hi, c - 2)),
                                     av_ev.get((hi, c - 2)),
                                     l_ev.get((hi, c - 2)),
                                     d0 if (hi == 0 and c < 2) else None))
                exp_ev[(hi, c)] = ee
                em = k.v("tensor_tensor", ptp[bs][:, lo:256],
                         ptp[bs][:, lo:256], maskcat[:, lo:256], ALU.mult,
                         waits=_ev(ee, ev_mc, ev_mc2))
                msk_ev[(hi, c)] = em
                if c > 0:
                    cc = c - 1
                    nmm = 2 if cc > 0 else 1
                    lo2 = 128 if cc == 0 else 0
                    ev_av = None
                    ev_l = None
                    for u in range(nmm):
                        kt = cc - (nmm - 1) + u
                        ev_av = k.mm(
                            ctx_ps[cc % 2],
                            vsl[j][:, kt * 64:(kt + 1) * 64],
                            ptp[cc % 2][:, lo2 + u * 128:lo2 + u * 128 + 128],
                            start=(u == 0), stop=(u == nmm - 1),
                            waits=_ev(ld_v[hi], msk_ev[(hi, cc)],
                                      ec_ev.get((hi, cc - 2))),
                            inc=(u == nmm - 1))
                    for u in range(nmm):
                        ev_l = k.mm(
                            lb_ps[cc % 2][0:1, 0:128], onescb_sb,
                            ptp[cc % 2][:, lo2 + u * 128:lo2 + u * 128 + 128],
                            start=(u == 0), stop=(u == nmm - 1),
                            waits=_ev(lc_ev.get((hi, cc - 2)) if u == 0
                                      else None),
                            inc=(u == nmm - 1))
                    av_ev[(hi, cc)] = ev_av
                    l_ev[(hi, cc)] = ev_l
                    last_av[j] = ev_av
                    if cc % 2 == 0:
                        ec2 = k.act(ctxu[hi % 2][:, cc * 128:cc * 128 + 128],
                                    ctx_ps[cc % 2], AF.Copy,
                                    waits=_ev(ev_av, st["en"]))
                        lc2 = k.v("tensor_copy",
                                  lrow[hi % 2][:, cc * 128:cc * 128 + 128],
                                  lb_ps[cc % 2][0:1, 0:128],
                                  waits=_ev(ev_l, st["lrow_rd"][hi % 2]))
                    else:
                        ec2 = k.v("tensor_copy",
                                  ctxu[hi % 2][:, cc * 128:cc * 128 + 128],
                                  ctx_ps[cc % 2],
                                  waits=_ev(ev_av, st["en"]))
                        lc2 = k.act(lrow[hi % 2][:, cc * 128:cc * 128 + 128],
                                    lb_ps[cc % 2][0:1, 0:128], AF.Copy,
                                    waits=_ev(ev_l, st["lrow_rd"][hi % 2]))
                    ec_ev[(hi, cc)] = ec2
                    lc_ev[(hi, cc)] = lc2
                if c == 4 and hi >= 1:
                    fin(hi - 1)
            cc = 15
            ev_av = None
            ev_l = None
            for u in range(2):
                kt = cc - 1 + u
                ev_av = k.mm(ctx_ps[cc % 2],
                             vsl[j][:, kt * 64:(kt + 1) * 64],
                             ptp[cc % 2][:, u * 128:u * 128 + 128],
                             start=(u == 0), stop=(u == 1),
                             waits=_ev(ld_v[hi], msk_ev[(hi, cc)],
                                       ec_ev.get((hi, cc - 2))),
                             inc=(u == 1))
            for u in range(2):
                ev_l = k.mm(lb_ps[cc % 2][0:1, 0:128], onescb_sb,
                            ptp[cc % 2][:, u * 128:u * 128 + 128],
                            start=(u == 0), stop=(u == 1),
                            waits=_ev(lc_ev.get((hi, cc - 2)) if u == 0
                                      else None),
                            inc=(u == 1))
            av_ev[(hi, cc)] = ev_av
            l_ev[(hi, cc)] = ev_l
            last_av[j] = ev_av
            ec2 = k.act(ctxu[hi % 2][:, cc * 128:cc * 128 + 128],
                        ctx_ps[cc % 2], AF.Copy, waits=_ev(ev_av, st["en"]))
            lc2 = k.v("tensor_copy", lrow[hi % 2][:, cc * 128:cc * 128 + 128],
                      lb_ps[cc % 2][0:1, 0:128],
                      waits=_ev(ev_l, st["lrow_rd"][hi % 2]))
            ec_ev[(hi, cc)] = ec2
            lc_ev[(hi, cc)] = lc2
            st["lrow_rd"][hi % 2] = None
        fin(nh2 - 1)
        k.barrier(name)


    def emit_scan(name="scan"):
        ar.reset()
        at = [ar.take(128, T, F32) for _ in range(2)]
        xt = [ar.take(128, T, F32) for _ in range(2)]
        rt = [ar.take(128, T, F32) for _ in range(2)]
        ht = [ar.take(128, T, F32) for _ in range(2)]
        ev_a = [None, None]
        ev_x = [None, None]
        ev_r = [None, None]
        ev_h = [None, None]
        for s in range(SQ):
            for kc in range(KC):
                i = (s * KC + kc) % 2
                ev_a[i] = k.qdma(at[i], AA[s, :, kc, :], waits=_ev(ev_a[i]))
                ev_x[i] = k.qdma(xt[i], XS[s, :, kc, :], waits=_ev(ev_x[i]))
                ev_r[i] = k.qdma(rt[i], RS[s, :, kc, :], waits=_ev(ev_r[i]))
                es = k.v("tensor_tensor_scan", ht[i], at[i], xt[i], 0.0,
                         ALU.mult, ALU.add,
                         waits=_ev(ev_a[i], ev_x[i], ev_h[i]))
                ev_a[i] = es
                ev_x[i] = es
                eo = k.v("tensor_tensor", ht[i], ht[i], rt[i], ALU.add,
                         waits=_ev(ev_r[i]))
                ev_r[i] = eo
                ev_h[i] = k.qdma(RS[s, :, kc, :], ht[i], waits=_ev(eo))
        k.barrier(name)


    def emit_glu(name="glu"):
        ar.reset()
        xs = [ar.take(128, KC * T, BF16) for _ in range(2)]
        wg = [wslot[0][:, 0:KC * 512], wslot[0][:, KC * 512:KC * 1024]]
        wu = [wslot[1][:, 0:KC * 512], wslot[1][:, KC * 512:KC * 1024]]
        glu_war = _ev(wstate["free"][0], wstate["free"][1])
        evb = [ar.take(128, TG, BF16) for _ in range(4)]
        evf = [ar.take(128, TG, F32) for _ in range(2)]
        evb_ev = [None] * 4
        evf_ev = [None] * 2
        ev_xs = []
        for s in range(SQ):
            ev_xs.append(k.qdma(
                xs[s].rearrange("p (kc t) -> p kc t", kc=KC), XN[s]))
        ev_wg = [None, None]
        ev_wu = [None, None]
        ev_wgfree = [None, None]
        ev_wufree = [None, None]
        bank_free = [None] * 4
        banks = [pbig[0][:, 0:512], pbig[0][:, 512:1024],
                 pbig[1][:, 0:512], pbig[1][:, 512:1024]]
        nb = 0
        wgv = W_gate.rearrange("(kc p) o -> p kc o", p=128)
        wuv = W_up.rearrange("(kc p) o -> p kc o", p=128)
        for ff in range(DFF // 512):
            i = ff % 2
            ev_wg[i] = k.gdma(
                wg[i].rearrange("p (kc o) -> p kc o", kc=KC),
                wgv[:, :, ff * 512:(ff + 1) * 512],
                waits=_ev(ev_wgfree[i], glu_war if ff < 2 else None))
            ev_wu[i] = k.gdma(
                wu[i].rearrange("p (kc o) -> p kc o", kc=KC),
                wuv[:, :, ff * 512:(ff + 1) * 512],
                waits=_ev(ev_wufree[i], glu_war if ff < 2 else None))
            for s in range(SQ):
                for tg in range(NTG):
                    t0 = tg * TG
                    for ocq in range(4):
                        bg_ = nb % 4
                        nb += 1
                        bu_ = nb % 4
                        nb += 1
                        ev_g = ev_u = None
                        for kc in range(KC):
                            ev_g = k.mm(
                                banks[bg_],
                                wg[i][:, kc * 512 + ocq * 128:
                                      kc * 512 + (ocq + 1) * 128],
                                xs[s][:, kc * T + t0:kc * T + t0 + TG],
                                start=(kc == 0), stop=(kc == KC - 1),
                                waits=_ev(ev_wg[i], ev_xs[s],
                                          bank_free[bg_]),
                                inc=(kc == KC - 1))
                        for kc in range(KC):
                            ev_u = k.mm(
                                banks[bu_],
                                wu[i][:, kc * 512 + ocq * 128:
                                      kc * 512 + (ocq + 1) * 128],
                                xs[s][:, kc * T + t0:kc * T + t0 + TG],
                                start=(kc == 0), stop=(kc == KC - 1),
                                waits=_ev(ev_wu[i], bank_free[bu_]),
                                inc=(kc == KC - 1))
                        sb_ = (ff * 8 + tg * 4 + ocq) % 2
                        eg = k.act(evf[sb_], banks[bg_], AF.Silu,
                                   waits=_ev(ev_g, evf_ev[sb_]))
                        bank_free[bg_] = eg
                        sl = (ff * 8 + tg * 4 + ocq) % 4
                        em = k.v("tensor_tensor", evb[sl], evf[sb_],
                                 banks[bu_], ALU.mult,
                                 waits=_ev(eg, ev_u, evb_ev[sl]))
                        bank_free[bu_] = em
                        evf_ev[sb_] = em
                        evb_ev[sl] = k.qdma(
                            GU[s, :, ff * 4 + ocq, t0:t0 + TG], evb[sl],
                            waits=_ev(em))
                        ev_wgfree[i] = ev_g
                        ev_wufree[i] = ev_u
        wstate["free"][0] = ev_g
        wstate["free"][1] = ev_u
        k.barrier(name)


    emit_ln(x_ext, "ln", 1e-5, dst_bf=XN, name="ln1")

    def emit_ffn(call, rsrc, rdst):
        nsrc = {-1: XN}
        for c in range(5):
            nsrc[c] = ND[c]
        for (c, eq, ek, evv, iq, ik, iv) in PLAN:
            for e, i, roles in _node_projs(c):
                emit_gemm(nsrc[i], W_We[e * D:(e + 1) * D, :], D, D,
                          mk_evac_edge(c, e, roles),
                          name="f%dn%de%d" % (call, c, e))
            emit_gemm(EQ, W_Wq[c * D:(c + 1) * D, :], D, D,
                      mk_evac_plain(QP), name="f%dn%dwq" % (call, c))
            emit_gemm(EK, W_Wk[c * D:(c + 1) * D, :], D, D,
                      mk_evac_plain(KP), name="f%dn%dwk" % (call, c))
            emit_gemm_tm(EV, W_Wv[c * D:(c + 1) * D, :], D, D, VP,
                         name="f%dn%dwv" % (call, c))
            emit_mha(name="f%dn%dmha" % (call, c))

            def pre_rt(ctx):
                ctx["rt_load"] = QD
            emit_gemm(CTX, W_Wo[c * D:(c + 1) * D, :], D, D,
                      mk_evac_combine(c), name="f%dn%dwo" % (call, c),
                      pre_alloc=pre_rt)
        emit_ln(N5, "lnc", 1e-6, combine={"rsrc": rsrc, "rdst": rdst},
                name="f%dfin" % call)

    emit_ffn(0, x_ext, R)

    emit_ln(R, "ln", 1e-5, dst_bf=XN, dst_f32=RS, name="ln2")
    emit_ln(RS, "rms", 1e-6, dst_bf=XN, rms_g=0, name="rmsmix")
    pre_mg, evac_mg = mk_evac_mingru()
    emit_gemm(XN, W_gvd, D, 3 * D, evac_mg, name="gvd", pre_alloc=pre_mg)
    emit_scan()
    emit_ln(RS, "rms", 1e-6, dst_bf=XN, rms_g=1, name="rmsattn")
    emit_gemm(XN, W_qkv, D, 2 * D, mk_evac_plain(QP, KP, split_oc=KC),
              name="qkvfm")
    emit_gemm_tm(XN, W_qkv, D, D, VP, wcol0=2 * D, name="qkvtm")
    emit_sliding()

    def pre_rs(ctx):
        ctx["rt_load"] = RS
    emit_gemm(CTX, W_oT, D, D, mk_evac_resid(RS), name="slwo",
              pre_alloc=pre_rs)
    emit_ln(RS, "rms", 1e-6, dst_bf=XN, rms_g=2, name="rmsffn")
    emit_glu()

    def pre_rs2(ctx):
        ctx["rt_load"] = RS
        nxb = ctx["nxb"]
        ctx["rt2"] = [WPOOL[:, 32768 + i * 8192:40960 + i * 8192].bitcast(F32)
                      for i in range(nxb)]
        ctx["rt2_ev"] = [None] * nxb
        ctx["rt2_rd"] = [None] * nxb

    def evac_wout(ctx, s, oc, tg, ps, ev_mm, i):
        t0 = tg * TG
        if oc == 0:
            ctx["rt2_ev"][i] = k.qdma(
                ctx["rt2"][i].rearrange("p (kc t) -> p kc t", kc=KC),
                R[s, :, :, t0:t0 + TG],
                waits=_ev(ctx["rt2_ev"][i], ctx["rt2_rd"][i]))
        sl = oc % 4
        rt = ctx["rt"][i][:, oc * TG:(oc + 1) * TG]
        rt2 = ctx["rt2"][i][:, oc * TG:(oc + 1) * TG]
        e1 = k.v("tensor_tensor", ctx["evf"][sl], ps, rt, ALU.add,
                 waits=_ev(ev_mm, ctx["rt_ev"][i], ctx["evf_ev"][sl]))
        ctx["rt_rd"][i] = e1
        eo = k.v("tensor_tensor", ctx["evf"][sl], ctx["evf"][sl], rt2,
                 ALU.add, waits=_ev(ctx["rt2_ev"][i]))
        ctx["rt2_rd"][i] = eo
        ctx["evf_ev"][sl] = k.qdma(R[s, :, oc, t0:t0 + TG], ctx["evf"][sl],
                                   waits=_ev(eo))
        return e1

    emit_gemm(GU, W_out, DFF, D, evac_wout, name="wout", pre_alloc=pre_rs2)

    emit_ln(R, "ln", 1e-5, dst_bf=XN, name="ln3")
    emit_ffn(1, R, out_ext)

    if DEBUG_DUMP:
        k.unstop()
        for nm in DEBUG_DUMP:
            k.qdma(dbg_outs[nm], SCRATCH[nm])
    k.unstop()
    k.barrier("final", final=True)

    with ExitStack() as es:
        k.finalize(es)
    global LAST_K
    LAST_K = k
    return nc


LAST_K = None


def _prep_inputs(inputs):
    f = np.float32
    g = {}
    x = np.asarray(inputs["L0_attn_x"], f)
    g["x_cores"] = np.ascontiguousarray(
        x.reshape(16, T, KC, 128).transpose(0, 3, 2, 1))

    def tp(a):
        return np.ascontiguousarray(np.asarray(a, f).T)

    g["We"] = np.ascontiguousarray(
        np.asarray(inputs["L0_ffn_We"], f).reshape(20 * D, D))
    g["Wqn"] = np.ascontiguousarray(
        np.asarray(inputs["L0_ffn_Wq"], f).reshape(6 * D, D))
    g["Wkn"] = np.ascontiguousarray(
        np.asarray(inputs["L0_ffn_Wk"], f).reshape(6 * D, D))
    g["Wvn"] = np.ascontiguousarray(
        np.asarray(inputs["L0_ffn_Wv"], f).reshape(6 * D, D))
    g["Won"] = np.ascontiguousarray(
        np.asarray(inputs["L0_ffn_Wo"], f).reshape(6 * D, D))
    g["Wgvd"] = np.ascontiguousarray(np.concatenate(
        [np.asarray(inputs["L0_attn_Wg"], f).T,
         np.asarray(inputs["L0_attn_Wv"], f).T,
         np.asarray(inputs["L0_attn_Wd"], f).T], axis=1))
    g["Wqkv"] = tp(inputs["L0_attn_Wqkv"])
    g["WoT"] = tp(inputs["L0_attn_Wo"])
    g["WgateT"] = tp(inputs["L0_attn_Wgate"])
    g["WupT"] = tp(inputs["L0_attn_Wup"])
    g["WoutT"] = tp(inputs["L0_attn_Wout"])

    be = np.asarray(inputs["L0_ffn_be"], f)
    g["becols"] = np.ascontiguousarray(
        be.reshape(20, KC, 128).transpose(2, 0, 1).reshape(128, 20 * KC))
    bgvd = np.stack([np.asarray(inputs["L0_attn_bg"], f),
                     np.asarray(inputs["L0_attn_bv"], f),
                     np.asarray(inputs["L0_attn_bd"], f)])
    g["bgvd"] = np.ascontiguousarray(
        bgvd.reshape(3, KC, 128).transpose(2, 0, 1).reshape(128, 3 * KC))
    g["ewt"] = np.ascontiguousarray(np.tile(
        np.asarray(inputs["L0_ffn_ew"], f).reshape(1, 18), (128, 1)))
    g["actwt"] = np.ascontiguousarray(np.tile(
        np.asarray(inputs["L0_ffn_act_w"], f).reshape(1, 6), (128, 1)))
    g["lng"] = np.ascontiguousarray(
        np.asarray(inputs["L0_ffn_ln_g"], f).reshape(KC, 128).T)
    g["lnb"] = np.ascontiguousarray(
        np.asarray(inputs["L0_ffn_ln_b"], f).reshape(KC, 128).T)
    rms = np.stack([np.asarray(inputs["L0_attn_w_rms_mix"], f),
                    np.asarray(inputs["L0_attn_w_rms_attn"], f),
                    np.asarray(inputs["L0_attn_w_rms_ffn"], f)])
    g["rmsc"] = np.ascontiguousarray(
        rms.reshape(3, KC, 128).transpose(2, 0, 1).reshape(128, 3 * KC))
    return g


SHARED_KEYS = ("We", "Wqn", "Wkn", "Wvn", "Won", "Wgvd", "Wqkv", "WoT",
               "WgateT", "WupT", "WoutT", "becols", "bgvd", "ewt", "actwt",
               "lng", "lnb", "rmsc")


def kernel(**inputs):
    g = _prep_inputs(inputs)
    nc = build_program()
    in_maps = []
    for c in range(NCORES):
        m = {n: g[n] for n in SHARED_KEYS}
        m["x"] = np.ascontiguousarray(g["x_cores"][2 * c:2 * c + 2])
        in_maps.append(m)
    res = run_bass_kernel_spmd(nc, in_maps, core_ids=list(range(NCORES)))
    outs = []
    for c in range(NCORES):
        o = res.results[c]["out"]
        outs.append(o.transpose(0, 3, 2, 1).reshape(2, T, D))
    return np.ascontiguousarray(
        np.concatenate(outs, axis=0).astype(np.float32))


# version 13
# speedup vs baseline: 1.0642x; 1.0086x over previous
import numpy as np
import ml_dtypes
from contextlib import ExitStack

import concourse.bass as bass
import concourse.mybir as mybir
import concourse.dve_ops as dve_ops
from concourse.bass_utils import run_bass_kernel_spmd

F32 = mybir.dt.float32
BF16 = mybir.dt.bfloat16
AF = mybir.ActivationFunctionType
ALU = mybir.AluOpType

NCORES = 8
SQ = 2
T = 2048
D = 1024
KC = D // 128
TG = 512
NTG = T // TG
DFF = 4096
CSOFT = 24.0
ISQD = 1.0 / np.sqrt(128.0)
ISQS = 1.0 / np.sqrt(64.0)
BF16NP = ml_dtypes.bfloat16


def _dag_plan():
    plan, lind = [], 0
    for c in range(6):
        nsrc = min(c + 1, 5)
        snode = c - nsrc
        plan.append((c, lind + nsrc - 1, lind, lind + nsrc // 2,
                     c - 1, snode, snode + nsrc // 2))
        lind += nsrc
    return plan


PLAN = _dag_plan()


def _node_projs(c):
    (_, eq, ek, ev, iq, ik, iv) = PLAN[c]
    projs, order = {}, []
    for e, i, r in ((eq, iq, 0), (ek, ik, 1), (ev, iv, 2)):
        if (e, i) not in projs:
            projs[(e, i)] = []
            order.append((e, i))
        projs[(e, i)].append(r)
    return [(e, i, projs[(e, i)]) for (e, i) in order]


EWBE_TRIPLES = []
for _c in range(6):
    for _e, _i, _roles in _node_projs(_c):
        for _r in _roles:
            EWBE_TRIPLES.append((_c, _r, _e))

PHASE_LIMIT = None
DEBUG_DUMP = []
DEBUG_LN = None
PROFILE_SCOPES = False
SIM_SAFE = False


def _ev(*evs):
    out = []
    for e in evs:
        if e is None:
            continue
        if isinstance(e, tuple) and e and isinstance(e[0], str):
            out.append(e)
        else:
            for x in e:
                if x is not None:
                    out.append(x)
    return tuple(out)


class K:
    ENGINES = ("tensor", "vector", "scalar", "sync", "gpsimd")

    def __init__(self, nc):
        self.nc = nc
        self.streams = {e: [] for e in self.ENGINES}
        self.cnt = {}
        self.lastwait = {}
        self.ndma_q = 0
        self.nphase = 0
        self.stopped = False
        self.phase_names = {}

    def _rec(self, eng, fn, waits=(), inc=None):
        if self.stopped:
            return None
        ws = []
        for w in waits:
            if w is None:
                continue
            s, c = w
            if c <= 0:
                continue
            if s in self.DMA_STREAMS or s in ("w0", "w1"):
                assert c == self.cnt.get(s, 0), (
                    "unsound dma wait on %s: %d != %d (phase %d)" % (
                        s, c, self.cnt.get(s, 0), self.nphase))
            key = (eng, s)
            if c > self.lastwait.get(key, 0):
                ws.append((s, c))
                self.lastwait[key] = c
        ev = None
        if inc is not None:
            s, amt = inc
            self.cnt[s] = self.cnt.get(s, 0) + amt
            ev = (s, self.cnt[s])
        self.streams[eng].append((ws, fn, inc, self.nphase))
        return ev

    DMA_STREAMS = ("xa", "xb2", "ra", "rb", "r2", "s0", "s1", "s2", "s3",
                   "f0", "f1", "f2", "f3", "mq0", "mq1", "mc0", "mc1",
                   "ha", "hb")

    def qdma(self, out, in_, waits=(), stream="xa"):
        self.ndma_q += 1
        return self._rec("sync", lambda e: e.dma_start(out=out, in_=in_),
                         waits, (stream, 16))

    def gdma(self, out, in_, waits=(), stream="w0"):
        return self._rec("gpsimd", lambda e: e.dma_start(out=out, in_=in_),
                         waits, (stream, 16))

    def mm(self, out, lhsT, rhs, start, stop, waits=(), inc=False):
        return self._rec(
            "tensor",
            lambda e: e.matmul(out, lhsT, rhs, start=start, stop=stop),
            waits, ("t", 1) if inc else None)

    def act(self, out, in_, func, bias=0.0, scale=1.0, accum_out=None,
            waits=(), inc=True):
        return self._rec(
            "scalar",
            lambda e: e.activation(out, in_, func, bias=bias, scale=scale,
                                   accum_out=accum_out),
            waits, ("s", 1) if inc else None)

    def v(self, opname, *args, waits=(), inc=True, **kw):
        return self._rec(
            "vector", lambda e: getattr(e, opname)(*args, **kw),
            waits, ("v", 1) if inc else None)

    def vrecip(self, out, in_, scratch, waits=()):
        return self.v("reciprocal", out, in_, waits=waits)

    def now(self, sem):
        c = self.cnt.get(sem, 0)
        return (sem, c) if c > 0 else None

    STORE_STREAMS = ("s0", "s1", "s2", "s3", "f0", "f1", "f2", "f3",
                     "mc0", "mc1", "ha", "hb", "r2")

    def dma_now(self):
        return tuple(w for w in (self.now(n) for n in self.STORE_STREAMS)
                     if w is not None)

    def barrier(self, name="", final=False):
        if self.stopped:
            return
        self.nphase += 1
        bs = {"tensor": "bt", "vector": "bv", "scalar": "bs",
              "sync": "bq", "gpsimd": "bg"}
        evs = []
        for eng in self.ENGINES:
            s = bs[eng]
            evs.append(self._rec(eng, lambda e: e.nop(), (), (s, 1)))
        allw = list(evs)
        for s in list(self.cnt):
            if not s.startswith("b") and (final or s not in ("w0", "w1")):
                allw.append(self.now(s))
        sync_w = [w for w in evs if w is not None]
        sync_w += [w for w in (self.now(x) for x in ("t", "v", "s"))
                   if w is not None]
        for eng in self.ENGINES:
            ws = sync_w if (eng == "sync" and not final) else allw
            for w in ws:
                if w is not None:
                    self._rec(eng, None, (w,), None)
        self.phase_names[self.nphase - 1] = name
        if PHASE_LIMIT is not None and self.nphase >= PHASE_LIMIT:
            self.stopped = True

    def unstop(self):
        self.stopped = False

    def finalize(self, es):
        nc = self.nc
        names = (["t", "v", "s", "w0", "w1"]
                 + list(self.DMA_STREAMS)
                 + ["bt", "bv", "bs", "bq", "bg"])
        self.semh = {n: es.enter_context(nc.semaphore(n)) for n in names}
        blk = es.enter_context(nc.Block())

        def runner(stream):
            def run(eng):
                for (ws, fn, inc, _ph) in stream:
                    for (s, c) in ws:
                        eng.wait_ge(self.semh[s], c)
                    if fn is None:
                        continue
                    ins = fn(eng)
                    if inc is not None:
                        ins.then_inc(self.semh[inc[0]], inc[1])
            return run

        if not PROFILE_SCOPES:
            blk.tensor(runner(self.streams["tensor"]))
            blk.vector(runner(self.streams["vector"]))
            blk.scalar(runner(self.streams["scalar"]))
            blk.sync(runner(self.streams["sync"]))
            blk.gpsimd(runner(self.streams["gpsimd"]))
        else:
            nph = self.nphase + 1
            byph = {e: {} for e in self.ENGINES}
            for e in self.ENGINES:
                for rec in self.streams[e]:
                    byph[e].setdefault(rec[3], []).append(rec)
            for ph in range(nph + 1):
                nm = "ph%03d_%s" % (ph, self.phase_names.get(ph, "x"))
                has = any(byph[e].get(ph) for e in self.ENGINES)
                if not has:
                    continue
                with nc.named_scope(nm):
                    for e in self.ENGINES:
                        recs = byph[e].get(ph)
                        if recs:
                            getattr(blk, e)(runner(recs))


class Arena:
    def __init__(self, ap_f32):
        self.ap = ap_f32
        self.cap = ap_f32.shape[1]
        self.off = 0

    def reset(self):
        self.off = 0

    def take(self, p, n, dtype):
        nf = n if dtype == F32 else (n + 1) // 2
        off = (self.off + 7) & ~7
        self.off = off + nf
        assert self.off <= self.cap, "arena overflow: %d > %d" % (
            self.off, self.cap)
        sl = self.ap[0:p, off:off + nf]
        return sl if dtype == F32 else sl.bitcast(BF16)


def build_program():
    nc = bass.Bass("TRN2", target_bir_lowering=False)
    k = K(nc)

    def ein(name, shape, dt=F32):
        return nc.dram_tensor(name, list(shape), dt, kind="ExternalInput").ap()

    x_ext = ein("x", (SQ, 128, KC, T))
    out_ext = nc.dram_tensor("out", [SQ, 128, KC, T], F32,
                             kind="ExternalOutput").ap()

    W_We = ein("We", (20 * D, D))
    W_Wq = ein("Wqn", (6 * D, D))
    W_Wk = ein("Wkn", (6 * D, D))
    W_Wv = ein("Wvn", (6 * D, D))
    W_Wo = ein("Won", (6 * D, D))
    W_gvd = ein("Wgvd", (D, 3 * D))
    W_qkv = ein("Wqkv", (D, 3 * D))
    W_oT = ein("WoT", (D, D))
    W_gate = ein("WgateT", (D, DFF))
    W_up = ein("WupT", (D, DFF))
    W_out = ein("WoutT", (DFF, D))

    becols = ein("becols", (128, 20 * KC))
    bgvdc = ein("bgvd", (128, 3 * KC))
    ew_t = ein("ewt", (128, 18))
    actw_t = ein("actwt", (128, 6))
    lngc = ein("lng", (128, KC))
    lnbc = ein("lnb", (128, KC))
    rmscc = ein("rmsc", (128, 3 * KC))

    ones_col_f = nc.inline_tensor(np.ones((128, 1), np.float32), "onescf")
    ones_col_b = nc.inline_tensor(np.ones((128, 1), BF16NP), "onescb")
    ones_row_f = nc.inline_tensor(np.ones((1, 128), np.float32), "onesrf")
    ones_row_b = nc.inline_tensor(np.ones((1, 128), BF16NP), "onesrb")
    kk_, jj_ = np.meshgrid(np.arange(128), np.arange(128), indexing="ij")
    mask_cur = nc.inline_tensor((kk_ <= jj_).astype(BF16NP), "mcur")
    mask_prev = nc.inline_tensor((kk_ >= jj_ + 1).astype(BF16NP), "mprev")
    CONSTVALS = [1e-5, 1e-6, -CSOFT, -0.5, float(ISQD), float(ISQS),
                 0.5, 0.998, 0.001, 1.0, 0.0]
    cvals_d = nc.inline_tensor(
        np.tile(np.asarray(CONSTVALS, np.float32)[None, :], (128, 1)),
        "cvals")

    def scr(name, shape, dt):
        return nc.dram_tensor(name, list(shape), dt, kind="Internal").ap()

    fmshape = (SQ, 128, KC, T)
    R = scr("R", fmshape, F32)
    RS = scr("RS", fmshape, F32)
    XN = scr("XN", fmshape, BF16)
    ND = scr("ND", (5,) + fmshape, BF16)
    EQ = scr("EQ", fmshape, BF16)
    EK = scr("EK", fmshape, BF16)
    EV = scr("EV", fmshape, BF16)
    QD = scr("QD", fmshape, F32)
    N5 = scr("N5", fmshape, F32)
    QP = scr("QP", fmshape, BF16)
    KP = scr("KP", fmshape, BF16)
    VP = scr("VP", (SQ, 128, T // 128, D), BF16)
    CTX = scr("CTX", fmshape, BF16)
    XS = scr("XS", fmshape, F32)
    AA = scr("AA", fmshape, F32)
    GU = scr("GU", (SQ, 128, DFF // 128, T), BF16)

    SCRATCH = {"R": R, "RS": RS, "XN": XN, "ND": ND, "EQ": EQ, "EK": EK,
               "EV": EV, "QD": QD, "N5": N5, "QP": QP, "KP": KP, "VP": VP,
               "CTX": CTX, "XS": XS, "AA": AA, "GU": GU}
    global DEBUG_LN
    if DEBUG_LN is not None:
        DEBUG_LN = nc.dram_tensor("dbg_lnrows", [5, 128, TG], F32,
                                  kind="ExternalOutput").ap()
    dbg_outs = {}
    for nm in DEBUG_DUMP:
        src = SCRATCH[nm]
        dbg_outs[nm] = nc.dram_tensor(
            "dbg_%s" % nm, list(src.shape), src.dtype,
            kind="ExternalOutput").ap()

    cst = nc.alloc_sbuf_tensor("cst", [128, 400], F32).ap()
    onescf_sb = nc.alloc_sbuf_tensor("onescf_s", [128, 1], F32).ap()
    onescb_sb = nc.alloc_sbuf_tensor("onescb_s", [128, 1], BF16).ap()
    onesrf_sb = nc.alloc_sbuf_tensor("onesrf_s", [1, 128], F32).ap()
    onesrb_sb = nc.alloc_sbuf_tensor("onesrb_s", [1, 128], BF16).ap()
    mcur_sb = nc.alloc_sbuf_tensor("mcur_s", [128, 128], BF16).ap()
    mprev_sb = nc.alloc_sbuf_tensor("mprev_s", [128, 128], BF16).ap()
    rows = nc.alloc_sbuf_tensor("rows", [1, 3 * T], F32).ap()
    rowsb = nc.alloc_sbuf_tensor("rowsb", [1, 2 * T], BF16).ap()
    cvals_sb = nc.alloc_sbuf_tensor("cvals_s", [128, len(CONSTVALS)],
                                    F32).ap()
    for ci, cv in enumerate(CONSTVALS):
        nc.const_aps.register(F32, cv, cvals_sb[:, ci:ci + 1])

    WPOOL = nc.alloc_sbuf_tensor("wpool", [128, 49152], BF16).ap()
    wslot = [WPOOL[:, 0:24576], WPOOL[:, 24576:49152]]
    wstate = {"i": 0, "free": [None, None]}
    rem = nc.sbuf_bytes_remaining
    arena_words = (rem - 2048) // 4
    AR = nc.alloc_sbuf_tensor("arena", [128, arena_words], F32).ap()
    ar = Arena(AR)

    pbig = [nc.alloc_psum_tensor("pA", [128, 1024], F32).ap(),
            nc.alloc_psum_tensor("pB", [128, 1024], F32).ap()]
    pb = [nc.alloc_psum_tensor("p%d" % i, [128, 512], F32).ap()
          for i in range(4)]

    c_be, c_bgvd, c_ew, c_actw = 0, 160, 184, 202
    c_lng, c_lnb, c_rms, c_ewbe = 208, 216, 224, 248

    evs = [
        k.qdma(cst[:, c_be:c_be + 160], becols),
        k.qdma(cst[:, c_bgvd:c_bgvd + 24], bgvdc),
        k.qdma(cst[:, c_ew:c_ew + 18], ew_t),
        k.qdma(cst[:, c_actw:c_actw + 6], actw_t),
        k.qdma(cst[:, c_lng:c_lng + 8], lngc),
        k.qdma(cst[:, c_lnb:c_lnb + 8], lnbc),
        k.qdma(cst[:, c_rms:c_rms + 24], rmscc),
        k.qdma(onescf_sb, ones_col_f.ap()),
        k.qdma(onescb_sb, ones_col_b.ap()),
        k.qdma(onesrf_sb, ones_row_f.ap()),
        k.qdma(onesrb_sb, ones_row_b.ap()),
        k.qdma(mcur_sb, mask_cur.ap()),
        k.qdma(mprev_sb, mask_prev.ap()),
        k.qdma(cvals_sb, cvals_d.ap()),
    ]
    for idx, (c, r, e) in enumerate(EWBE_TRIPLES):
        k.v("tensor_scalar",
            cst[:, c_ewbe + idx * 8:c_ewbe + (idx + 1) * 8],
            cst[:, c_be + e * 8:c_be + (e + 1) * 8],
            cst[:, c_ew + c * 3 + r:c_ew + c * 3 + r + 1], None,
            ALU.mult, waits=_ev(k.now("xa"), k.now("ra"), k.now("mq0")))
    k.barrier("prep")

    def ew_col(c, r):
        return cst[:, c_ew + c * 3 + r:c_ew + c * 3 + r + 1]

    def ewbe_col(c, r, e, kc):
        i = EWBE_TRIPLES.index((c, r, e))
        return cst[:, c_ewbe + i * 8 + kc:c_ewbe + i * 8 + kc + 1]

    def becol(e, kc):
        return cst[:, c_be + e * 8 + kc:c_be + e * 8 + kc + 1]

    def bgvd_col(g, kc):
        return cst[:, c_bgvd + g * 8 + kc:c_bgvd + g * 8 + kc + 1]

    def actw_col(c):
        return cst[:, c_actw + c:c_actw + c + 1]

    def rms_col(g, kc):
        return cst[:, c_rms + g * 8 + kc:c_rms + g * 8 + kc + 1]

    def lng_col(kc):
        return cst[:, c_lng + kc:c_lng + kc + 1]

    def lnb_col(kc):
        return cst[:, c_lnb + kc:c_lnb + kc + 1]


    def emit_ln(src, kind, eps, dst_bf=None, dst_f32=None, rms_g=None,
                combine=None, name="ln"):
        ar.reset()
        xb = [ar.take(128, KC * TG, F32) for _ in range(2)]
        xsq = ar.take(128, KC * TG, BF16)
        tmp = ar.take(128, TG, F32)
        tmp2 = ar.take(128, TG, F32)
        ob = [ar.take(128, TG, F32) for _ in range(4)]
        ob2 = [ar.take(128, TG, BF16) for _ in range(4)]
        rt = [ar.take(128, KC * TG, F32)]
        mrow = rows[:, 0:TG]
        vrow = rows[:, TG:2 * TG]
        m2row = rows[:, 2 * TG:3 * TG]
        lnrow = rows[:, 3 * TG:4 * TG]
        rstdrow = rows[:, 4 * TG:5 * TG]
        sum_ps, sq_ps, mbc, rbc = pb[0], pb[1], pb[2], pb[3]
        ev_x = [None, None]
        ev_xrd = [None, None]
        ev_rt = [None, None]
        ev_rt_rd = [None, None]
        ev_sum_rd = ev_sq_rd = None
        ev_xsq_rd = None
        ev_mbc_rd = ev_rbc_rd = None
        ev_rows_t = None
        ev_ob = [None] * 4
        ev_ob2 = [None] * 4
        tiles_ln = [(s, tg) for s in range(SQ) for tg in range(NTG)]
        ev_xl = [None] * 8

        def issue_x(idx):
            s, tg = tiles_ln[idx]
            i = idx % 2
            ev_xl[idx] = k.qdma(
                xb[i].rearrange("p (kc t) -> p kc t", kc=KC),
                src[s, :, :, tg * TG:(tg + 1) * TG],
                waits=_ev(ev_x[i], ev_xrd[i]), stream=("xa", "xb2")[i])

        issue_x(0)
        for idx in range(len(tiles_ln)):
            s, tg = tiles_ln[idx]
            if True:
                i = idx % 2
                ri = 0
                t0 = tg * TG
                if idx + 1 < len(tiles_ln):
                    issue_x(idx + 1)
                ev_x[i] = ev_xl[idx]
                if combine is not None:
                    ev_rt[ri] = k.qdma(
                        rt[ri].rearrange("p (kc t) -> p kc t", kc=KC),
                        combine["rsrc"][s, :, :, t0:t0 + TG],
                        waits=_ev(ev_rt[ri], ev_rt_rd[ri]), stream="ra")
                ev_sqa = []
                for c in range(KC):
                    ev_sqa.append(k.act(
                        xsq[:, c * TG:(c + 1) * TG],
                        xb[i][:, c * TG:(c + 1) * TG], AF.Square,
                        waits=_ev(ev_x[i], ev_xsq_rd if c == 0 else None)))
                ev_sum = None
                if kind != "rms":
                    for c in range(KC):
                        ev_sum = k.mm(
                            sum_ps[0:1, :], onescf_sb,
                            xb[i][:, c * TG:(c + 1) * TG],
                            start=(c == 0), stop=(c == KC - 1),
                            waits=_ev(ev_x[i], ev_sum_rd),
                            inc=(c == KC - 1))
                ev_sq = None
                for c in range(KC):
                    ev_sq = k.mm(
                        sq_ps[0:1, :], onescb_sb,
                        xsq[:, c * TG:(c + 1) * TG],
                        start=(c == 0), stop=(c == KC - 1),
                        waits=_ev(ev_sqa[c], ev_sq_rd), inc=(c == KC - 1))
                ev_xsq_rd = ev_sq
                if kind != "rms":
                    ev_m = k.v("tensor_scalar", mrow, sum_ps[0:1, :],
                               1.0 / D, None, ALU.mult,
                               waits=_ev(ev_sum, ev_rows_t))
                    ev_sum_rd = ev_m
                    ev_m2 = k.v("tensor_mul", m2row, mrow, mrow)
                    ev_v0 = k.v("tensor_scalar", vrow, sq_ps[0:1, :],
                                1.0 / D, None, ALU.mult, waits=_ev(ev_sq))
                    ev_sq_rd = ev_v0
                    ev_var = k.v("tensor_sub", vrow, vrow, m2row,
                                 waits=_ev(ev_m2))
                else:
                    ev_var = k.v("tensor_scalar", vrow, sq_ps[0:1, :],
                                 1.0 / D, None, ALU.mult,
                                 waits=_ev(ev_sq, ev_rows_t))
                    ev_sq_rd = ev_var
                    ev_m = None
                ev_ln = k.act(lnrow, vrow, AF.Ln, bias=eps, waits=_ev(ev_var))
                ev_rstd = k.act(rstdrow, lnrow, AF.Exp, scale=-0.5)
                ev_mbc = None
                if kind != "rms":
                    ev_mbc = k.mm(mbc, onesrf_sb, mrow, True, True,
                                  waits=_ev(ev_m, ev_mbc_rd), inc=True)
                ev_rbc = k.mm(rbc, onesrf_sb, rstdrow, True, True,
                              waits=_ev(ev_rstd, ev_rbc_rd), inc=True)
                ev_rows_t = (ev_mbc, ev_rbc)
                if DEBUG_LN is not None and s == 0 and tg == 0:
                    dr = DEBUG_LN
                    k.qdma(dr[0, 0:1, 0:TG], mrow, waits=_ev(ev_m))
                    k.qdma(dr[1, 0:1, 0:TG], vrow, waits=_ev(ev_var))
                    k.qdma(dr[2, 0:1, 0:TG], rstdrow, waits=_ev(ev_rstd))
                    ebc = k.v("tensor_copy", ob[3], rbc, waits=_ev(ev_rbc))
                    k.qdma(dr[3, :, 0:TG], ob[3], waits=_ev(ebc))
                    ebc2 = k.v("tensor_copy", ob[2], mbc, waits=_ev(ev_mbc))
                    k.qdma(dr[4, :, 0:TG], ob[2], waits=_ev(ebc2))
                    ev_ob[3] = k.now("q%d" % (((k.ndma_q - 1) // 256) % 8))
                    ev_ob[2] = ev_ob[3]
                last_apply = None
                for c in range(KC):
                    j = c % 4
                    xc = xb[i][:, c * TG:(c + 1) * TG]
                    if kind == "rms":
                        e1 = k.v("tensor_tensor", tmp, xc, rbc, ALU.mult,
                                 waits=_ev(ev_rbc))
                        eo = k.v("tensor_scalar", ob2[j], tmp,
                                 rms_col(rms_g, c), None, ALU.mult,
                                 waits=_ev(ev_ob2[j]))
                        ev_rbc_rd = e1
                        ev_ob2[j] = k.qdma(dst_bf[s, :, c, t0:t0 + TG],
                                           ob2[j], waits=_ev(eo),
                                           stream="s%d" % j)
                        last_apply = e1
                    elif kind == "ln":
                        e1 = k.v("tensor_tensor", tmp, xc, mbc, ALU.subtract,
                                 waits=_ev(ev_mbc))
                        ev_mbc_rd = e1
                        if dst_f32 is not None:
                            eo = k.v("tensor_tensor", ob[j], tmp, rbc,
                                     ALU.mult, waits=_ev(ev_rbc, ev_ob[j]))
                            ev_rbc_rd = eo
                            ev_f = k.qdma(dst_f32[s, :, c, t0:t0 + TG],
                                          ob[j], waits=_ev(eo),
                                          stream="f%d" % j)
                            ec = k.act(ob2[j], ob[j], AF.Copy,
                                       waits=_ev(eo, ev_ob2[j]))
                            ev_ob[j] = (ev_f, ec)
                            ev_ob2[j] = k.qdma(dst_bf[s, :, c, t0:t0 + TG],
                                               ob2[j], waits=_ev(ec),
                                               stream="s%d" % j)
                            last_apply = e1
                        else:
                            eo = k.v("tensor_tensor", ob2[j], tmp, rbc,
                                     ALU.mult, waits=_ev(ev_rbc, ev_ob2[j]))
                            ev_rbc_rd = eo
                            if (DEBUG_LN is not None and s == 0 and tg == 0
                                    and c == 0):
                                k.qdma(DEBUG_LN[1, :, 0:TG], tmp,
                                       waits=_ev(e1, eo))
                            ev_ob2[j] = k.qdma(dst_bf[s, :, c, t0:t0 + TG],
                                               ob2[j], waits=_ev(eo),
                                               stream="s%d" % j)
                    else:
                        e1 = k.v("tensor_tensor", tmp, xc, mbc, ALU.subtract,
                                 waits=_ev(ev_mbc))
                        ev_mbc_rd = e1
                        e2 = k.v("tensor_tensor", tmp2, tmp, rbc, ALU.mult,
                                 waits=_ev(ev_rbc))
                        ev_rbc_rd = e2
                        e3 = k.v("tensor_scalar", tmp, tmp2, lng_col(c),
                                 lnb_col(c), ALU.mult, ALU.add)
                        eo = k.v("scalar_tensor_tensor", ob[j], tmp, 0.5,
                                 rt[ri][:, c * TG:(c + 1) * TG],
                                 ALU.mult, ALU.add,
                                 waits=_ev(ev_rt[ri], ev_ob[j]))
                        ev_rt_rd[ri] = eo
                        ev_ob[j] = k.qdma(
                            combine["rdst"][s, :, c, t0:t0 + TG], ob[j],
                            waits=_ev(eo), stream="f%d" % j)
                        last_apply = e1
                ev_xrd[i] = last_apply
        k.barrier(name)

    def load_weight(wsb, Wd, nkc, O, col0=0, war=None, stream="w0"):
        evl = None
        for kc in range(nkc):
            evl = k.gdma(wsb[:, kc * O:(kc + 1) * O],
                         Wd[kc * 128:(kc + 1) * 128, col0:col0 + O],
                         waits=_ev(war) if kc == 0 else (), stream=stream)
        return evl

    def emit_gemm(src, Wd, Kdim, O, evac, wcol0=0, name="gemm",
                  pre_alloc=None):
        ar.reset()
        nkc, noc = Kdim // 128, O // 128
        nxb = 2 if nkc <= 16 else 1
        if nkc * O <= 24576:
            si = wstate["i"] % 2
            wstate["i"] += 1
            wsb = wslot[si][:, 0:nkc * O]
            wwar = wstate["free"][si]
            wowner = [si]
        else:
            si = 0
            wsb = WPOOL[:, 0:nkc * O]
            wwar = _ev(wstate["free"][0], wstate["free"][1])
            wowner = [0, 1]
        xb = [ar.take(128, nkc * TG, BF16) for _ in range(nxb)]
        ctx = {"evf": [ar.take(128, TG, F32) for _ in range(4)],
               "evb": [ar.take(128, TG, BF16) for _ in range(4)],
               "evf_ev": [None] * 4, "evb_ev": [None] * 4,
               "rt_ev": [None] * nxb, "rt_rd": [None] * nxb, "ar": ar,
               "nxb": nxb}
        if pre_alloc:
            pre_alloc(ctx)
        if "rt_load" in ctx:
            ctx["rt"] = [ar.take(128, KC * TG, F32) for _ in range(nxb)]
        ev_w = load_weight(wsb, Wd, nkc, O, wcol0, wwar,
                           stream="w%d" % si)
        banks = [pbig[0][:, 0:512], pbig[0][:, 512:1024],
                 pbig[1][:, 0:512], pbig[1][:, 512:1024]]
        bank_free = [None] * 4
        ev_xfree = [None] * nxb
        ev_xl = [None] * 8
        nb = 0
        tiles = [(s, tg) for s in range(SQ) for tg in range(NTG)]

        def issue_loads(idx):
            s, tg = tiles[idx]
            i = idx % nxb
            t0 = tg * TG
            ev_xl[idx] = k.qdma(
                xb[i].rearrange("p (kc t) -> p kc t", kc=nkc),
                src[s, :, 0:nkc, t0:t0 + TG], waits=_ev(ev_xfree[i]),
                stream=("xa", "xb2")[i])
            if "rt_load" in ctx:
                ctx["rt_ev"][i] = k.qdma(
                    ctx["rt"][i].rearrange("p (kc t) -> p kc t", kc=KC),
                    ctx["rt_load"][s, :, :, t0:t0 + TG],
                    waits=_ev(ctx["rt_ev"][i], ctx["rt_rd"][i]),
                    stream=("ra", "rb")[i])

        issue_loads(0)
        for idx in range(len(tiles)):
            s, tg = tiles[idx]
            i = idx % nxb
            t0 = tg * TG
            if nxb > 1 and idx + 1 < len(tiles):
                issue_loads(idx + 1)
            ev_x = ev_xl[idx]
            if True:
                ev_last = None
                for oc in range(noc):
                    b = nb % 4
                    nb += 1
                    ev_mm = None
                    for kc in range(nkc):
                        ev_mm = k.mm(
                            banks[b],
                            wsb[:, kc * O + oc * 128:kc * O + (oc + 1) * 128],
                            xb[i][:, kc * TG:(kc + 1) * TG],
                            start=(kc == 0), stop=(kc == nkc - 1),
                            waits=_ev(ev_w, ev_x, bank_free[b]),
                            inc=(kc == nkc - 1))
                    bank_free[b] = evac(ctx, s, oc, tg, banks[b], ev_mm, i)
                    ev_last = ev_mm
                ev_xfree[i] = ev_last
                if nxb == 1 and idx + 1 < len(tiles):
                    issue_loads(idx + 1)
        for si in wowner:
            wstate["free"][si] = ev_last
        k.barrier(name)

    def emit_gemm_tm(src, Wd, Kdim, O, dst, dcol0=0, wcol0=0, name="gemmtm"):
        ar.reset()
        nkc, noh = Kdim // 128, O // 512
        si = wstate["i"] % 2
        wstate["i"] += 1
        wsb = wslot[si][:, 0:nkc * O]
        xb = [ar.take(128, nkc * TG, BF16) for _ in range(2)]
        evb = [ar.take(128, TG, BF16) for _ in range(4)]
        evb_ev = [None] * 4
        ev_w = load_weight(wsb, Wd, nkc, O, wcol0, wstate["free"][si])
        banks = [pbig[0][:, 0:512], pbig[0][:, 512:1024],
                 pbig[1][:, 0:512], pbig[1][:, 512:1024]]
        bank_free = [None] * 4
        ev_xfree = [None, None]
        ev_xl = [None] * 8
        nb = 0
        tiles = [(s, tg) for s in range(SQ) for tg in range(NTG)]

        def issue_loads(idx):
            s, tg = tiles[idx]
            i = idx % 2
            ev_xl[idx] = k.qdma(
                xb[i].rearrange("p (kc t) -> p kc t", kc=nkc),
                src[s, :, 0:nkc, tg * TG:(tg + 1) * TG],
                waits=_ev(ev_xfree[i]), stream=("xa", "xb2")[i])

        issue_loads(0)
        for idx in range(len(tiles)):
            s, tg = tiles[idx]
            i = idx % 2
            t0 = tg * TG
            if idx + 1 < len(tiles):
                issue_loads(idx + 1)
            ev_x = ev_xl[idx]
            if True:
                ev_last = None
                for j in range(TG // 128):
                    for oh in range(noh):
                        b = nb % 4
                        nb += 1
                        ev_mm = None
                        for kc in range(nkc):
                            ev_mm = k.mm(
                                banks[b],
                                xb[i][:, kc * TG + j * 128:
                                      kc * TG + (j + 1) * 128],
                                wsb[:, kc * O + oh * 512:
                                    kc * O + (oh + 1) * 512],
                                start=(kc == 0), stop=(kc == nkc - 1),
                                waits=_ev(ev_w, ev_x, bank_free[b]),
                                inc=(kc == nkc - 1))
                        sl = b
                        ec = k.act(evb[sl], banks[b], AF.Copy,
                                   waits=_ev(ev_mm, evb_ev[sl]))
                        bank_free[b] = ec
                        evb_ev[sl] = k.qdma(
                            dst[s, :, tg * (TG // 128) + j,
                                dcol0 + oh * 512:dcol0 + (oh + 1) * 512],
                            evb[sl], waits=_ev(ec), stream="s%d" % sl)
                        ev_last = ev_mm
                ev_xfree[i] = ev_last
        wstate["free"][si] = ev_last
        k.barrier(name)


    def mk_evac_plain(dst, dst2=None, split_oc=None):
        def evac(ctx, s, oc, tg, ps, ev_mm, i):
            sl = oc % 4
            d, o2 = (dst, oc) if (split_oc is None or oc < split_oc) \
                else (dst2, oc - split_oc)
            ec = k.act(ctx["evb"][sl], ps, AF.Copy,
                       waits=_ev(ev_mm, ctx["evb_ev"][sl]))
            ctx["evb_ev"][sl] = k.qdma(
                d[s, :, o2, tg * TG:(tg + 1) * TG], ctx["evb"][sl],
                waits=_ev(ec), stream="s%d" % sl)
            return ec
        return evac

    def mk_evac_edge(c, e, roles):
        multi = len(roles) > 1
        def evac(ctx, s, oc, tg, ps, ev_mm, i):
            t0 = tg * TG
            if multi:
                sf = oc % 2
                e_base = k.v("tensor_scalar", ctx["evf"][sf], ps,
                             becol(e, oc), None, ALU.add,
                             waits=_ev(ev_mm, ctx["evf_ev"][sf]))
                base = ctx["evf"][sf]
                readers = [e_base]
                for r in roles:
                    dstd = (EQ, EK, EV)[r]
                    sl = (oc * 2 + (0 if r == 0 else 1)) % 4
                    if r == 0:
                        sf2 = 2 + oc % 2
                        eq_ = k.v("tensor_scalar", ctx["evf"][sf2], base,
                                  ew_col(c, 0), None, ALU.mult,
                                  waits=_ev(ctx["evf_ev"][sf2]))
                        eqd = k.qdma(QD[s, :, oc, t0:t0 + TG],
                                     ctx["evf"][sf2], waits=_ev(eq_),
                                     stream="f%d" % sf2)
                        ecp = k.act(ctx["evb"][sl], ctx["evf"][sf2], AF.Copy,
                                    waits=_ev(eq_, ctx["evb_ev"][sl]))
                        ctx["evf_ev"][sf2] = (eqd, ecp)
                        ctx["evb_ev"][sl] = k.qdma(
                            dstd[s, :, oc, t0:t0 + TG], ctx["evb"][sl],
                            waits=_ev(ecp), stream="s%d" % sl)
                        readers.append(eq_)
                    else:
                        ecp = k.act(ctx["evb"][sl], base, AF.Identity,
                                    scale=ew_col(c, r),
                                    waits=_ev(e_base, ctx["evb_ev"][sl]))
                        ctx["evb_ev"][sl] = k.qdma(
                            dstd[s, :, oc, t0:t0 + TG], ctx["evb"][sl],
                            waits=_ev(ecp), stream="s%d" % sl)
                        readers.append(ecp)
                ctx["evf_ev"][sf] = tuple(readers)
                return e_base
            r = roles[0]
            if r == 0:
                sf = oc % 4
                sl = oc % 4
                eq_ = k.v("tensor_scalar", ctx["evf"][sf], ps,
                          becol(e, oc), ew_col(c, 0), ALU.add, ALU.mult,
                          waits=_ev(ev_mm, ctx["evf_ev"][sf]))
                eqd = k.qdma(QD[s, :, oc, t0:t0 + TG], ctx["evf"][sf],
                             waits=_ev(eq_), stream="f%d" % sf)
                ecp = k.act(ctx["evb"][sl], ctx["evf"][sf], AF.Copy,
                            waits=_ev(eq_, ctx["evb_ev"][sl]))
                ctx["evf_ev"][sf] = (eqd, ecp)
                ctx["evb_ev"][sl] = k.qdma(EQ[s, :, oc, t0:t0 + TG],
                                           ctx["evb"][sl], waits=_ev(ecp),
                                           stream="s%d" % sl)
                return eq_
            dstd = (EQ, EK, EV)[r]
            sl = oc % 4
            ecp = k.act(ctx["evb"][sl], ps, AF.Identity, scale=ew_col(c, r),
                        bias=ewbe_col(c, r, e, oc),
                        waits=_ev(ev_mm, ctx["evb_ev"][sl]))
            ctx["evb_ev"][sl] = k.qdma(dstd[s, :, oc, t0:t0 + TG],
                                       ctx["evb"][sl], waits=_ev(ecp),
                                       stream="s%d" % sl)
            return ecp
        return evac

    def mk_evac_combine(c):
        def evac(ctx, s, oc, tg, ps, ev_mm, i):
            t0 = tg * TG
            rt = ctx["rt"][i][:, oc * TG:(oc + 1) * TG]
            sl = oc % 4
            buf = ctx["evb"] if c < 5 else ctx["evf"]
            bev = "evb_ev" if c < 5 else "evf_ev"
            eo = k.v("scalar_tensor_tensor", buf[sl], ps, actw_col(c), rt,
                     ALU.mult, ALU.add,
                     waits=_ev(ev_mm, ctx["rt_ev"][i], ctx[bev][sl]))
            ctx["rt_rd"][i] = eo
            dstd = ND[c] if c < 5 else N5
            ctx[bev][sl] = k.qdma(dstd[s, :, oc, t0:t0 + TG], buf[sl],
                                  waits=_ev(eo),
                                  stream=("s%d" if c < 5 else "f%d") % sl)
            return eo
        return evac

    def mk_evac_resid(rdst):
        def evac(ctx, s, oc, tg, ps, ev_mm, i):
            t0 = tg * TG
            sl = oc % 4
            rt = ctx["rt"][i][:, oc * TG:(oc + 1) * TG]
            eo = k.v("tensor_tensor", ctx["evf"][sl], ps, rt, ALU.add,
                     waits=_ev(ev_mm, ctx["rt_ev"][i], ctx["evf_ev"][sl]))
            ctx["rt_rd"][i] = eo
            ctx["evf_ev"][sl] = k.qdma(rdst[s, :, oc, t0:t0 + TG],
                                       ctx["evf"][sl], waits=_ev(eo),
                                       stream="f%d" % sl)
            return eo
        return evac

    def mk_evac_mingru():
        def pre(ctx):
            ctx["sg"] = ctx["ar"].take(128, KC * TG, F32)
            ctx["sg_w"] = [None] * KC
            ctx["sg_rd"] = [None] * KC
        def evac(ctx, s, oc, tg, ps, ev_mm, i):
            t0 = tg * TG
            grp, cc = oc // KC, oc % KC
            if grp == 0:
                eo = k.act(ctx["sg"][:, cc * TG:(cc + 1) * TG], ps,
                           AF.Sigmoid, bias=bgvd_col(0, cc),
                           waits=_ev(ev_mm, ctx["sg_rd"][cc]))
                ctx["sg_w"][cc] = eo
                return eo
            if grp == 1:
                sf = cc % 2
                et = k.act(ctx["evf"][sf], ps, AF.Tanh, bias=bgvd_col(1, cc),
                           waits=_ev(ev_mm, ctx["evf_ev"][sf]))
                ex = k.v("tensor_tensor", ctx["evf"][2 + sf],
                         ctx["sg"][:, cc * TG:(cc + 1) * TG], ctx["evf"][sf],
                         ALU.mult,
                         waits=_ev(et, ctx["sg_w"][cc],
                                   ctx["evf_ev"][2 + sf]))
                ctx["sg_rd"][cc] = ex
                ctx["evf_ev"][sf] = ex
                ctx["evf_ev"][2 + sf] = k.qdma(
                    XS[s, :, cc, t0:t0 + TG], ctx["evf"][2 + sf],
                    waits=_ev(ex), stream="f%d" % (2 + sf))
                return et
            sf = cc % 2
            et = k.act(ctx["evf"][sf], ps, AF.Sigmoid, bias=bgvd_col(2, cc),
                       waits=_ev(ev_mm, ctx["evf_ev"][sf]))
            ea = k.v("tensor_scalar", ctx["evf"][2 + sf], ctx["evf"][sf],
                     0.998, 0.001, ALU.mult, ALU.add,
                     waits=_ev(et, ctx["evf_ev"][2 + sf]))
            ctx["evf_ev"][sf] = ea
            ctx["evf_ev"][2 + sf] = k.qdma(
                AA[s, :, cc, t0:t0 + TG], ctx["evf"][2 + sf],
                waits=_ev(ea), stream="f%d" % (2 + sf))
            return et
        return pre, evac


    def emit_mha(name="mha"):
        ar.reset()
        qsl = [ar.take(128, T, BF16) for _ in range(2)]
        ksl = [ar.take(128, T, BF16) for _ in range(2)]
        vsl = [ar.take(128, T, BF16) for _ in range(2)]
        ptb = [ar.take(128, 16 * TG, BF16) for _ in range(2)]
        invbc = ar.take(128, TG, BF16)
        ctxn = [ar.take(128, TG, BF16) for _ in range(2)]
        lrow = rows[:, 0:TG]
        invrow = rows[:, TG:2 * TG]
        rscr = rows[:, 2 * TG:3 * TG]
        invrow_b = [rowsb[:, 0:TG], rowsb[:, TG:2 * TG]]
        sc_ps = [pbig[0], pbig[1]]
        ctx_ps = [pb[0], pb[1]]
        lb_ps = [pb[2], pb[3]]
        heads = [(s, h) for s in range(SQ) for h in range(KC)]
        nh = len(heads)
        nu = nh * 4
        ld_q = [None] * nh
        ld_k = [None] * nh
        ld_v = [None] * nh
        exp_ev = {}
        av_ev = [None] * nu
        l_ev = [None] * nu
        lc_ev = [None] * nu
        cast_ev = [None] * nu
        bcm_ev = [None] * nu
        ib_ev = [None] * nu
        ptb_war = [None, None]
        ctx_war = [None, None]
        ctxn_war = [None, None]
        st = {"lrow_war": None}

        def do_load(hi, war):
            sH, hH = heads[hi]
            jj = hi % 2
            k.qdma(qsl[jj], QP[sH, :, hH, :],
                   waits=_ev(ld_q[hi - 2] if hi >= 2 else None, war),
                   stream=("mq0", "mq1")[jj])
            k.qdma(ksl[jj], KP[sH, :, hH, :], stream=("mq0", "mq1")[jj])
            ev = k.qdma(
                vsl[jj].rearrange("p (kt d) -> p kt d", kt=16),
                VP[sH, :, :, hH * 128:(hH + 1) * 128],
                stream=("mq0", "mq1")[jj])
            ld_q[hi] = ld_k[hi] = ld_v[hi] = ev

        def fin1(v):
            lc_ev[v] = k.v("tensor_copy", lrow, lb_ps[v % 2][0:1, :],
                           waits=_ev(l_ev[v], st["lrow_war"]))
            ev_r = k.vrecip(invrow, lrow, rscr,
                            waits=_ev(lc_ev[v],
                                      cast_ev[v - 1] if v >= 1 else None))
            st["lrow_war"] = ev_r
            cast_ev[v] = k.v("tensor_copy", invrow_b[v % 2], invrow,
                             waits=_ev(ev_r,
                                       bcm_ev[v - 2] if v >= 2 else None))

        def fin2(v):
            hv = v // 4
            sH, hH = heads[hv]
            q0 = (v % 4) * TG
            sl = v % 2
            bcm_ev[v] = k.mm(lb_ps[v % 2], onesrb_sb, invrow_b[v % 2],
                             True, True,
                             waits=_ev(cast_ev[v], lc_ev[v]), inc=True)
            ib_ev[v] = k.v("tensor_copy", invbc, lb_ps[v % 2],
                           waits=_ev(bcm_ev[v]))
            en = k.v("tensor_tensor", ctxn[sl], ctx_ps[v % 2], invbc,
                     ALU.mult, waits=_ev(av_ev[v], ctxn_war[sl]))
            ctx_war[v % 2] = en
            ctxn_war[sl] = k.qdma(CTX[sH, :, hH, q0:q0 + TG], ctxn[sl],
                                  waits=_ev(en), stream=("mc0", "mc1")[sl])

        def emit_avl_pair(v, kt2):
            hv = v // 4
            jv = hv % 2
            ptv = ptb[v % 2]
            for uu in range(2):
                kt = kt2 * 2 + uu
                av_ev[v] = k.mm(
                    ctx_ps[v % 2], vsl[jv][:, kt * 128:(kt + 1) * 128],
                    ptv[:, kt * TG:(kt + 1) * TG],
                    start=(kt == 0), stop=(kt == 15),
                    waits=_ev(ld_v[hv], exp_ev[(v, kt2)],
                              ctx_war[v % 2] if kt == 0 else None),
                    inc=(kt == 15))
            for uu in range(2):
                kt = kt2 * 2 + uu
                l_ev[v] = k.mm(
                    lb_ps[v % 2][0:1, :], onescb_sb,
                    ptv[:, kt * TG:(kt + 1) * TG],
                    start=(kt == 0), stop=(kt == 15),
                    waits=_ev(ib_ev[v - 2] if (kt == 0 and v >= 2) else None,
                              lc_ev[v - 2] if (kt == 0 and v >= 2) else None),
                    inc=(kt == 15))
            if kt2 == 7:
                ptb_war[v % 2] = av_ev[v]

        do_load(0, None)
        for u in range(nu):
            hi, qg = divmod(u, 4)
            j = hi % 2
            pt = ptb[u % 2]
            q0 = qg * TG
            if qg == 1 and hi + 1 < nh:
                do_load(hi + 1, av_ev[hi * 4 - 1] if hi >= 1 else None)
            for kt2 in range(8):
                sp = sc_ps[kt2 % 2]
                bw = (exp_ev.get((u, kt2 - 2)) if kt2 >= 2
                      else exp_ev.get((u - 1, 6 + kt2)))
                if u > 0:
                    emit_avl_pair(u - 1, kt2)
                ev_s = None
                for uu in range(2):
                    kt = kt2 * 2 + uu
                    ev_s = k.mm(
                        sp[:, uu * 512:(uu + 1) * 512],
                        ksl[j][:, kt * 128:(kt + 1) * 128],
                        qsl[j][:, q0:q0 + TG], start=True, stop=True,
                        waits=_ev(ld_q[hi], ld_k[hi], bw), inc=(uu == 1))
                ee = k.act(pt[:, kt2 * 1024:(kt2 + 1) * 1024], sp, AF.Exp,
                           bias=-CSOFT, scale=ISQD,
                           waits=_ev(ev_s,
                                     ptb_war[u % 2] if kt2 == 0 else None))
                exp_ev[(u, kt2)] = ee
            if u >= 2:
                fin2(u - 2)
            if u > 0:
                fin1(u - 1)
        for kt2 in range(8):
            emit_avl_pair(nu - 1, kt2)
        fin2(nu - 2)
        fin1(nu - 1)
        fin2(nu - 1)
        k.barrier(name)


    def emit_sliding(name="slide"):
        ar.reset()
        qsl = [ar.take(64, T, BF16) for _ in range(2)]
        ksl = [ar.take(64, T, BF16) for _ in range(2)]
        vsl = [ar.take(128, 16 * 64, BF16) for _ in range(2)]
        ptp = [ar.take(128, 256, BF16) for _ in range(2)]
        ctxu = [ar.take(64, T, BF16) for _ in range(2)]
        invbc = ar.take(64, T, BF16)
        ctxn = ar.take(64, T, BF16)
        maskcat = ar.take(128, 256, BF16)
        lrow = [rows[:, 0:T], rows[:, T:2 * T]]
        invrow = rows[:, 2 * T:3 * T]
        invrow_b = [rowsb[:, 0:T], rowsb[:, T:2 * T]]
        sc_ps = [pb[0][:, 0:256], pb[1][:, 0:256]]
        ctx_ps = [pb[2][0:64, 0:128], pb[3][0:64, 0:128]]
        lb_ps = [pbig[0][:, 0:512], pbig[0][:, 512:1024]]
        bc_ps = pbig[1][0:64, 0:512]
        ev_mc = k.qdma(maskcat[:, 0:128], mask_prev.ap(), stream="ra")
        ev_mc2 = k.qdma(maskcat[:, 128:256], mask_cur.ap(), stream="ra")
        heads = [(s, h) for s in range(SQ) for h in range(16)]
        nh2 = len(heads)
        ld_q = [None] * nh2
        ld_k = [None] * nh2
        ld_v = [None] * nh2
        d0 = k.dma_now()

        def do_load(hi, war):
            sH, hH = heads[hi]
            jj = hi % 2
            pbase = 64 * (hH % 2)
            k.qdma(qsl[jj], QP[sH, pbase:pbase + 64, hH // 2, :],
                   waits=_ev(ld_q[hi - 2] if hi >= 2 else None, war),
                   stream=("mq0", "mq1")[jj])
            k.qdma(ksl[jj], KP[sH, pbase:pbase + 64, hH // 2, :],
                   stream=("mq0", "mq1")[jj])
            ev = k.qdma(
                vsl[jj].rearrange("p (kt d) -> p kt d", kt=16),
                VP[sH, :, :, hH * 64:(hH + 1) * 64],
                stream=("mq0", "mq1")[jj])
            ld_q[hi] = ld_k[hi] = ld_v[hi] = ev

        exp_ev = {}
        msk_ev = {}
        av_ev = {}
        l_ev = {}
        lc_ev = {}
        ec_ev = {}
        last_av = [None, None]
        fin_cast = [None] * nh2
        fin_bcm = [None] * nh2
        st = {"en": None, "inv_rd": None, "ctxn_rd": d0, "lrow_rd": [None,
              None], "ibc_rd": None}

        def fin(hv):
            sH, hH = heads[hv]
            pbase = 64 * (hH % 2)
            e2 = k.v("reciprocal", invrow, lrow[hv % 2],
                     waits=_ev(lc_ev[(hv, 15)], st["inv_rd"]))
            ec = k.v("tensor_copy", invrow_b[hv % 2], invrow,
                     waits=_ev(e2, fin_bcm[hv - 2] if hv >= 2 else None))
            fin_cast[hv] = ec
            st["inv_rd"] = ec
            ev_ib = None
            for qg in range(4):
                ebc = k.mm(bc_ps, onesrb_sb[0:1, 0:64],
                           invrow_b[hv % 2][:, qg * 512:(qg + 1) * 512],
                           True, True,
                           waits=_ev(ec, st["ibc_rd"]), inc=True)
                fin_bcm[hv] = ebc
                ev_ib = k.v("tensor_copy",
                            invbc[:, qg * 512:(qg + 1) * 512], bc_ps,
                            waits=_ev(ebc,))
                st["ibc_rd"] = ev_ib
            en = k.v("tensor_tensor", ctxn, ctxu[hv % 2], invbc, ALU.mult,
                     waits=_ev(ev_ib, ec_ev[(hv, 15)], st["ctxn_rd"]))
            st["en"] = en
            st["ctxn_rd"] = k.qdma(CTX[sH, pbase:pbase + 64, hH // 2, :],
                                   ctxn, waits=_ev(en), stream="mc0")

        do_load(0, None)
        for hi in range(nh2):
            sH, hH = heads[hi]
            j = hi % 2
            if hi + 1 < nh2:
                do_load(hi + 1, last_av[(hi + 1) % 2])
            for c in range(16):
                bs = c % 2
                q0 = c * 128
                ev_s = None
                if c > 0:
                    ev_s = k.mm(sc_ps[bs][:, 0:128],
                                ksl[j][:, (c - 1) * 128:c * 128],
                                qsl[j][:, q0:q0 + 128],
                                start=True, stop=False,
                                waits=_ev(ld_q[hi], ld_k[hi],
                                          exp_ev.get((hi, c - 2))))
                ev_s = k.mm(sc_ps[bs][:, 128:256],
                            ksl[j][:, c * 128:(c + 1) * 128],
                            qsl[j][:, q0:q0 + 128],
                            start=(c == 0), stop=True,
                            waits=_ev(ld_q[hi], ld_k[hi],
                                      exp_ev.get((hi, c - 2))), inc=True)
                lo = 128 if c == 0 else 0
                ee = k.act(ptp[bs][:, lo:256], sc_ps[bs][:, lo:256], AF.Exp,
                           bias=-CSOFT, scale=ISQS,
                           waits=_ev(ev_s, msk_ev.get((hi, c - 2)),
                                     av_ev.get((hi, c - 2)),
                                     l_ev.get((hi, c - 2)),
                                     d0 if (hi == 0 and c < 2) else None))
                exp_ev[(hi, c)] = ee
                em = k.v("tensor_tensor", ptp[bs][:, lo:256],
                         ptp[bs][:, lo:256], maskcat[:, lo:256], ALU.mult,
                         waits=_ev(ee, ev_mc2))
                msk_ev[(hi, c)] = em
                if c > 0:
                    cc = c - 1
                    nmm = 2 if cc > 0 else 1
                    lo2 = 128 if cc == 0 else 0
                    ev_av = None
                    ev_l = None
                    for u in range(nmm):
                        kt = cc - (nmm - 1) + u
                        ev_av = k.mm(
                            ctx_ps[cc % 2],
                            vsl[j][:, kt * 64:(kt + 1) * 64],
                            ptp[cc % 2][:, lo2 + u * 128:lo2 + u * 128 + 128],
                            start=(u == 0), stop=(u == nmm - 1),
                            waits=_ev(ld_v[hi], msk_ev[(hi, cc)],
                                      ec_ev.get((hi, cc - 2))),
                            inc=(u == nmm - 1))
                    for u in range(nmm):
                        ev_l = k.mm(
                            lb_ps[cc % 2][0:1, 0:128], onescb_sb,
                            ptp[cc % 2][:, lo2 + u * 128:lo2 + u * 128 + 128],
                            start=(u == 0), stop=(u == nmm - 1),
                            waits=_ev(lc_ev.get((hi, cc - 2)) if u == 0
                                      else None),
                            inc=(u == nmm - 1))
                    av_ev[(hi, cc)] = ev_av
                    l_ev[(hi, cc)] = ev_l
                    last_av[j] = ev_av
                    if cc % 2 == 0:
                        ec2 = k.act(ctxu[hi % 2][:, cc * 128:cc * 128 + 128],
                                    ctx_ps[cc % 2], AF.Copy,
                                    waits=_ev(ev_av, st["en"]))
                        lc2 = k.v("tensor_copy",
                                  lrow[hi % 2][:, cc * 128:cc * 128 + 128],
                                  lb_ps[cc % 2][0:1, 0:128],
                                  waits=_ev(ev_l, st["lrow_rd"][hi % 2]))
                    else:
                        ec2 = k.v("tensor_copy",
                                  ctxu[hi % 2][:, cc * 128:cc * 128 + 128],
                                  ctx_ps[cc % 2],
                                  waits=_ev(ev_av, st["en"]))
                        lc2 = k.act(lrow[hi % 2][:, cc * 128:cc * 128 + 128],
                                    lb_ps[cc % 2][0:1, 0:128], AF.Copy,
                                    waits=_ev(ev_l, st["lrow_rd"][hi % 2]))
                    ec_ev[(hi, cc)] = ec2
                    lc_ev[(hi, cc)] = lc2
                if c == 4 and hi >= 1:
                    fin(hi - 1)
            cc = 15
            ev_av = None
            ev_l = None
            for u in range(2):
                kt = cc - 1 + u
                ev_av = k.mm(ctx_ps[cc % 2],
                             vsl[j][:, kt * 64:(kt + 1) * 64],
                             ptp[cc % 2][:, u * 128:u * 128 + 128],
                             start=(u == 0), stop=(u == 1),
                             waits=_ev(ld_v[hi], msk_ev[(hi, cc)],
                                       ec_ev.get((hi, cc - 2))),
                             inc=(u == 1))
            for u in range(2):
                ev_l = k.mm(lb_ps[cc % 2][0:1, 0:128], onescb_sb,
                            ptp[cc % 2][:, u * 128:u * 128 + 128],
                            start=(u == 0), stop=(u == 1),
                            waits=_ev(lc_ev.get((hi, cc - 2)) if u == 0
                                      else None),
                            inc=(u == 1))
            av_ev[(hi, cc)] = ev_av
            l_ev[(hi, cc)] = ev_l
            last_av[j] = ev_av
            ec2 = k.act(ctxu[hi % 2][:, cc * 128:cc * 128 + 128],
                        ctx_ps[cc % 2], AF.Copy, waits=_ev(ev_av, st["en"]))
            lc2 = k.v("tensor_copy", lrow[hi % 2][:, cc * 128:cc * 128 + 128],
                      lb_ps[cc % 2][0:1, 0:128],
                      waits=_ev(ev_l, st["lrow_rd"][hi % 2]))
            ec_ev[(hi, cc)] = ec2
            lc_ev[(hi, cc)] = lc2
            st["lrow_rd"][hi % 2] = None
        fin(nh2 - 1)
        k.barrier(name)


    def emit_scan(name="scan"):
        ar.reset()
        at = [ar.take(128, T, F32) for _ in range(2)]
        xt = [ar.take(128, T, F32) for _ in range(2)]
        rt = [ar.take(128, T, F32) for _ in range(2)]
        ht = [ar.take(128, T, F32) for _ in range(2)]
        ev_a = [None, None]
        ev_x = [None, None]
        ev_r = [None, None]
        ev_h = [None, None]
        for s in range(SQ):
            for kc in range(KC):
                i = (s * KC + kc) % 2
                ev_a[i] = k.qdma(at[i], AA[s, :, kc, :], waits=_ev(ev_a[i]),
                                 stream=("xa", "xb2")[i])
                ev_x[i] = k.qdma(xt[i], XS[s, :, kc, :], waits=_ev(ev_x[i]),
                                 stream=("ra", "rb")[i])
                ev_r[i] = k.qdma(rt[i], RS[s, :, kc, :], waits=_ev(ev_r[i]),
                                 stream=("mq0", "mq1")[i])
                es = k.v("tensor_tensor_scan", ht[i], at[i], xt[i], 0.0,
                         ALU.mult, ALU.add,
                         waits=_ev(ev_a[i], ev_x[i], ev_h[i]))
                ev_a[i] = es
                ev_x[i] = es
                eo = k.v("tensor_tensor", ht[i], ht[i], rt[i], ALU.add,
                         waits=_ev(ev_r[i]))
                ev_r[i] = eo
                ev_h[i] = k.qdma(RS[s, :, kc, :], ht[i], waits=_ev(eo),
                                 stream=("ha", "hb")[i])
        k.barrier(name)


    def emit_glu(name="glu"):
        ar.reset()
        xs = [ar.take(128, KC * T, BF16) for _ in range(2)]
        wg = [wslot[0][:, 0:KC * 512], wslot[0][:, KC * 512:KC * 1024]]
        wu = [wslot[1][:, 0:KC * 512], wslot[1][:, KC * 512:KC * 1024]]
        glu_war = _ev(wstate["free"][0], wstate["free"][1])
        evb = [ar.take(128, TG, BF16) for _ in range(4)]
        evf = [ar.take(128, TG, F32) for _ in range(2)]
        evb_ev = [None] * 4
        evf_ev = [None] * 2
        ev_xs = []
        for s in range(SQ):
            ev_xs.append(k.qdma(
                xs[s].rearrange("p (kc t) -> p kc t", kc=KC), XN[s],
                stream=("xa", "xb2")[s]))
        ev_wg = [None, None]
        ev_wu = [None, None]
        ev_wgfree = [None, None]
        ev_wufree = [None, None]
        bank_free = [None] * 4
        banks = [pbig[0][:, 0:512], pbig[0][:, 512:1024],
                 pbig[1][:, 0:512], pbig[1][:, 512:1024]]
        nb = 0
        wgv = W_gate.rearrange("(kc p) o -> p kc o", p=128)
        wuv = W_up.rearrange("(kc p) o -> p kc o", p=128)
        for ff in range(DFF // 512):
            i = ff % 2
            ev_wg[i] = k.gdma(
                wg[i].rearrange("p (kc o) -> p kc o", kc=KC),
                wgv[:, :, ff * 512:(ff + 1) * 512],
                waits=_ev(ev_wgfree[i], glu_war if ff < 2 else None),
                stream="w0")
            ev_wu[i] = k.gdma(
                wu[i].rearrange("p (kc o) -> p kc o", kc=KC),
                wuv[:, :, ff * 512:(ff + 1) * 512],
                waits=_ev(ev_wufree[i], glu_war if ff < 2 else None),
                stream="w1")
            for s in range(SQ):
                for tg in range(NTG):
                    t0 = tg * TG
                    for ocq in range(4):
                        bg_ = nb % 4
                        nb += 1
                        bu_ = nb % 4
                        nb += 1
                        ev_g = ev_u = None
                        for kc in range(KC):
                            ev_g = k.mm(
                                banks[bg_],
                                wg[i][:, kc * 512 + ocq * 128:
                                      kc * 512 + (ocq + 1) * 128],
                                xs[s][:, kc * T + t0:kc * T + t0 + TG],
                                start=(kc == 0), stop=(kc == KC - 1),
                                waits=_ev(ev_wg[i], ev_xs[s],
                                          bank_free[bg_]),
                                inc=(kc == KC - 1))
                        for kc in range(KC):
                            ev_u = k.mm(
                                banks[bu_],
                                wu[i][:, kc * 512 + ocq * 128:
                                      kc * 512 + (ocq + 1) * 128],
                                xs[s][:, kc * T + t0:kc * T + t0 + TG],
                                start=(kc == 0), stop=(kc == KC - 1),
                                waits=_ev(ev_wu[i], bank_free[bu_]),
                                inc=(kc == KC - 1))
                        sb_ = (ff * 8 + tg * 4 + ocq) % 2
                        eg = k.act(evf[sb_], banks[bg_], AF.Silu,
                                   waits=_ev(ev_g, evf_ev[sb_]))
                        bank_free[bg_] = eg
                        sl = (ff * 8 + tg * 4 + ocq) % 4
                        em = k.v("tensor_tensor", evb[sl], evf[sb_],
                                 banks[bu_], ALU.mult,
                                 waits=_ev(eg, ev_u, evb_ev[sl]))
                        bank_free[bu_] = em
                        evf_ev[sb_] = em
                        evb_ev[sl] = k.qdma(
                            GU[s, :, ff * 4 + ocq, t0:t0 + TG], evb[sl],
                            waits=_ev(em), stream="s%d" % sl)
                        ev_wgfree[i] = ev_g
                        ev_wufree[i] = ev_u
        wstate["free"][0] = ev_g
        wstate["free"][1] = ev_u
        k.barrier(name)


    emit_ln(x_ext, "ln", 1e-5, dst_bf=XN, name="ln1")

    def emit_ffn(call, rsrc, rdst):
        nsrc = {-1: XN}
        for c in range(5):
            nsrc[c] = ND[c]
        for (c, eq, ek, evv, iq, ik, iv) in PLAN:
            for e, i, roles in _node_projs(c):
                emit_gemm(nsrc[i], W_We[e * D:(e + 1) * D, :], D, D,
                          mk_evac_edge(c, e, roles),
                          name="f%dn%de%d" % (call, c, e))
            emit_gemm(EQ, W_Wq[c * D:(c + 1) * D, :], D, D,
                      mk_evac_plain(QP), name="f%dn%dwq" % (call, c))
            emit_gemm(EK, W_Wk[c * D:(c + 1) * D, :], D, D,
                      mk_evac_plain(KP), name="f%dn%dwk" % (call, c))
            emit_gemm_tm(EV, W_Wv[c * D:(c + 1) * D, :], D, D, VP,
                         name="f%dn%dwv" % (call, c))
            emit_mha(name="f%dn%dmha" % (call, c))

            def pre_rt(ctx):
                ctx["rt_load"] = QD
            emit_gemm(CTX, W_Wo[c * D:(c + 1) * D, :], D, D,
                      mk_evac_combine(c), name="f%dn%dwo" % (call, c),
                      pre_alloc=pre_rt)
        emit_ln(N5, "lnc", 1e-6, combine={"rsrc": rsrc, "rdst": rdst},
                name="f%dfin" % call)

    emit_ffn(0, x_ext, R)

    emit_ln(R, "ln", 1e-5, dst_bf=XN, dst_f32=RS, name="ln2")
    emit_ln(RS, "rms", 1e-6, dst_bf=XN, rms_g=0, name="rmsmix")
    pre_mg, evac_mg = mk_evac_mingru()
    emit_gemm(XN, W_gvd, D, 3 * D, evac_mg, name="gvd", pre_alloc=pre_mg)
    emit_scan()
    emit_ln(RS, "rms", 1e-6, dst_bf=XN, rms_g=1, name="rmsattn")
    emit_gemm(XN, W_qkv, D, 2 * D, mk_evac_plain(QP, KP, split_oc=KC),
              name="qkvfm")
    emit_gemm_tm(XN, W_qkv, D, D, VP, wcol0=2 * D, name="qkvtm")
    emit_sliding()

    def pre_rs(ctx):
        ctx["rt_load"] = RS
    emit_gemm(CTX, W_oT, D, D, mk_evac_resid(RS), name="slwo",
              pre_alloc=pre_rs)
    emit_ln(RS, "rms", 1e-6, dst_bf=XN, rms_g=2, name="rmsffn")
    emit_glu()

    def pre_rs2(ctx):
        ctx["rt_load"] = RS
        nxb = ctx["nxb"]
        ctx["rt2"] = [WPOOL[:, 32768 + i * 8192:40960 + i * 8192].bitcast(F32)
                      for i in range(nxb)]
        ctx["rt2_ev"] = [None] * nxb
        ctx["rt2_rd"] = [None] * nxb

    def evac_wout(ctx, s, oc, tg, ps, ev_mm, i):
        t0 = tg * TG
        if oc == 0:
            ctx["rt2_ev"][i] = k.qdma(
                ctx["rt2"][i].rearrange("p (kc t) -> p kc t", kc=KC),
                R[s, :, :, t0:t0 + TG],
                waits=_ev(ctx["rt2_ev"][i], ctx["rt2_rd"][i]), stream="r2")
        sl = oc % 4
        rt = ctx["rt"][i][:, oc * TG:(oc + 1) * TG]
        rt2 = ctx["rt2"][i][:, oc * TG:(oc + 1) * TG]
        e1 = k.v("tensor_tensor", ctx["evf"][sl], ps, rt, ALU.add,
                 waits=_ev(ev_mm, ctx["rt_ev"][i], ctx["evf_ev"][sl]))
        ctx["rt_rd"][i] = e1
        eo = k.v("tensor_tensor", ctx["evf"][sl], ctx["evf"][sl], rt2,
                 ALU.add, waits=_ev(ctx["rt2_ev"][i]))
        ctx["rt2_rd"][i] = eo
        ctx["evf_ev"][sl] = k.qdma(R[s, :, oc, t0:t0 + TG], ctx["evf"][sl],
                                   waits=_ev(eo), stream="f%d" % sl)
        return e1

    emit_gemm(GU, W_out, DFF, D, evac_wout, name="wout", pre_alloc=pre_rs2)

    emit_ln(R, "ln", 1e-5, dst_bf=XN, name="ln3")
    emit_ffn(1, R, out_ext)

    if DEBUG_DUMP:
        k.unstop()
        for nm in DEBUG_DUMP:
            k.qdma(dbg_outs[nm], SCRATCH[nm])
    k.unstop()
    k.barrier("final", final=True)

    with ExitStack() as es:
        k.finalize(es)
    global LAST_K
    LAST_K = k
    return nc


LAST_K = None


def _prep_inputs(inputs):
    f = np.float32
    g = {}
    x = np.asarray(inputs["L0_attn_x"], f)
    g["x_cores"] = np.ascontiguousarray(
        x.reshape(16, T, KC, 128).transpose(0, 3, 2, 1))

    def tp(a):
        return np.ascontiguousarray(np.asarray(a, f).T)

    g["We"] = np.ascontiguousarray(
        np.asarray(inputs["L0_ffn_We"], f).reshape(20 * D, D))
    g["Wqn"] = np.ascontiguousarray(
        np.asarray(inputs["L0_ffn_Wq"], f).reshape(6 * D, D))
    g["Wkn"] = np.ascontiguousarray(
        np.asarray(inputs["L0_ffn_Wk"], f).reshape(6 * D, D))
    g["Wvn"] = np.ascontiguousarray(
        np.asarray(inputs["L0_ffn_Wv"], f).reshape(6 * D, D))
    g["Won"] = np.ascontiguousarray(
        np.asarray(inputs["L0_ffn_Wo"], f).reshape(6 * D, D))
    g["Wgvd"] = np.ascontiguousarray(np.concatenate(
        [np.asarray(inputs["L0_attn_Wg"], f).T,
         np.asarray(inputs["L0_attn_Wv"], f).T,
         np.asarray(inputs["L0_attn_Wd"], f).T], axis=1))
    g["Wqkv"] = tp(inputs["L0_attn_Wqkv"])
    g["WoT"] = tp(inputs["L0_attn_Wo"])
    g["WgateT"] = tp(inputs["L0_attn_Wgate"])
    g["WupT"] = tp(inputs["L0_attn_Wup"])
    g["WoutT"] = tp(inputs["L0_attn_Wout"])

    be = np.asarray(inputs["L0_ffn_be"], f)
    g["becols"] = np.ascontiguousarray(
        be.reshape(20, KC, 128).transpose(2, 0, 1).reshape(128, 20 * KC))
    bgvd = np.stack([np.asarray(inputs["L0_attn_bg"], f),
                     np.asarray(inputs["L0_attn_bv"], f),
                     np.asarray(inputs["L0_attn_bd"], f)])
    g["bgvd"] = np.ascontiguousarray(
        bgvd.reshape(3, KC, 128).transpose(2, 0, 1).reshape(128, 3 * KC))
    g["ewt"] = np.ascontiguousarray(np.tile(
        np.asarray(inputs["L0_ffn_ew"], f).reshape(1, 18), (128, 1)))
    g["actwt"] = np.ascontiguousarray(np.tile(
        np.asarray(inputs["L0_ffn_act_w"], f).reshape(1, 6), (128, 1)))
    g["lng"] = np.ascontiguousarray(
        np.asarray(inputs["L0_ffn_ln_g"], f).reshape(KC, 128).T)
    g["lnb"] = np.ascontiguousarray(
        np.asarray(inputs["L0_ffn_ln_b"], f).reshape(KC, 128).T)
    rms = np.stack([np.asarray(inputs["L0_attn_w_rms_mix"], f),
                    np.asarray(inputs["L0_attn_w_rms_attn"], f),
                    np.asarray(inputs["L0_attn_w_rms_ffn"], f)])
    g["rmsc"] = np.ascontiguousarray(
        rms.reshape(3, KC, 128).transpose(2, 0, 1).reshape(128, 3 * KC))
    return g


SHARED_KEYS = ("We", "Wqn", "Wkn", "Wvn", "Won", "Wgvd", "Wqkv", "WoT",
               "WgateT", "WupT", "WoutT", "becols", "bgvd", "ewt", "actwt",
               "lng", "lnb", "rmsc")


def kernel(**inputs):
    g = _prep_inputs(inputs)
    nc = build_program()
    in_maps = []
    for c in range(NCORES):
        m = {n: g[n] for n in SHARED_KEYS}
        m["x"] = np.ascontiguousarray(g["x_cores"][2 * c:2 * c + 2])
        in_maps.append(m)
    res = run_bass_kernel_spmd(nc, in_maps, core_ids=list(range(NCORES)))
    outs = []
    for c in range(NCORES):
        o = res.results[c]["out"]
        outs.append(o.transpose(0, 3, 2, 1).reshape(2, T, D))
    return np.ascontiguousarray(
        np.concatenate(outs, axis=0).astype(np.float32))
